# version 04
# baseline (speedup 1.0000x reference)
import numpy as np
import concourse.bass as bass
import concourse.mybir as mybir
from concourse.bass_utils import run_bass_kernel_spmd
from contextlib import ExitStack

F32 = mybir.dt.float32
BF16 = mybir.dt.bfloat16
AF = mybir.ActivationFunctionType
ALU = mybir.AluOpType
AX = mybir.AxisListType

SEQ = 4096
DM = 1024
NT = SEQ // 128
NCC = DM // 128
B4_FFP = 2816
B4_NFC = 22


ENGS = ("pe", "act", "dve", "pool", "sp")
DMA_SEM = {"sp": "spd", "pool": "poold", "act": "actd"}


class Buf:
    __slots__ = ("name", "t", "w", "r")

    def __init__(self, name, t):
        self.name = name
        self.t = t
        self.w = None
        self.r = {}

    def __getitem__(self, key):
        return self.t[key]


class Ctx:
    def __init__(self, nc):
        self.nc = nc
        self.stack = ExitStack()
        self.q = {e: [] for e in ENGS}
        self.cnt = {s: 0 for s in ("pe", "act", "dve", "pool", "sp",
                                   "spd", "poold", "actd")}
        self.seen = {e: {} for e in ENGS}
        self.sems = {}
        for s in self.cnt:
            self.sems[s] = self.stack.enter_context(nc.semaphore("sem_" + s))

    def sb(self, name, shape, dtype):
        t = self.stack.enter_context(self.nc.sbuf_tensor(name, shape, dtype))
        return Buf(name, t)

    def arena_init(self, cols):
        self.ar = self.stack.enter_context(
            self.nc.sbuf_tensor("ARENA", [128, cols], BF16))
        self.acur = 0
        self.amax = 0

    def arena_reset(self):
        self.acur = 0

    def vsb(self, name, shape, dtype):
        p, c = shape
        cols = c if dtype == BF16 else 2 * c
        acols = (cols + 15) // 16 * 16
        off = self.acur
        self.acur += acols
        self.amax = max(self.amax, self.acur)
        assert self.acur <= self.ar.shape[1], f"arena overflow at {name}"
        ap = self.ar[0:p, off:off + cols]
        if dtype != BF16:
            ap = ap.bitcast(dtype)
        return Buf(name, ap)

    def barrier(self):
        for e in ENGS:
            for s, v in self.cnt.items():
                if v:
                    self._wait(e, s, v)

    def ps(self, name, shape, dtype=F32):
        t = self.stack.enter_context(self.nc.psum_tensor(name, shape, dtype))
        return Buf(name, t)

    def dram(self, name, shape, dtype):
        t = self.nc.dram_tensor(name, shape, dtype, kind="Internal")
        return Buf(name, t.ap())

    def din(self, name, shape, dtype=F32):
        t = self.nc.dram_tensor(name, shape, dtype, kind="ExternalInput")
        return Buf(name, t.ap())

    def dout(self, name, shape, dtype=F32):
        t = self.nc.dram_tensor(name, shape, dtype, kind="ExternalOutput")
        return Buf(name, t.ap())

    def _wait(self, eng, sem, val):
        if eng == "pe" and sem == "pe":
            return
        if self.seen[eng].get(sem, 0) >= val:
            return
        self.q[eng].append(
            lambda E, s=sem, v=val: E.wait_ge(self.sems[s], v))
        self.seen[eng][sem] = val

    def _dval(self, sem, val):
        return self.cnt[sem] if sem.endswith("d") else val

    def _deps(self, eng, reads, writes):
        for b in reads:
            if b.w is not None:
                self._wait(eng, b.w[0], self._dval(b.w[0], b.w[1]))
        for b in writes:
            if b.w is not None:
                self._wait(eng, b.w[0], self._dval(b.w[0], b.w[1]))
            for sem, val in b.r.items():
                self._wait(eng, sem, self._dval(sem, val))

    def op(self, eng, fn, reads=(), writes=()):
        self._deps(eng, reads, writes)
        self.cnt[eng] += 1
        v = self.cnt[eng]
        self.q[eng].append(
            lambda E, f=fn, s=eng: f(E).then_inc(self.sems[s], 1))
        tok = (eng, v)
        for b in reads:
            b.r[eng] = v
        for b in writes:
            b.w = tok
            b.r = {}
        return tok

    def dma(self, queue, fn, reads=(), writes=()):
        self._deps(queue, reads, writes)
        dsem = DMA_SEM[queue]
        self.cnt[dsem] += 16
        v = self.cnt[dsem]
        self.q[queue].append(
            lambda E, f=fn, s=dsem: f(E).then_inc(self.sems[s], 16))
        tok = (dsem, v)
        for b in reads:
            b.r[dsem] = v
        for b in writes:
            b.w = tok
            b.r = {}
        return tok

    def barrier_out(self):
        for s in ("spd", "poold", "actd"):
            if self.cnt[s]:
                self.q["sp"].append(
                    lambda E, ss=s, v=self.cnt[s]: E.wait_ge(self.sems[ss], v))

    def finalize(self):
        nc = self.nc
        qs = self.q
        with nc.Block() as blk:
            @blk.tensor
            def _(e):
                for f in qs["pe"]:
                    f(e)

            @blk.scalar
            def _(e):
                for f in qs["act"]:
                    f(e)

            @blk.vector
            def _(e):
                for f in qs["dve"]:
                    f(e)

            @blk.gpsimd
            def _(e):
                for f in qs["pool"]:
                    f(e)

            @blk.sync
            def _(e):
                for f in qs["sp"]:
                    f(e)
        self.stack.close()



def emit_ln(ctx, out, out_sl, x, x_sl, stat, sqbuf, sq_sl, D,
            res=None, res_sl=None, eps=None):
    xap = x[x_sl]
    sqap = sqbuf[sq_sl]
    ctx.op("dve", lambda E: E.reduce_sum(stat[:, 0:1], xap, axis=AX.X),
           reads=[x], writes=[stat])
    ctx.op("act", lambda E: E.activation(sqap, xap, AF.Square,
                                         bias=0.0, scale=1.0),
           reads=[x], writes=[sqbuf])
    ctx.op("dve", lambda E: E.reduce_sum(stat[:, 1:2], sqap, axis=AX.X),
           reads=[sqbuf], writes=[stat])
    ctx.op("dve", lambda E: E.tensor_scalar_mul(stat[:, 2:3], stat[:, 0:1],
                                                1.0 / D),
           reads=[stat], writes=[stat])
    ctx.op("dve", lambda E: E.tensor_tensor(stat[:, 3:4], stat[:, 2:3],
                                            stat[:, 2:3], ALU.mult),
           reads=[stat], writes=[stat])
    ctx.op("dve", lambda E: E.scalar_tensor_tensor(
        stat[:, 4:5], stat[:, 1:2], 1.0 / D, stat[:, 3:4],
        ALU.mult, ALU.subtract),
        reads=[stat], writes=[stat])
    ctx.op("act", lambda E: E.activation(stat[:, 5:6], stat[:, 4:5], AF.Sqrt,
                                         bias=eps[:, 0:1], scale=1.0),
           reads=[stat, eps], writes=[stat])
    ctx.op("dve", lambda E: E.reciprocal(stat[:, 6:7], stat[:, 5:6]),
           reads=[stat], writes=[stat])
    ctx.op("dve", lambda E: E.tensor_scalar_mul(stat[:, 7:8], stat[:, 2:3],
                                                -1.0),
           reads=[stat], writes=[stat])
    oap = out[out_sl]
    ctx.op("dve", lambda E: E.tensor_scalar(oap, xap, stat[:, 7:8],
                                            stat[:, 6:7], ALU.add, ALU.mult),
           reads=[x, stat], writes=[out])
    if res is not None:
        ctx.op("dve", lambda E: E.tensor_tensor(oap, oap, res[res_sl],
                                                ALU.add),
               reads=[out, res], writes=[out])



def emit_b3(ctx, com, xin, xout):
    nc = ctx.nc
    qtw = ctx.din("qtw", [8, DM, 128])
    ktw = ctx.din("ktw", [8, DM, 128])
    vtw = ctx.din("vtw", [8, DM, 128])
    wtw = ctx.din("wtw", [DM, DM])
    consts = ctx.din("consts", [128, 128 + 128 + 4 * 512])

    yfm_d = ctx.dram("b3yfm_d", [NCC, 128, SEQ], BF16)

    xt2 = ctx.vsb("xt2", [128, DM], F32)
    xtile = [com.xtile, xt2]
    lntile, stat, ident = com.lntile, com.stat, com.ident
    epsc, onesc = com.epsc, com.onesf
    pp, pst = com.pp, com.pst

    ht = ctx.vsb("ht", [128, NCC * SEQ], BF16)
    sq = ctx.vsb("sq", [128, DM], F32)
    tri = ctx.vsb("tri", [128, 4 * 512], BF16)
    qw = ctx.vsb("qw", [128, NCC * 128], BF16)
    kw = ctx.vsb("kw", [128, NCC * 128], BF16)
    vw = ctx.vsb("vw", [128, NCC * 128], BF16)
    wt = ctx.vsb("wt", [128, NCC * DM], BF16)
    qt = ctx.vsb("qt", [128, SEQ], BF16)
    kt = ctx.vsb("kt", [128, SEQ], BF16)
    vfm = ctx.vsb("vfm", [128, SEQ], BF16)
    va = ctx.vsb("va", [128, NT * 65], BF16)
    vb = ctx.vsb("vb", [128, NT * 65], BF16)
    pt = ctx.vsb("pt", [128, 512], BF16)
    recip = ctx.vsb("recip", [1, 512], F32)
    ybc = ctx.vsb("ybc", [128, 512], F32)
    ystage = ctx.vsb("ystage", [128, 512], BF16)
    yfmt = [ctx.vsb(f"yfmt{i}", [128, DM], BF16) for i in range(2)]
    outln = ctx.vsb("outln", [128, DM], F32)

    ctx.dma("pool", lambda E: E.dma_start(ident[:, :], consts[:, 0:128]),
            reads=[consts], writes=[ident])
    ctx.dma("sp", lambda E: E.dma_start(onesc[:, :], consts[0:1, 128:256]),
            reads=[consts], writes=[onesc])
    ctx.dma("pool", lambda E: E.dma_start(tri[:, :],
                                          consts[:, 256:256 + 4 * 512]),
            reads=[consts], writes=[tri])
    ctx.dma("pool", lambda E: E.dma_start(
        wt[:, :].rearrange("p (c m) -> p c m", c=NCC),
        wtw.t.rearrange("(c p) m -> p c m", p=128)),
        reads=[wtw], writes=[wt])

    ctx.op("dve", lambda E: E.memset(epsc[:, :], 1e-5),
           reads=[], writes=[epsc])

    for i in range(NT):
        xt = xtile[i % 2]
        ctx.dma("sp", lambda E, i=i, xt=xt: E.dma_start(
            xt[:, :], xin.t[i * 128:(i + 1) * 128, :]),
            reads=[xin], writes=[xt])
        emit_ln(ctx, lntile, np.s_[:, :], xt, np.s_[:, :], stat, sq,
                np.s_[:, :], DM, eps=epsc)
        for c in range(NCC):
            ctx.op("pe", lambda E, c=c: E.transpose(
                pst[:, 0:128], lntile[:, c * 128:(c + 1) * 128], ident[:, :]),
                reads=[lntile, ident], writes=[pst])
            ctx.op("dve", lambda E, c=c, i=i: E.tensor_copy(
                ht[:, c * SEQ + i * 128: c * SEQ + (i + 1) * 128],
                pst[:, 0:128]),
                reads=[pst], writes=[ht])

    SC_QK = float(np.sqrt(64.0 / 1024.0) / 8.0)
    SC_V = float(np.sqrt(64.0 / 1024.0))
    for pair in range(8):
        for wbuf, src in ((qw, qtw), (kw, ktw), (vw, vtw)):
            ctx.dma("pool", lambda E, wbuf=wbuf, src=src, pair=pair:
                    E.dma_start(
                        wbuf[:, :].rearrange("p (c m) -> p c m", c=NCC),
                        src.t[pair].rearrange("(c p) m -> p c m", p=128)),
                    reads=[src], writes=[wbuf])
        for tens, wbuf, scl in ((qt, qw, SC_QK), (kt, kw, SC_QK),
                                (vfm, vw, SC_V)):
            for ns in range(8):
                pb = pp[ns % 2]
                for c in range(NCC):
                    ctx.op("pe", lambda E, c=c, ns=ns, wbuf=wbuf, pb=pb:
                           E.matmul(pb[:, :],
                                    wbuf[:, c * 128:(c + 1) * 128],
                                    ht[:, c * SEQ + ns * 512:
                                       c * SEQ + (ns + 1) * 512],
                                    start=(c == 0), stop=(c == NCC - 1)),
                           reads=[wbuf, ht], writes=[pb])
                ctx.op("dve", lambda E, ns=ns, tens=tens, scl=scl, pb=pb:
                       E.tensor_scalar_mul(
                           tens[:, ns * 512:(ns + 1) * 512], pb[:, :], scl),
                       reads=[pb], writes=[tens])
        for ktn in range(NT):
            ctx.op("pe", lambda E, ktn=ktn: E.transpose(
                pst[:, 0:128], vfm[:, ktn * 128:(ktn + 1) * 128], ident[:, :]),
                reads=[vfm, ident], writes=[pst])
            ctx.op("dve", lambda E, ktn=ktn: E.tensor_copy(
                va[:, ktn * 65:ktn * 65 + 64], pst[:, 0:64]),
                reads=[pst], writes=[va])
            ctx.op("dve", lambda E, ktn=ktn: E.tensor_copy(
                vb[:, ktn * 65:ktn * 65 + 64], pst[:, 64:128]),
                reads=[pst], writes=[vb])
        ctx.op("dve", lambda E: E.memset(
            va[:, :].rearrange("p (n c) -> p n c", c=65)[:, :, 64:65], 1.0),
            reads=[], writes=[va])
        ctx.op("dve", lambda E: E.memset(
            vb[:, :].rearrange("p (n c) -> p n c", c=65)[:, :, 64:65], 1.0),
            reads=[], writes=[vb])

        for hh in range(2):
            hsl = slice(0, 64) if hh == 0 else slice(64, 128)
            vx = va if hh == 0 else vb
            for qs in range(8):
                outp = pp[2]
                ktmax = 4 * qs + 3
                for ktn in range(ktmax + 1):
                    sb_ = pp[ktn % 2]
                    ctx.op("pe", lambda E, ktn=ktn, qs=qs, hsl=hsl, sb_=sb_:
                           E.matmul(sb_[:, :],
                                    kt[hsl, ktn * 128:(ktn + 1) * 128],
                                    qt[hsl, qs * 512:(qs + 1) * 512],
                                    start=True, stop=True),
                           reads=[kt, qt], writes=[sb_])
                    band_j = ktn - 4 * qs
                    if band_j < 0:
                        ctx.op("act", lambda E, sb_=sb_: E.activation(
                            pt[:, :], sb_[:, :], AF.Exp, bias=0.0, scale=1.0),
                            reads=[sb_], writes=[pt])
                    else:
                        ctx.op("act", lambda E, sb_=sb_: E.activation(
                            sq[:, 0:512], sb_[:, :], AF.Exp, bias=0.0,
                            scale=1.0),
                            reads=[sb_], writes=[sq])
                        ctx.op("dve", lambda E, j=band_j: E.tensor_tensor(
                            pt[:, :], sq[:, 0:512],
                            tri[:, j * 512:(j + 1) * 512], ALU.mult),
                            reads=[sq, tri], writes=[pt])
                    ctx.op("pe", lambda E, ktn=ktn, vx=vx, outp=outp,
                           st=(ktn == 0), sp_=(ktn == ktmax):
                           E.matmul(outp[0:65, :],
                                    vx[:, ktn * 65:(ktn + 1) * 65],
                                    pt[:, :], start=st, stop=sp_),
                           reads=[vx, pt], writes=[outp])
                ctx.op("dve", lambda E, outp=outp: E.reciprocal(
                    recip[:, :], outp[64:65, :]),
                    reads=[outp], writes=[recip])
                bc = pp[3]
                ctx.op("pe", lambda E, bc=bc: E.matmul(
                    bc[:, :], onesc[:, :], recip[:, :], start=True, stop=True),
                    reads=[onesc, recip], writes=[bc])
                ctx.op("dve", lambda E, bc=bc: E.tensor_copy(
                    ybc[:, :], bc[:, :]), reads=[bc], writes=[ybc])
                ctx.op("dve", lambda E, outp=outp: E.tensor_tensor(
                    ystage[0:64, :], outp[0:64, :], ybc[0:64, :], ALU.mult),
                    reads=[outp, ybc], writes=[ystage])
                ctx.dma("sp", lambda E, pair=pair, hh=hh, qs=qs: E.dma_start(
                    yfm_d.t[pair, hh * 64:(hh + 1) * 64,
                            qs * 512:(qs + 1) * 512],
                    ystage[0:64, :]),
                    reads=[ystage], writes=[yfm_d])

    WSC = float(np.sqrt(1024.0 / 64.0) / 16.0)
    for i in range(NT):
        yb = yfmt[i % 2]
        ctx.dma("sp", lambda E, i=i, yb=yb: E.dma_start(
            yb[:, :].rearrange("p (c m) -> p c m", c=NCC),
            yfm_d.t[:, :, i * 128:(i + 1) * 128].rearrange(
                "c p m -> p c m")),
            reads=[yfm_d], writes=[yb])
        for half in range(2):
            zp = pp[4 + half]
            for c in range(NCC):
                ctx.op("pe", lambda E, c=c, half=half, zp=zp, yb=yb:
                       E.matmul(zp[:, :],
                                yb[:, c * 128:(c + 1) * 128],
                                wt[:, c * DM + half * 512:
                                   c * DM + (half + 1) * 512],
                                start=(c == 0), stop=(c == NCC - 1)),
                       reads=[yb, wt], writes=[zp])
        xt = xtile[i % 2]
        ctx.dma("sp", lambda E, i=i, xt=xt: E.dma_start(
            xt[:, :], xin.t[i * 128:(i + 1) * 128, :]),
            reads=[xin], writes=[xt])
        for half in range(2):
            zp = pp[4 + half]
            ctx.op("dve", lambda E, half=half, zp=zp: E.tensor_scalar_mul(
                sq[:, half * 512:(half + 1) * 512], zp[:, :], WSC),
                reads=[zp], writes=[sq])
        emit_ln(ctx, outln, np.s_[:, 0:DM], sq, np.s_[:, 0:DM], stat,
                outln, np.s_[:, 0:DM], DM, res=xt, res_sl=np.s_[:, :],
                eps=epsc)
        ctx.dma("sp", lambda E, i=i: E.dma_start(
            xout.t[i * 128:(i + 1) * 128, :], outln[:, 0:DM]),
            reads=[outln], writes=[xout])




def host_consts_b3():
    c = np.zeros((128, 128 + 128 + 4 * 512), dtype=np.float32)
    c[:, 0:128] = np.eye(128, dtype=np.float32)
    c[0, 128:256] = 1.0
    for j in range(4):
        m = np.zeros((128, 512), dtype=np.float32)
        for sub in range(4):
            if sub > j:
                m[:, sub * 128:(sub + 1) * 128] = 1.0
            elif sub == j:
                m[:, sub * 128:(sub + 1) * 128] = np.tril(
                    np.ones((128, 128), dtype=np.float32)).T
        c[:, 256 + j * 512:256 + (j + 1) * 512] = m
    return c


def prep_b3_weights(Q, K, V, W):
    def pairs(q):
        t = np.ascontiguousarray(np.transpose(q, (2, 1, 0)))
        t = t.reshape(8, 2, DM, 64).transpose(0, 2, 1, 3).reshape(8, DM, 128)
        return np.ascontiguousarray(t)
    wtw = np.ascontiguousarray(np.transpose(W, (2, 1, 0)).reshape(DM, DM))
    return pairs(Q), pairs(K), pairs(V), wtw


def emit_b2(ctx, com, xin, xout, dbg=False, nexp=8):
    nc = ctx.nc
    wq = ctx.din("wq", [DM, DM])
    wk = ctx.din("wk", [DM, DM])
    wv = ctx.din("wv", [DM, DM])
    wo = ctx.din("wo", [DM, DM])
    brow_in = ctx.din("brow_in", [1, 4 * DM])
    wg_in = ctx.din("wg_in", [DM, 8])
    bg_in = ctx.din("bg_in", [1, 8])
    w1_in = ctx.din("w1_in", [8, DM, 4096])
    b1_in = ctx.din("b1_in", [8, 4096])
    w2_in = ctx.din("w2_in", [8, 4096, DM])
    b2_in = ctx.din("b2_in", [8, DM])
    grow_in = ctx.din("grow_in", [1, 4 * DM])
    ropec = ctx.din("ropec", [SEQ, 64])
    ropes = ctx.din("ropes", [SEQ, 64])
    cst2 = ctx.din("cst2", [128, 768])

    mk = ctx.dout if dbg else ctx.dram
    h_d = mk("b2h_d", [SEQ, DM], F32)
    qT_d = mk("b2qT_d", [NCC, 128, SEQ], BF16)
    kT_d = mk("b2kT_d", [NCC, 128, SEQ], BF16)
    yfm_d = mk("b2yfm_d", [NCC, 128, SEQ], BF16)
    x1p_d = mk("b2x1p_d", [SEQ, DM], F32)
    x1pT_d = mk("b2x1pT_d", [NCC, 128, SEQ], BF16)
    moe_d = mk("b2moe_d", [SEQ, DM], F32)

    xtile, lntile, stage = com.xtile, com.lntile, com.stage
    tmp1, tmp2, stat, epsc = com.tmp1, com.tmp2, com.stat, com.epsc
    ident, onesb, onesf = com.ident, com.onesb, com.onesf
    pp, pst = com.pp, com.pst

    big0 = ctx.vsb("big0", [128, NCC * SEQ], BF16)
    big1 = ctx.vsb("big1", [128, NT * 1040], BF16)
    ar0 = ctx.vsb("ar0", [128, 2 * NCC * DM], BF16)
    argb = ctx.vsb("argb", [128, 2 * NT * 64], BF16)
    rowbuf = ctx.vsb("rowbuf", [65, 4608], BF16)
    wgw = ctx.vsb("wgw", [128, NCC * 8], BF16)
    pt = ctx.vsb("pt", [128, 512], BF16)
    ystage = ctx.vsb("ystage", [128, 512], BF16)
    comb = ctx.vsb("comb", [128, NT * 8], F32)
    mtile = ctx.vsb("mtile", [128, DM], F32)

    qhalf = big0
    gbufv = argb.t[:, :].bitcast(F32)

    def wp(i3, sl0, sl1):
        return ar0[:, i3 * NCC * DM + sl0:i3 * NCC * DM + sl1]

    ctx.dma("pool", lambda E: E.dma_start(ident[:, :], cst2[:, 0:128]),
            reads=[cst2], writes=[ident])
    ctx.dma("pool", lambda E: E.dma_start(onesb[:, :], cst2[0:1, 128:640]),
            reads=[cst2], writes=[onesb])
    ctx.dma("sp", lambda E: E.dma_start(onesf[:, :], cst2[0:1, 128:256]),
            reads=[cst2], writes=[onesf])
    ctx.dma("pool", lambda E: E.dma_start(rowbuf[0:1, 0:4096], brow_in.t),
            reads=[brow_in], writes=[rowbuf])
    ctx.dma("pool", lambda E: E.dma_start(rowbuf[0:1, 4096:4104], bg_in.t),
            reads=[bg_in], writes=[rowbuf])
    ctx.op("dve", lambda E: E.memset(rowbuf[32:33, 4096:4608], 1.0),
           reads=[], writes=[rowbuf])
    ctx.op("dve", lambda E: E.memset(rowbuf[64:65, 4096:4224], 1.0),
           reads=[], writes=[rowbuf])
    ctx.dma("pool", lambda E: E.dma_start(
        wgw[:, :].rearrange("p (c m) -> p c m", c=NCC),
        wg_in.t.rearrange("(c p) m -> p c m", p=128)),
        reads=[wg_in], writes=[wgw])
    for i3, w in enumerate((wq, wk)):
        ctx.dma("pool", lambda E, i3=i3, w=w: E.dma_start(
            ar0[:, i3 * NCC * DM:(i3 + 1) * NCC * DM].rearrange(
                "p (c m) -> p c m", c=NCC),
            w.t.rearrange("(c p) m -> p c m", p=128)),
            reads=[w], writes=[ar0])
    ctx.dma("pool", lambda E: E.dma_start(
        argb[:, 0:NT * 64].rearrange("p (t m) -> p t m", t=NT),
        ropec.t.rearrange("(t p) m -> p t m", p=128)),
        reads=[ropec], writes=[argb])
    ctx.dma("pool", lambda E: E.dma_start(
        argb[:, NT * 64:2 * NT * 64].rearrange("p (t m) -> p t m", t=NT),
        ropes.t.rearrange("(t p) m -> p t m", p=128)),
        reads=[ropes], writes=[argb])
    ctx.op("dve", lambda E: E.memset(epsc[:, 0:1], 1e-5),
           reads=[], writes=[epsc])
    ctx.op("dve", lambda E: E.memset(
        big1[:, :].rearrange("p (n c) -> p n c", c=65)[:, :, 64:65], 1.0),
        reads=[], writes=[big1])

    for i in range(NT):
        ctx.dma("sp", lambda E, i=i: E.dma_start(
            xtile[:, :], xin.t[i * 128:(i + 1) * 128, :]),
            reads=[xin], writes=[xtile])
        emit_ln(ctx, tmp1, np.s_[:, :], xtile, np.s_[:, :], stat, tmp2,
                np.s_[:, :], DM, eps=epsc)
        ctx.dma("sp", lambda E, i=i: E.dma_start(
            h_d.t[i * 128:(i + 1) * 128, :], tmp1[:, :]),
            reads=[tmp1], writes=[h_d])
        ctx.op("dve", lambda E: E.tensor_copy(lntile[:, :], tmp1[:, :]),
               reads=[tmp1], writes=[lntile])
        for c in range(NCC):
            ctx.op("pe", lambda E, c=c: E.transpose(
                pst[:, 0:128], lntile[:, c * 128:(c + 1) * 128], ident[:, :]),
                reads=[lntile, ident], writes=[pst])
            ctx.op("dve", lambda E, c=c, i=i: E.tensor_copy(
                big0[:, c * SEQ + i * 128: c * SEQ + (i + 1) * 128],
                pst[:, 0:128]),
                reads=[pst], writes=[big0])

    for i in range(NT):
        for t3 in range(2):
            for half in range(2):
                pb = pp[t3 * 2 + half]
                ctx.op("pe", lambda E, t3=t3, half=half, pb=pb: E.matmul(
                    pb[:, :], onesb[0:1, 0:128],
                    rowbuf[0:1, t3 * DM + half * 512:
                           t3 * DM + (half + 1) * 512],
                    start=True, stop=False),
                    reads=[onesb, rowbuf], writes=[pb])
                for c in range(NCC):
                    ctx.op("pe", lambda E, t3=t3, half=half, c=c, i=i, pb=pb:
                           E.matmul(pb[:, :],
                                    big0[:, c * SEQ + i * 128:
                                         c * SEQ + (i + 1) * 128],
                                    wp(t3, c * DM + half * 512,
                                       c * DM + (half + 1) * 512),
                                    start=False, stop=(c == NCC - 1)),
                           reads=[big0, ar0], writes=[pb])
        cosap = argb[:, i * 64:(i + 1) * 64]
        sinap = argb[:, NT * 64 + i * 64:NT * 64 + (i + 1) * 64]
        for t3 in range(2):
            for half in range(2):
                pb = pp[t3 * 2 + half]
                hofs = half * 512
                pb3 = pb[:, :].rearrange("p (h j) -> p h j", j=64)
                ctx.op("dve", lambda E, pb3=pb3, hofs=hofs, cosap=cosap:
                       E.tensor_tensor(
                           tmp1[:, hofs:hofs + 512].rearrange(
                               "p (h j) -> p h j", j=64),
                           pb3,
                           cosap.unsqueeze(1).broadcast_to(
                               (128, 8, 64)), ALU.mult),
                       reads=[pb, argb], writes=[tmp1])
                ctx.op("dve", lambda E, pb3=pb3, hofs=hofs, sinap=sinap:
                       E.tensor_tensor(
                           tmp2[:, hofs:hofs + 512].rearrange(
                               "p (h j) -> p h j", j=64)[:, :, 0:32],
                           pb3[:, :, 32:64],
                           sinap[:, 0:32].unsqueeze(1).broadcast_to((128, 8, 32)),
                           ALU.mult),
                       reads=[pb, argb], writes=[tmp2])
                ctx.op("dve", lambda E, pb3=pb3, hofs=hofs, sinap=sinap:
                       E.tensor_tensor(
                           tmp2[:, hofs:hofs + 512].rearrange(
                               "p (h j) -> p h j", j=64)[:, :, 32:64],
                           pb3[:, :, 0:32],
                           sinap[:, 32:64].unsqueeze(1).broadcast_to((128, 8, 32)),
                           ALU.mult),
                       reads=[pb, argb], writes=[tmp2])
            ctx.op("dve", lambda E: E.tensor_add(stage[:, :], tmp1[:, :],
                                                 tmp2[:, :]),
                   reads=[tmp1, tmp2], writes=[stage])
            dst = qT_d if t3 == 0 else kT_d
            for c in range(NCC):
                ctx.op("pe", lambda E, c=c: E.transpose(
                    pst[:, 0:128], stage[:, c * 128:(c + 1) * 128],
                    ident[:, :]),
                    reads=[stage, ident], writes=[pst])
                ctx.op("dve", lambda E, c=c: E.tensor_copy(
                    lntile[:, c * 128:(c + 1) * 128], pst[:, 0:128]),
                    reads=[pst], writes=[lntile])
            ctx.dma("sp", lambda E, i=i, dst=dst: E.dma_start(
                dst.t[:, :, i * 128:(i + 1) * 128].rearrange(
                    "c p m -> p c m"),
                lntile[:, :].rearrange("p (c m) -> p c m", c=NCC)),
                reads=[lntile], writes=[dst])

    ctx.dma("pool", lambda E: E.dma_start(
        ar0[:, 0:NCC * DM].rearrange("p (c m) -> p c m", c=NCC),
        wv.t.rearrange("(c p) m -> p c m", p=128)),
        reads=[wv], writes=[ar0])
    for i in range(NT):
        for half in range(2):
            pb = pp[4 + half]
            ctx.op("pe", lambda E, half=half, pb=pb: E.matmul(
                pb[:, :], onesb[0:1, 0:128],
                rowbuf[0:1, 2 * DM + half * 512:2 * DM + (half + 1) * 512],
                start=True, stop=False),
                reads=[onesb, rowbuf], writes=[pb])
            for c in range(NCC):
                ctx.op("pe", lambda E, half=half, c=c, i=i, pb=pb:
                       E.matmul(pb[:, :],
                                big0[:, c * SEQ + i * 128:
                                     c * SEQ + (i + 1) * 128],
                                wp(0, c * DM + half * 512,
                                   c * DM + (half + 1) * 512),
                                start=False, stop=(c == NCC - 1)),
                       reads=[big0, ar0], writes=[pb])
            ctx.op("dve", lambda E, i=i, half=half, pb=pb: E.tensor_copy(
                big1[:, :].rearrange("p (n h j) -> p n h j", h=16, j=65)
                [:, i, half * 8:(half + 1) * 8, 0:64],
                pb[:, :].rearrange("p (h j) -> p h j", j=64)),
                reads=[pb], writes=[big1])

    for pair in range(8):
        ctx.dma("sp", lambda E, pair=pair: E.dma_start(
            qhalf[:, 0:SEQ], qT_d.t[pair]), reads=[qT_d], writes=[qhalf])
        ctx.dma("sp", lambda E, pair=pair: E.dma_start(
            qhalf[:, SEQ:2 * SEQ], kT_d.t[pair]), reads=[kT_d],
            writes=[qhalf])
        for hh in range(2):
            hsl = slice(0, 64) if hh == 0 else slice(64, 128)
            habs = pair * 2 + hh
            for qs in range(8):
                outp = pp[2]
                for ktn in range(NT):
                    sb_ = pp[ktn % 2]
                    ctx.op("pe", lambda E, ktn=ktn, qs=qs, hsl=hsl, sb_=sb_:
                           E.matmul(sb_[:, :],
                                    qhalf[hsl, SEQ + ktn * 128:
                                          SEQ + (ktn + 1) * 128],
                                    qhalf[hsl, qs * 512:(qs + 1) * 512],
                                    start=True, stop=True),
                           reads=[qhalf], writes=[sb_])
                    ctx.op("act", lambda E, sb_=sb_: E.activation(
                        pt[:, :], sb_[:, :], AF.Exp, bias=0.0, scale=1.0),
                        reads=[sb_], writes=[pt])
                    ctx.op("pe", lambda E, ktn=ktn, habs=habs, outp=outp,
                           st=(ktn == 0), sp_=(ktn == NT - 1):
                           E.matmul(outp[0:65, :],
                                    big1[:, :].rearrange(
                                        "p (n h j) -> p n h j", h=16, j=65)
                                    [:, ktn, habs, :],
                                    pt[:, :], start=st, stop=sp_),
                           reads=[big1, pt], writes=[outp])
                ctx.op("dve", lambda E, outp=outp: E.reciprocal(
                    tmp2[0:1, 0:512], outp[64:65, :]),
                    reads=[outp], writes=[tmp2])
                bc = pp[3]
                ctx.op("pe", lambda E, bc=bc: E.matmul(
                    bc[:, :], onesf[:, :], tmp2[0:1, 0:512], start=True,
                    stop=True),
                    reads=[onesf, tmp2], writes=[bc])
                ctx.op("dve", lambda E, bc=bc: E.tensor_copy(
                    tmp1[:, 0:512], bc[:, :]), reads=[bc], writes=[tmp1])
                ctx.op("dve", lambda E, outp=outp: E.tensor_tensor(
                    ystage[0:64, 0:512], outp[0:64, :], tmp1[0:64, 0:512],
                    ALU.mult),
                    reads=[outp, tmp1], writes=[ystage])
                ctx.dma("sp", lambda E, pair=pair, hh=hh, qs=qs: E.dma_start(
                    yfm_d.t[pair, hh * 64:(hh + 1) * 64,
                            qs * 512:(qs + 1) * 512],
                    ystage[0:64, 0:512]),
                    reads=[ystage], writes=[yfm_d])

    ctx.dma("sp", lambda E: E.dma_start(
        gbufv[:, 0:DM], grow_in.t[0:1, 0:DM].broadcast_to((128, DM))),
        reads=[grow_in], writes=[argb])
    ctx.dma("sp", lambda E: E.dma_start(
        gbufv[:, DM:2 * DM],
        grow_in.t[0:1, DM:2 * DM].broadcast_to((128, DM))),
        reads=[grow_in], writes=[argb])
    ctx.dma("pool", lambda E: E.dma_start(
        ar0[:, 0:NCC * DM].rearrange("p (c m) -> p c m", c=NCC),
        wo.t.rearrange("(c p) m -> p c m", p=128)),
        reads=[wo], writes=[ar0])
    for i in range(NT):
        yb = lntile
        ctx.dma("sp", lambda E, i=i, yb=yb: E.dma_start(
            yb[:, :].rearrange("p (c m) -> p c m", c=NCC),
            yfm_d.t[:, :, i * 128:(i + 1) * 128].rearrange("c p m -> p c m")),
            reads=[yfm_d], writes=[yb])
        for half in range(2):
            zp = pp[4 + half]
            ctx.op("pe", lambda E, half=half, zp=zp: E.matmul(
                zp[:, :], onesb[0:1, 0:128],
                rowbuf[0:1, 3 * DM + half * 512:3 * DM + (half + 1) * 512],
                start=True, stop=False),
                reads=[onesb, rowbuf], writes=[zp])
            for c in range(NCC):
                ctx.op("pe", lambda E, c=c, half=half, zp=zp, yb=yb:
                       E.matmul(zp[:, :],
                                yb[:, c * 128:(c + 1) * 128],
                                wp(0, c * DM + half * 512,
                                   c * DM + (half + 1) * 512),
                                start=False, stop=(c == NCC - 1)),
                       reads=[yb, ar0], writes=[zp])
        ctx.dma("sp", lambda E, i=i: E.dma_start(
            xtile[:, :], h_d.t[i * 128:(i + 1) * 128, :]),
            reads=[h_d], writes=[xtile])
        for half in range(2):
            zp = pp[4 + half]
            ctx.op("dve", lambda E, half=half, zp=zp: E.tensor_tensor(
                tmp2[:, half * 512:(half + 1) * 512], zp[:, :],
                xtile[:, half * 512:(half + 1) * 512], ALU.add),
                reads=[zp, xtile], writes=[tmp2])
        emit_ln(ctx, mtile, np.s_[:, 0:DM], tmp2, np.s_[:, 0:DM], stat,
                tmp1, np.s_[:, 0:DM], DM, eps=epsc)
        ctx.op("dve", lambda E: E.tensor_tensor(
            mtile[:, 0:DM], mtile[:, 0:DM], gbufv[:, 0:DM], ALU.mult),
            reads=[mtile, argb], writes=[mtile])
        ctx.op("dve", lambda E: E.tensor_tensor(
            mtile[:, 0:DM], mtile[:, 0:DM], gbufv[:, DM:2 * DM], ALU.add),
            reads=[mtile, argb], writes=[mtile])
        ctx.dma("sp", lambda E, i=i: E.dma_start(
            x1p_d.t[i * 128:(i + 1) * 128, :], mtile[:, 0:DM]),
            reads=[mtile], writes=[x1p_d])
        ctx.op("dve", lambda E: E.tensor_copy(lntile[:, :], mtile[:, 0:DM]),
               reads=[mtile], writes=[lntile])
        for c in range(NCC):
            ctx.op("pe", lambda E, c=c: E.transpose(
                pst[:, 0:128], lntile[:, c * 128:(c + 1) * 128], ident[:, :]),
                reads=[lntile, ident], writes=[pst])
            ctx.op("dve", lambda E, c=c: E.tensor_copy(
                stage[:, c * 128:(c + 1) * 128], pst[:, 0:128]),
                reads=[pst], writes=[stage])
        ctx.dma("sp", lambda E, i=i: E.dma_start(
            x1pT_d.t[:, :, i * 128:(i + 1) * 128].rearrange("c p m -> p c m"),
            stage[:, :].rearrange("p (c m) -> p c m", c=NCC)),
            reads=[stage], writes=[x1pT_d])
        gp = pp[3]
        ctx.op("pe", lambda E, gp=gp: E.matmul(
            gp[:, 0:8], onesb[0:1, 0:128], rowbuf[0:1, 4096:4104],
            start=True, stop=False),
            reads=[onesb, rowbuf], writes=[gp])
        for c in range(NCC):
            ctx.op("pe", lambda E, c=c, gp=gp: E.matmul(
                gp[:, 0:8], stage[:, c * 128:(c + 1) * 128],
                wgw[:, c * 8:(c + 1) * 8],
                start=False, stop=(c == NCC - 1)),
                reads=[stage, wgw], writes=[gp])
        ctx.op("act", lambda E, gp=gp: E.activation(
            tmp2[:, 0:8], gp[:, 0:8], AF.Exp, bias=0.0, scale=1.0,
            accum_out=stat[:, 8:9]),
            reads=[gp], writes=[tmp2, stat])
        ctx.op("dve", lambda E: E.reciprocal(stat[:, 9:10], stat[:, 8:9]),
               reads=[stat], writes=[stat])
        ctx.op("dve", lambda E: E.tensor_scalar_mul(
            tmp2[:, 0:8], tmp2[:, 0:8], stat[:, 9:10]),
            reads=[stat, tmp2], writes=[tmp2])
        ctx.op("dve", lambda E: E.reduce_max(stat[:, 10:11], tmp2[:, 0:8],
                                             axis=AX.X),
               reads=[tmp2], writes=[stat])
        ctx.op("dve", lambda E: E.tensor_tensor(
            tmp2[:, 8:16], tmp2[:, 0:8],
            stat[:, 10:11].broadcast_to((128, 8)), ALU.is_ge),
            reads=[tmp2, stat], writes=[tmp2])
        ctx.op("dve", lambda E: E.scalar_tensor_tensor(
            tmp1[:, 8:16], tmp2[:, 8:16], -1e9, tmp2[:, 0:8],
            ALU.mult, ALU.add),
            reads=[tmp2], writes=[tmp1])
        ctx.op("dve", lambda E: E.reduce_max(stat[:, 12:13], tmp1[:, 8:16],
                                             axis=AX.X),
               reads=[tmp1], writes=[stat])
        ctx.op("dve", lambda E: E.tensor_tensor(
            tmp1[:, 0:8], tmp2[:, 0:8],
            stat[:, 12:13].broadcast_to((128, 8)), ALU.is_ge),
            reads=[tmp2, stat], writes=[tmp1])
        ctx.op("dve", lambda E: E.tensor_tensor(
            tmp1[:, 8:16], tmp1[:, 0:8], tmp2[:, 0:8], ALU.mult),
            reads=[tmp1, tmp2], writes=[tmp1])
        ctx.op("dve", lambda E: E.reduce_sum(stat[:, 13:14], tmp1[:, 8:16],
                                             axis=AX.X),
               reads=[tmp1], writes=[stat])
        ctx.op("dve", lambda E: E.reciprocal(stat[:, 14:15], stat[:, 13:14]),
               reads=[stat], writes=[stat])
        ctx.op("dve", lambda E, i=i: E.tensor_scalar_mul(
            comb[:, i * 8:(i + 1) * 8], tmp1[:, 8:16], stat[:, 14:15]),
            reads=[tmp1, stat], writes=[comb])

    if dbg:
        comb_d = ctx.dout("comb_d", [128, NT * 8], F32)
        ctx.dma("sp", lambda E: E.dma_start(comb_d.t, comb[:, :]),
                reads=[comb], writes=[comb_d])

    HF = 8192
    for e in range(nexp):
        ctx.dma("pool", lambda E, e=e: E.dma_start(
            big0[:, :].rearrange("p (c m) -> p c m", c=NCC),
            w1_in.t[e].rearrange("(c p) m -> p c m", p=128)),
            reads=[w1_in], writes=[big0])
        ctx.dma("pool", lambda E, e=e: E.dma_start(
            big1[:, 0:32 * DM].rearrange("p (c m) -> p c m", c=32),
            w2_in.t[e].rearrange("(c p) m -> p c m", p=128)),
            reads=[w2_in], writes=[big1])
        ctx.dma("pool", lambda E, e=e: E.dma_start(
            rowbuf[32:33, 0:4096], b1_in.t[e:e + 1, :]),
            reads=[b1_in], writes=[rowbuf])
        ctx.dma("pool", lambda E, e=e: E.dma_start(
            rowbuf[64:65, 0:DM], b2_in.t[e:e + 1, :]),
            reads=[b2_in], writes=[rowbuf])
        for ns in range(8):
            ctx.dma("sp", lambda E, ns=ns: E.dma_start(
                argb[:, 0:4096].rearrange("p (c m) -> p c m", c=NCC),
                x1pT_d.t[:, :, ns * 512:(ns + 1) * 512].rearrange(
                    "c p m -> p c m")),
                reads=[x1pT_d], writes=[argb])
            for fc in range(32):
                pb = pp[fc % 2]
                ctx.op("pe", lambda E, fc=fc, pb=pb: E.matmul(
                    pb[:, :], rowbuf[32:33, fc * 128:(fc + 1) * 128],
                    rowbuf[32:33, 4096:4608], start=True, stop=False),
                    reads=[rowbuf], writes=[pb])
                for c in range(NCC):
                    ctx.op("pe", lambda E, fc=fc, c=c, pb=pb:
                           E.matmul(pb[:, :],
                                    big0[:, c * SEQ + fc * 128:
                                         c * SEQ + (fc + 1) * 128],
                                    argb[:, c * 512:(c + 1) * 512],
                                    start=False, stop=(c == NCC - 1)),
                           reads=[big0, argb], writes=[pb])
                ctx.op("act", lambda E, fc=fc, pb=pb: E.activation(
                    ar0[:, fc * 512:(fc + 1) * 512], pb[:, :], AF.Gelu,
                    bias=0.0, scale=1.0),
                    reads=[pb], writes=[ar0])
            if dbg and e == 0 and ns == 0:
                hfm_d = ctx.dout("hfm_d", [128, 32 * 512], BF16)
                ctx.dma("sp", lambda E: E.dma_start(
                    hfm_d.t, ar0[:, 0:16384]), reads=[ar0], writes=[hfm_d])
            for nsub in range(4):
                i_abs = ns * 4 + nsub
                for half in range(2):
                    yp = pp[(nsub * 2 + half) % 6]
                    ctx.op("pe", lambda E, half=half, yp=yp: E.matmul(
                        yp[:, :], rowbuf[64:65, 4096:4224],
                        rowbuf[64:65, half * 512:(half + 1) * 512],
                        start=True, stop=False),
                        reads=[rowbuf], writes=[yp])
                    for fc in range(32):
                        ctx.op("pe", lambda E, fc=fc, nsub=nsub, half=half,
                               yp=yp:
                               E.matmul(yp[:, :],
                                        ar0[:, fc * 512 + nsub * 128:
                                            fc * 512 + (nsub + 1) * 128],
                                        big1[:, fc * DM + half * 512:
                                             fc * DM + (half + 1) * 512],
                                        start=False, stop=(fc == 31)),
                               reads=[ar0, big1], writes=[yp])
                    ctx.op("dve", lambda E, i_abs=i_abs, half=half, e=e,
                           yp=yp:
                           E.tensor_scalar_mul(
                               mtile[:, half * 512:(half + 1) * 512],
                               yp[:, :], comb[:, i_abs * 8 + e:
                                              i_abs * 8 + e + 1]),
                           reads=[yp, comb], writes=[mtile])
                ctx.dma("pool", lambda E, i_abs=i_abs, e=e: E.dma_start(
                    moe_d.t[i_abs * 128:(i_abs + 1) * 128, :],
                    mtile[:, 0:DM],
                    accum_op=(ALU.bypass if e == 0 else ALU.add)),
                    reads=[mtile], writes=[moe_d])

    ctx.dma("sp", lambda E: E.dma_start(
        gbufv[:, 0:DM],
        grow_in.t[0:1, 2 * DM:3 * DM].broadcast_to((128, DM))),
        reads=[grow_in], writes=[argb])
    ctx.dma("sp", lambda E: E.dma_start(
        gbufv[:, DM:2 * DM],
        grow_in.t[0:1, 3 * DM:4 * DM].broadcast_to((128, DM))),
        reads=[grow_in], writes=[argb])
    for i in range(NT):
        ctx.dma("sp", lambda E, i=i: E.dma_start(
            xtile[:, :], x1p_d.t[i * 128:(i + 1) * 128, :]),
            reads=[x1p_d], writes=[xtile])
        ctx.dma("sp", lambda E, i=i: E.dma_start(
            mtile[:, :], moe_d.t[i * 128:(i + 1) * 128, :]),
            reads=[moe_d], writes=[mtile])
        ctx.op("dve", lambda E: E.tensor_tensor(
            tmp2[:, 0:DM], xtile[:, :], mtile[:, :], ALU.add),
            reads=[xtile, mtile], writes=[tmp2])
        emit_ln(ctx, tmp1, np.s_[:, 0:DM], tmp2, np.s_[:, 0:DM], stat,
                mtile, np.s_[:, 0:DM], DM, eps=epsc)
        ctx.op("dve", lambda E: E.tensor_tensor(
            tmp1[:, 0:DM], tmp1[:, 0:DM], gbufv[:, 0:DM], ALU.mult),
            reads=[tmp1, argb], writes=[tmp1])
        ctx.op("dve", lambda E: E.tensor_tensor(
            tmp1[:, 0:DM], tmp1[:, 0:DM], gbufv[:, DM:2 * DM], ALU.add),
            reads=[tmp1, argb], writes=[tmp1])
        ctx.dma("sp", lambda E, i=i: E.dma_start(
            xtile[:, :], xin.t[i * 128:(i + 1) * 128, :]),
            reads=[xin], writes=[xtile])
        emit_ln(ctx, mtile, np.s_[:, 0:DM], tmp1, np.s_[:, 0:DM], stat,
                tmp2, np.s_[:, 0:DM], DM, res=xtile, res_sl=np.s_[:, :],
                eps=epsc)
        ctx.dma("sp", lambda E, i=i: E.dma_start(
            xout.t[i * 128:(i + 1) * 128, :], mtile[:, 0:DM]),
            reads=[mtile], writes=[xout])




def host_consts_b2():
    c = np.zeros((128, 768), dtype=np.float32)
    c[:, 0:128] = np.eye(128, dtype=np.float32)
    c[0, 128:640] = 1.0
    return c


def host_rope_b2():
    HD = 64
    SC = float(1.0 / (2.0 * np.sqrt(2.0)))
    inv = 1.0 / 10000.0 ** (np.arange(0, HD, 2, dtype=np.float32) / HD)
    t = np.arange(SEQ, dtype=np.float32)
    fr = (t[:, None] * inv[None, :]).astype(np.float32)
    emb = np.concatenate([fr, fr], axis=-1)
    cos = (np.cos(emb) * SC).astype(np.float32)
    sin = np.sin(emb).astype(np.float32)
    ss = np.empty((SEQ, HD), dtype=np.float32)
    ss[:, 0:32] = -sin[:, 0:32] * SC
    ss[:, 32:64] = sin[:, 32:64] * SC
    return cos, ss


def emit_b4(ctx, com, xin, xout):
    nc = ctx.nc
    wqkv_in = ctx.din("wqkv_in", [6, DM, 3072])
    qs_in = ctx.din("qs_in", [6, 64])
    ks_in = ctx.din("ks_in", [6, 64])
    wo_in = ctx.din("wo_in", [6, DM, DM])
    lng_in = ctx.din("lng_in", [6, DM])
    lnb_in = ctx.din("lnb_in", [6, DM])
    w1a_in = ctx.din("w1a_in", [6, DM, B4_FFP])
    w1g_in = ctx.din("w1g_in", [6, DM, B4_FFP])
    b1a_in = ctx.din("b1a_in", [6, B4_FFP])
    b1g_in = ctx.din("b1g_in", [6, B4_FFP])
    w2p_in = ctx.din("w2p_in", [6, B4_FFP, DM])
    b2f_in = ctx.din("b2f_in", [6, DM])
    tab_in = ctx.din("tab_in", [SEQ, 4 * 64])
    msk_in = ctx.din("msk_in", [128, 4 * 256])
    cst = ctx.din("cst", [128, 768])

    xA_d = ctx.dram("b4xA_d", [SEQ, DM], F32)
    xB_d = ctx.dram("b4xB_d", [SEQ, DM], F32)
    qT_d = ctx.dram("b4qT_d", [NCC, 128, SEQ], BF16)
    kT_d = ctx.dram("b4kT_d", [NCC, 128, SEQ], BF16)
    yfm_d = ctx.dram("b4yfm_d", [NCC, 128, SEQ], BF16)

    xtile, lntile, stage = com.xtile, com.lntile, com.stage
    tmp1, tmp2, stat, epsc = com.tmp1, com.tmp2, com.stat, com.epsc
    ident, onesf = com.ident, com.onesf
    pp, pst = com.pp, com.pst

    arA = ctx.vsb("arA", [128, 24576], BF16)
    arB = ctx.vsb("arB", [128, 33792], BF16)
    arC = ctx.vsb("arC", [128, 22528], BF16)
    tabs = ctx.vsb("tabs", [128, 4 * NT * 64], BF16)
    qsb = ctx.vsb("qsb", [128, 4 * 64], BF16)
    xnslab = ctx.vsb("xnslab", [128, NCC * 512], BF16)
    shbuf = ctx.vsb("shbuf", [128, 3328], BF16)
    gg = ctx.vsb("gg", [128, 512], BF16)
    masks = shbuf
    rowbuf = shbuf
    pt = gg
    glbuf = tabs
    b2_in = b2f_in

    gbv = tabs.t[:, :].bitcast(F32)

    ctx.dma("pool", lambda E: E.dma_start(ident[:, :], cst[:, 0:128]),
            reads=[cst], writes=[ident])
    ctx.dma("sp", lambda E: E.dma_start(onesf[:, :], cst[0:1, 128:256]),
            reads=[cst], writes=[onesf])
    ctx.op("dve", lambda E: E.memset(epsc[:, 0:1], 1e-5),
           reads=[], writes=[epsc])
    ctx.op("dve", lambda E: E.memset(epsc[:, 1:2], -12.0),
           reads=[], writes=[epsc])

    for i in range(NT):
        ctx.dma("sp", lambda E, i=i: E.dma_start(
            xtile[:, :], xin.t[i * 128:(i + 1) * 128, :]),
            reads=[xin], writes=[xtile])
        emit_ln(ctx, tmp1, np.s_[:, :], xtile, np.s_[:, :], stat, tmp2,
                np.s_[:, :], DM, eps=epsc)
        ctx.dma("sp", lambda E, i=i: E.dma_start(
            xA_d.t[i * 128:(i + 1) * 128, :], tmp1[:, :]),
            reads=[tmp1], writes=[xA_d])

    def shifted_load(xsrc, i):
        ctx.dma("sp", lambda E, i=i: E.dma_start(
            xtile[:, 0:512], xsrc.t[i * 128:(i + 1) * 128, 0:512]),
            reads=[xsrc], writes=[xtile])
        if i == 0:
            ctx.op("dve", lambda E: E.memset(xtile[0:1, 512:1024], 0.0),
                   reads=[], writes=[xtile])
            ctx.dma("sp", lambda E: E.dma_start(
                xtile[1:128, 512:1024], xsrc.t[0:127, 512:1024]),
                reads=[xsrc], writes=[xtile])
        else:
            ctx.dma("sp", lambda E, i=i: E.dma_start(
                xtile[:, 512:1024],
                xsrc.t[i * 128 - 1:(i + 1) * 128 - 1, 512:1024]),
                reads=[xsrc], writes=[xtile])

    for lyr in range(6):
        xCur, xNxt = (xA_d, xB_d)
        ctx.dma("pool", lambda E, lyr=lyr: E.dma_start(
            arA[:, :].rearrange("p (c m) -> p c m", c=NCC),
            wqkv_in.t[lyr].rearrange("(c p) m -> p c m", p=128)),
            reads=[wqkv_in], writes=[arA])
        for base, srcv in ((0, qs_in), (128, ks_in)):
            ctx.dma("pool", lambda E, base=base, srcv=srcv, lyr=lyr:
                    E.dma_start(qsb[:, base:base + 64],
                                srcv.t[lyr:lyr + 1, :].broadcast_to(
                                    (128, 64))),
                    reads=[srcv], writes=[qsb])
            ctx.dma("pool", lambda E, base=base, srcv=srcv, lyr=lyr:
                    E.dma_start(qsb[:, base + 64:base + 96],
                                srcv.t[lyr:lyr + 1, 32:64].broadcast_to(
                                    (128, 32))),
                    reads=[srcv], writes=[qsb])
            ctx.dma("pool", lambda E, base=base, srcv=srcv, lyr=lyr:
                    E.dma_start(qsb[:, base + 96:base + 128],
                                srcv.t[lyr:lyr + 1, 0:32].broadcast_to(
                                    (128, 32))),
                    reads=[srcv], writes=[qsb])
        for t4 in range(4):
            qv = (0, 64, 128, 192)[t4]
            for g2 in range(2):
                ctx.dma("sp", lambda E, t4=t4, g2=g2: E.dma_start(
                    tmp1[:, :].rearrange("p (t m) -> p t m", t=16),
                    tab_in.t[g2 * 2048:(g2 + 1) * 2048,
                             t4 * 64:(t4 + 1) * 64].rearrange(
                        "(t p) m -> p t m", p=128)),
                    reads=[tab_in], writes=[tmp1])
                ctx.op("dve", lambda E, t4=t4, g2=g2, qv=qv: E.tensor_tensor(
                    tabs[:, t4 * 2048 + g2 * 1024:
                         t4 * 2048 + (g2 + 1) * 1024].rearrange(
                        "p (t m) -> p t m", t=16),
                    tmp1[:, :].rearrange("p (t m) -> p t m", t=16),
                    qsb[:, qv:qv + 64].unsqueeze(1).broadcast_to(
                        (128, 16, 64)),
                    ALU.mult),
                    reads=[tmp1, qsb], writes=[tabs])
        ctx.op("dve", lambda E: E.memset(
            arB[:, 0:NT * 1040].rearrange("p (n c) -> p n c", c=65)
            [:, :, 64:65], 1.0),
            reads=[], writes=[arB])

        for i in range(NT):
            shifted_load(xCur, i)
            ctx.op("dve", lambda E: E.tensor_copy(lntile[:, :], xtile[:, :]),
                   reads=[xtile], writes=[lntile])
            for c in range(NCC):
                ctx.op("pe", lambda E, c=c: E.transpose(
                    pst[:, 0:128], lntile[:, c * 128:(c + 1) * 128],
                    ident[:, :]),
                    reads=[lntile, ident], writes=[pst])
                ctx.op("dve", lambda E, c=c: E.tensor_copy(
                    stage[:, c * 128:(c + 1) * 128], pst[:, 0:128]),
                    reads=[pst], writes=[stage])
            for fb in range(6):
                pb = pp[fb]
                for c in range(NCC):
                    ctx.op("pe", lambda E, fb=fb, c=c, pb=pb: E.matmul(
                        pb[:, :], stage[:, c * 128:(c + 1) * 128],
                        arA[:, c * 3072 + fb * 512:c * 3072 + (fb + 1) * 512],
                        start=(c == 0), stop=(c == NCC - 1)),
                        reads=[stage, arA], writes=[pb])
            for tq in range(2):
                tb = tq * 2048
                cqb = (0 if tq == 0 else 2) * 2048
                sqb = (1 if tq == 0 else 3) * 2048
                for hb in range(2):
                    pb = pp[tq * 2 + hb]
                    hofs = hb * 512
                    ctx.op("act", lambda E, pb=pb, hofs=hofs: E.activation(
                        tmp1[:, hofs:hofs + 512], pb[:, :], AF.Square,
                        bias=0.0, scale=1.0),
                        reads=[pb], writes=[tmp1])
                    ctx.op("dve", lambda E, hofs=hofs, hb=hb: E.reduce_sum(
                        stat[:, hb * 8:hb * 8 + 8],
                        tmp1[:, hofs:hofs + 512].rearrange(
                            "p (h j) -> p h j", j=64),
                        axis=AX.X),
                        reads=[tmp1], writes=[stat])
                ctx.op("act", lambda E: E.activation(
                    stat[:, 0:16], stat[:, 0:16], AF.Sqrt, bias=0.0,
                    scale=1.0),
                    reads=[stat], writes=[stat])
                ctx.op("dve", lambda E: E.tensor_scalar_max(
                    stat[:, 0:16], stat[:, 0:16], 1e-12),
                    reads=[stat], writes=[stat])
                ctx.op("dve", lambda E: E.reciprocal(stat[:, 0:16],
                                                     stat[:, 0:16]),
                       reads=[stat], writes=[stat])
                for hb in range(2):
                    pb = pp[tq * 2 + hb]
                    hofs = hb * 512
                    ctx.op("dve", lambda E, pb=pb, hofs=hofs, hb=hb:
                           E.tensor_tensor(
                               tmp2[:, hofs:hofs + 512].rearrange(
                                   "p (h j) -> p h j", j=64),
                               pb[:, :].rearrange("p (h j) -> p h j", j=64),
                               stat[:, hb * 8:hb * 8 + 8].unsqueeze(
                                   2).broadcast_to((128, 8, 64)),
                               ALU.mult),
                           reads=[pb, stat], writes=[tmp2])
                cosap = tabs[:, cqb + i * 64:cqb + (i + 1) * 64]
                sinap = tabs[:, sqb + i * 64:sqb + (i + 1) * 64]
                t23 = tmp2[:, :].rearrange("p (h j) -> p h j", j=64)
                ctx.op("dve", lambda E, sinap=sinap, t23=t23:
                       E.tensor_tensor(
                           tmp1[:, :].rearrange(
                               "p (h j) -> p h j", j=64)[:, :, 0:32],
                           t23[:, :, 32:64],
                           sinap[:, 0:32].unsqueeze(1).broadcast_to(
                               (128, 16, 32)),
                           ALU.mult),
                       reads=[tmp2, tabs], writes=[tmp1])
                ctx.op("dve", lambda E, sinap=sinap, t23=t23:
                       E.tensor_tensor(
                           tmp1[:, :].rearrange(
                               "p (h j) -> p h j", j=64)[:, :, 32:64],
                           t23[:, :, 0:32],
                           sinap[:, 32:64].unsqueeze(1).broadcast_to(
                               (128, 16, 32)),
                           ALU.mult),
                       reads=[tmp2, tabs], writes=[tmp1])
                ctx.op("dve", lambda E, cosap=cosap, t23=t23:
                       E.tensor_tensor(
                           lntile[:, :].rearrange("p (h j) -> p h j", j=64),
                           t23,
                           cosap.unsqueeze(1).broadcast_to((128, 16, 64)),
                           ALU.mult),
                       reads=[tmp2, tabs], writes=[lntile])
                ctx.op("dve", lambda E: E.tensor_add(
                    lntile[:, :], lntile[:, :], tmp1[:, :]),
                    reads=[lntile, tmp1], writes=[lntile])
                for c in range(NCC):
                    ctx.op("pe", lambda E, c=c: E.transpose(
                        pst[:, 0:128], lntile[:, c * 128:(c + 1) * 128],
                        ident[:, :]),
                        reads=[lntile, ident], writes=[pst])
                    ctx.op("dve", lambda E, c=c: E.tensor_copy(
                        stage[:, c * 128:(c + 1) * 128], pst[:, 0:128]),
                        reads=[pst], writes=[stage])
                dst = qT_d if tq == 0 else kT_d
                ctx.dma("sp", lambda E, i=i, dst=dst: E.dma_start(
                    dst.t[:, :, i * 128:(i + 1) * 128].rearrange(
                        "c p m -> p c m"),
                    stage[:, :].rearrange("p (c m) -> p c m", c=NCC)),
                    reads=[stage], writes=[dst])
            for hb in range(2):
                pb = pp[4 + hb]
                ctx.op("dve", lambda E, i=i, hb=hb, pb=pb: E.tensor_copy(
                    arB[:, 0:NT * 1040].rearrange(
                        "p (n h j) -> p n h j", h=16, j=65)
                    [:, i, hb * 8:(hb + 1) * 8, 0:64],
                    pb[:, :].rearrange("p (h j) -> p h j", j=64)),
                    reads=[pb], writes=[arB])

        ctx.dma("pool", lambda E: E.dma_start(shbuf[:, 0:1024], msk_in.t),
                reads=[msk_in], writes=[shbuf])
        for pair in range(8):
            ctx.dma("sp", lambda E, pair=pair: E.dma_start(
                arC[:, 0:SEQ], qT_d.t[pair]), reads=[qT_d], writes=[arC])
            ctx.dma("sp", lambda E, pair=pair: E.dma_start(
                arC[:, SEQ:2 * SEQ], kT_d.t[pair]), reads=[kT_d],
                writes=[arC])
            for hh in range(2):
                hsl = slice(0, 64) if hh == 0 else slice(64, 128)
                habs = pair * 2 + hh
                for w in range(16):
                    outp = pp[2]
                    j0 = 2 if w == 0 else 0
                    for j in range(j0, 4):
                        ktn = 2 * w - 2 + j
                        sb_ = pp[j % 2]
                        ctx.op("pe", lambda E, ktn=ktn, w=w, hsl=hsl,
                               sb_=sb_:
                               E.matmul(sb_[:, 0:256],
                                        arC[hsl, SEQ + ktn * 128:
                                            SEQ + (ktn + 1) * 128],
                                        arC[hsl, w * 256:(w + 1) * 256],
                                        start=True, stop=True),
                               reads=[arC], writes=[sb_])
                        ctx.op("act", lambda E, sb_=sb_: E.activation(
                            tmp1[:, 0:256], sb_[:, 0:256], AF.Exp,
                            bias=epsc[:, 1:2], scale=1.0),
                            reads=[sb_, epsc], writes=[tmp1])
                        ctx.op("dve", lambda E, j=j: E.tensor_tensor(
                            gg[:, 0:256], tmp1[:, 0:256],
                            masks[:, j * 256:(j + 1) * 256], ALU.mult),
                            reads=[tmp1, masks], writes=[gg])
                        ctx.op("pe", lambda E, ktn=ktn, habs=habs,
                               outp=outp, st=(j == j0), sp_=(j == 3):
                               E.matmul(outp[0:65, 0:256],
                                        arB[:, ktn * 1040 + habs * 65:
                                            ktn * 1040 + habs * 65 + 65],
                                        gg[:, 0:256], start=st, stop=sp_),
                               reads=[arB, gg], writes=[outp])
                    ctx.op("dve", lambda E, outp=outp: E.reciprocal(
                        tmp2[0:1, 0:256], outp[64:65, 0:256]),
                        reads=[outp], writes=[tmp2])
                    bc = pp[3]
                    ctx.op("pe", lambda E, bc=bc: E.matmul(
                        bc[:, 0:256], onesf[:, :], tmp2[0:1, 0:256],
                        start=True, stop=True),
                        reads=[onesf, tmp2], writes=[bc])
                    ctx.op("dve", lambda E, bc=bc: E.tensor_copy(
                        tmp1[:, 0:256], bc[:, 0:256]),
                        reads=[bc], writes=[tmp1])
                    ctx.op("dve", lambda E, outp=outp: E.tensor_tensor(
                        gg[0:64, 256:512], outp[0:64, 0:256],
                        tmp1[0:64, 0:256], ALU.mult),
                        reads=[outp, tmp1], writes=[gg])
                    ctx.dma("sp", lambda E, pair=pair, hh=hh, w=w:
                            E.dma_start(
                                yfm_d.t[pair, hh * 64:(hh + 1) * 64,
                                        w * 256:(w + 1) * 256],
                                gg[0:64, 256:512]),
                            reads=[gg], writes=[yfm_d])

        ctx.dma("pool", lambda E, lyr=lyr: E.dma_start(
            arC[:, 2 * SEQ:2 * SEQ + NCC * DM].rearrange(
                "p (c m) -> p c m", c=NCC),
            wo_in.t[lyr].rearrange("(c p) m -> p c m", p=128)),
            reads=[wo_in], writes=[arC])
        for i in range(NT):
            ctx.dma("sp", lambda E, i=i: E.dma_start(
                lntile[:, :].rearrange("p (c m) -> p c m", c=NCC),
                yfm_d.t[:, :, i * 128:(i + 1) * 128].rearrange(
                    "c p m -> p c m")),
                reads=[yfm_d], writes=[lntile])
            for half in range(2):
                zp = pp[4 + half]
                for c in range(NCC):
                    ctx.op("pe", lambda E, c=c, half=half, zp=zp: E.matmul(
                        zp[:, :], lntile[:, c * 128:(c + 1) * 128],
                        arC[:, 2 * SEQ + c * DM + half * 512:
                            2 * SEQ + c * DM + (half + 1) * 512],
                        start=(c == 0), stop=(c == NCC - 1)),
                        reads=[lntile, arC], writes=[zp])
            ctx.dma("sp", lambda E, i=i: E.dma_start(
                xtile[:, :], xCur.t[i * 128:(i + 1) * 128, :]),
                reads=[xCur], writes=[xtile])
            for half in range(2):
                zp = pp[4 + half]
                ctx.op("dve", lambda E, half=half, zp=zp: E.tensor_tensor(
                    tmp1[:, half * 512:(half + 1) * 512], zp[:, :],
                    xtile[:, half * 512:(half + 1) * 512], ALU.add),
                    reads=[zp, xtile], writes=[tmp1])
            ctx.dma("sp", lambda E, i=i: E.dma_start(
                xNxt.t[i * 128:(i + 1) * 128, :], tmp1[:, :]),
                reads=[tmp1], writes=[xNxt])

        ctx.dma("pool", lambda E, lyr=lyr: E.dma_start(
            arA[:, 0:NCC * B4_FFP].rearrange("p (c m) -> p c m", c=NCC),
            w1a_in.t[lyr].rearrange("(c p) m -> p c m", p=128)),
            reads=[w1a_in], writes=[arA])
        ctx.dma("pool", lambda E, lyr=lyr: E.dma_start(
            arB[:, 0:NCC * B4_FFP].rearrange("p (c m) -> p c m", c=NCC),
            w1g_in.t[lyr].rearrange("(c p) m -> p c m", p=128)),
            reads=[w1g_in], writes=[arB])
        ctx.dma("pool", lambda E, lyr=lyr: E.dma_start(
            arC[:, :].rearrange("p (c m) -> p c m", c=B4_NFC),
            w2p_in.t[lyr].rearrange("(c p) m -> p c m", p=128)),
            reads=[w2p_in], writes=[arC])
        ctx.dma("pool", lambda E, lyr=lyr: E.dma_start(
            rowbuf[0:1, 0:B4_FFP], b1a_in.t[lyr:lyr + 1, :]),
            reads=[b1a_in], writes=[rowbuf])
        ctx.dma("pool", lambda E, lyr=lyr: E.dma_start(
            rowbuf[32:33, 0:B4_FFP], b1g_in.t[lyr:lyr + 1, :]),
            reads=[b1g_in], writes=[rowbuf])
        ctx.dma("pool", lambda E, lyr=lyr: E.dma_start(
            rowbuf[64:65, 0:DM], b2_in.t[lyr:lyr + 1, :]),
            reads=[b2_in], writes=[rowbuf])
        ctx.op("dve", lambda E: E.memset(rowbuf[0:1, 2816:3328], 1.0),
               reads=[], writes=[rowbuf])
        ctx.op("dve", lambda E: E.memset(rowbuf[32:33, 2816:3328], 1.0),
               reads=[], writes=[rowbuf])
        ctx.op("dve", lambda E: E.memset(rowbuf[64:65, 1024:1152], 1.0),
               reads=[], writes=[rowbuf])
        ctx.dma("sp", lambda E, lyr=lyr: E.dma_start(
            gbv[:, 0:DM],
            lng_in.t[lyr:lyr + 1, :].broadcast_to((128, DM))),
            reads=[lng_in], writes=[tabs])
        ctx.dma("sp", lambda E, lyr=lyr: E.dma_start(
            gbv[:, DM:2 * DM],
            lnb_in.t[lyr:lyr + 1, :].broadcast_to((128, DM))),
            reads=[lnb_in], writes=[tabs])
        for ns in range(8):
            for sub in range(4):
                i = ns * 4 + sub
                shifted_load(xNxt, i)
                emit_ln(ctx, tmp1, np.s_[:, :], xtile, np.s_[:, :], stat,
                        tmp2, np.s_[:, :], DM, eps=epsc)
                ctx.op("dve", lambda E: E.tensor_tensor(
                    tmp1[:, :], tmp1[:, :], gbv[:, 0:DM], ALU.mult),
                    reads=[tmp1, tabs], writes=[tmp1])
                ctx.op("dve", lambda E: E.tensor_tensor(
                    tmp1[:, :], tmp1[:, :], gbv[:, DM:2 * DM], ALU.add),
                    reads=[tmp1, tabs], writes=[tmp1])
                ctx.op("dve", lambda E: E.tensor_copy(lntile[:, :],
                                                      tmp1[:, :]),
                       reads=[tmp1], writes=[lntile])
                for c in range(NCC):
                    ctx.op("pe", lambda E, c=c: E.transpose(
                        pst[:, 0:128], lntile[:, c * 128:(c + 1) * 128],
                        ident[:, :]),
                        reads=[lntile, ident], writes=[pst])
                    ctx.op("dve", lambda E, c=c, sub=sub: E.tensor_copy(
                        xnslab[:, c * 512 + sub * 128:
                               c * 512 + (sub + 1) * 128],
                        pst[:, 0:128]),
                        reads=[pst], writes=[xnslab])
            for fc in range(B4_NFC):
                gp_ = pp[0]
                ctx.op("pe", lambda E, fc=fc, gp_=gp_: E.matmul(
                    gp_[:, :], rowbuf[32:33, fc * 128:(fc + 1) * 128],
                    rowbuf[32:33, 2816:3328], start=True, stop=False),
                    reads=[rowbuf], writes=[gp_])
                for c in range(NCC):
                    ctx.op("pe", lambda E, fc=fc, c=c, gp_=gp_: E.matmul(
                        gp_[:, :],
                        arB[:, c * B4_FFP + fc * 128:
                            c * B4_FFP + (fc + 1) * 128],
                        xnslab[:, c * 512:(c + 1) * 512],
                        start=False, stop=(c == NCC - 1)),
                        reads=[arB, xnslab], writes=[gp_])
                ctx.op("act", lambda E, gp_=gp_: E.activation(
                    gg[:, :], gp_[:, :], AF.Gelu, bias=0.0, scale=1.0),
                    reads=[gp_], writes=[gg])
                ap_ = pp[1]
                ctx.op("pe", lambda E, fc=fc, ap_=ap_: E.matmul(
                    ap_[:, :], rowbuf[0:1, fc * 128:(fc + 1) * 128],
                    rowbuf[0:1, 2816:3328], start=True, stop=False),
                    reads=[rowbuf], writes=[ap_])
                for c in range(NCC):
                    ctx.op("pe", lambda E, fc=fc, c=c, ap_=ap_: E.matmul(
                        ap_[:, :],
                        arA[:, c * B4_FFP + fc * 128:
                            c * B4_FFP + (fc + 1) * 128],
                        xnslab[:, c * 512:(c + 1) * 512],
                        start=False, stop=(c == NCC - 1)),
                        reads=[arA, xnslab], writes=[ap_])
                ctx.op("dve", lambda E, fc=fc, ap_=ap_: E.tensor_tensor(
                    arB[:, NCC * B4_FFP + fc * 512:
                        NCC * B4_FFP + (fc + 1) * 512],
                    ap_[:, :], gg[:, :], ALU.mult),
                    reads=[ap_, gg], writes=[arB])
            for nsub in range(4):
                i = ns * 4 + nsub
                ctx.dma("sp", lambda E, i=i: E.dma_start(
                    xtile[:, :], xNxt.t[i * 128:(i + 1) * 128, :]),
                    reads=[xNxt], writes=[xtile])
                for half in range(2):
                    yp = pp[2 + half]
                    ctx.op("pe", lambda E, half=half, yp=yp: E.matmul(
                        yp[:, :], rowbuf[64:65, 1024:1152],
                        rowbuf[64:65, half * 512:(half + 1) * 512],
                        start=True, stop=False),
                        reads=[rowbuf], writes=[yp])
                    for fc in range(B4_NFC):
                        ctx.op("pe", lambda E, fc=fc, nsub=nsub, half=half,
                               yp=yp: E.matmul(
                            yp[:, :],
                            arB[:, NCC * B4_FFP + fc * 512 + nsub * 128:
                                NCC * B4_FFP + fc * 512 + (nsub + 1) * 128],
                            arC[:, fc * DM + half * 512:
                                fc * DM + (half + 1) * 512],
                            start=False, stop=(fc == B4_NFC - 1)),
                            reads=[arB, arC], writes=[yp])
                    ctx.op("dve", lambda E, half=half, yp=yp:
                           E.tensor_tensor(
                               tmp1[:, half * 512:(half + 1) * 512],
                               yp[:, :],
                               xtile[:, half * 512:(half + 1) * 512],
                               ALU.add),
                           reads=[yp, xtile], writes=[tmp1])
                ctx.dma("sp", lambda E, i=i: E.dma_start(
                    xCur.t[i * 128:(i + 1) * 128, :], tmp1[:, :]),
                    reads=[tmp1], writes=[xCur])

    for i in range(NT):
        ctx.dma("sp", lambda E, i=i: E.dma_start(
            tmp2[:, :], xA_d.t[i * 128:(i + 1) * 128, :]),
            reads=[xA_d], writes=[tmp2])
        ctx.dma("sp", lambda E, i=i: E.dma_start(
            xtile[:, :], xin.t[i * 128:(i + 1) * 128, :]),
            reads=[xin], writes=[xtile])
        emit_ln(ctx, tmp1, np.s_[:, :], tmp2, np.s_[:, :], stat,
                lntile, np.s_[:, :], DM, res=xtile, res_sl=np.s_[:, :],
                eps=epsc)
        ctx.dma("sp", lambda E, i=i: E.dma_start(
            xout.t[i * 128:(i + 1) * 128, :], tmp1[:, :]),
            reads=[tmp1], writes=[xout])




def host_tabs_b4():
    DIM_HEAD, WINDOW, n = 64, 256, SEQ
    QKS = float(np.sqrt(8.0))
    half = np.arange(0, DIM_HEAD, 2, dtype=np.float64)
    inv_freq = 1.0 / 10000.0 ** (half / DIM_HEAD)
    t = np.arange(n, dtype=np.float64)
    freqs = t[:, None] * inv_freq[None, :]
    freqs = np.concatenate([freqs, freqs], -1)
    base_scale = (half + 0.4 * DIM_HEAD) / (1.4 * DIM_HEAD)
    power = (t - n // 2) / (WINDOW // 2)
    scale = base_scale[None, :] ** power[:, None]
    scale = np.concatenate([scale, scale], -1)
    cos, sin = np.cos(freqs), np.sin(freqs)
    sgn = np.empty((n, 64))
    sgn[:, 0:32] = -sin[:, 0:32]
    sgn[:, 32:64] = sin[:, 32:64]
    inv = 1.0 / scale
    tab = np.concatenate([cos * scale * QKS, sgn * scale * QKS,
                          cos * inv * QKS, sgn * inv * QKS],
                         axis=1).astype(np.float32)
    return tab


def host_masks_b4():
    m = np.zeros((128, 4 * 256), dtype=np.float32)
    pc = np.arange(128)[:, None]
    pf = np.arange(256)[None, :]
    m[:, 0 * 256:1 * 256] = ((pf < 128) & (pc >= pf)).astype(np.float32)
    m[:, 1 * 256:2 * 256] = ((pf < 128) | (pc >= pf - 128)).astype(
        np.float32)
    m[:, 2 * 256:3 * 256] = ((pf >= 128) | (pc <= pf)).astype(np.float32)
    m[:, 3 * 256:4 * 256] = ((pf >= 128) & (pc <= pf - 128)).astype(
        np.float32)
    return m


def emit_b1(ctx, com, xin, xout):
    nc = ctx.nc
    wqT = ctx.din("wqT", [DM, DM])
    wkT = ctx.din("wkT", [DM, DM])
    wvT = ctx.din("wvT", [DM, DM])
    wlT = ctx.din("wlT", [DM, 32])
    wbT = ctx.din("wbT", [DM, 32])
    woT = ctx.din("woT", [DM, DM])
    winT_h = ctx.din("winT_h", [128, 512])
    woutT_h = ctx.din("woutT_h", [128, 512])
    woutD_h = ctx.din("woutD_h", [128, 512])
    bsel_in = ctx.din("bsel_in", [32, 16 * 128])
    cstb = ctx.din("cstb", [128, 512])

    o_d = ctx.dram("b1o_d", [NCC, 128, SEQ], BF16)

    xtile, lntile = com.xtile, com.lntile
    tmp1, tmp2, stat, epsc = com.tmp1, com.tmp2, com.stat, com.epsc
    ident = com.ident
    pst = com.pst
    z0, z1 = com.pp[0], com.pp[1]
    ps_a, ps_o, ps_s, ps_sb = com.pp[2], com.pp[3], com.pp[4], com.pp[5]

    hT = ctx.vsb("hT", [128, NCC * SEQ], BF16)
    qf = ctx.vsb("qf", [128, 8 * 1024], BF16)
    kf = ctx.vsb("kf", [128, 8 * 1024], BF16)
    vf = ctx.vsb("vf", [128, 8 * 1024], BF16)
    wpa = ctx.vsb("wpa", [128, NCC * DM], BF16)
    winT = ctx.vsb("winT", [128, 512], F32)
    woutT = ctx.vsb("woutT", [128, 512], F32)
    woutD = ctx.vsb("woutD", [128, 512], F32)
    winTb = ctx.vsb("winTb", [128, 512], BF16)
    woutTb = ctx.vsb("woutTb", [128, 512], BF16)
    woutDb = ctx.vsb("woutDb", [128, 512], BF16)
    lwt = ctx.vsb("lwt", [128, 2048], F32)
    lrc = ctx.vsb("lrc", [32, 128], F32)
    lrcb = ctx.vsb("lrcb", [32, 128], BF16)
    bselb = ctx.vsb("bselb", [32, 2048], BF16)
    wlb = ctx.vsb("wlb", [128, 512], BF16)
    e_all = ctx.vsb("e_all", [128, 1024], F32)
    p_all = ctx.vsb("p_all", [128, 1024], BF16)
    t_all = ctx.vsb("t_all", [128, 512], BF16)
    u_all = ctx.vsb("u_all", [128, 512], F32)
    gz = ctx.vsb("gz", [128, 512], BF16)
    ktok = ctx.vsb("ktok", [64, 1024], BF16)
    vtok = ctx.vsb("vtok", [64, 1024], BF16)
    ptok = ctx.vsb("ptok", [64, 1024], BF16)
    gztok = ctx.vsb("gztok", [64, 1024], BF16)
    ostage = ctx.vsb("ostage", [128, 512], BF16)
    updt = ctx.vsb("updt", [128, 512], F32)
    sseg = ctx.vsb("sseg", [2, 512], F32)
    e2b = ctx.vsb("e2b", [128, 2], BF16)
    e2tf = ctx.vsb("e2tf", [2, 128], F32)

    ctx.dma("pool", lambda E: E.dma_start(ident[:, :], cstb[:, 0:128]),
            reads=[cstb], writes=[ident])
    ctx.dma("pool", lambda E: E.dma_start(e2b[:, :], cstb[:, 128:130]),
            reads=[cstb], writes=[e2b])
    ctx.dma("sp", lambda E: E.dma_start(e2tf[:, :], cstb[0:2, 130:258]),
            reads=[cstb], writes=[e2tf])
    ctx.dma("pool", lambda E: E.dma_start(bselb[:, :], bsel_in.t),
            reads=[bsel_in], writes=[bselb])
    ctx.dma("pool", lambda E: E.dma_start(
        wlb[:, 0:256].rearrange("p (c m) -> p c m", c=NCC),
        wlT.t.rearrange("(c p) m -> p c m", p=128)),
        reads=[wlT], writes=[wlb])
    ctx.dma("pool", lambda E: E.dma_start(
        wlb[:, 256:512].rearrange("p (c m) -> p c m", c=NCC),
        wbT.t.rearrange("(c p) m -> p c m", p=128)),
        reads=[wbT], writes=[wlb])
    ctx.dma("sp", lambda E: E.dma_start(winT[:, :], winT_h.t),
            reads=[winT_h], writes=[winT])
    ctx.dma("sp", lambda E: E.dma_start(woutT[:, :], woutT_h.t),
            reads=[woutT_h], writes=[woutT])
    ctx.dma("sp", lambda E: E.dma_start(woutD[:, :], woutD_h.t),
            reads=[woutD_h], writes=[woutD])
    ctx.op("dve", lambda E: E.memset(epsc[:, 0:1], 1e-5),
           reads=[], writes=[epsc])

    for i in range(NT):
        ctx.dma("sp", lambda E, i=i: E.dma_start(
            xtile[:, :], xin.t[i * 128:(i + 1) * 128, :]),
            reads=[xin], writes=[xtile])
        emit_ln(ctx, tmp1, np.s_[:, :], xtile, np.s_[:, :], stat, tmp2,
                np.s_[:, :], DM, eps=epsc)
        ctx.op("dve", lambda E: E.tensor_copy(lntile[:, :], tmp1[:, :]),
               reads=[tmp1], writes=[lntile])
        for c in range(NCC):
            ctx.op("pe", lambda E, c=c: E.transpose(
                pst[:, 0:128], lntile[:, c * 128:(c + 1) * 128], ident[:, :]),
                reads=[lntile, ident], writes=[pst])
            ctx.op("dve", lambda E, c=c, i=i: E.tensor_copy(
                hT[:, c * SEQ + i * 128: c * SEQ + (i + 1) * 128],
                pst[:, 0:128]),
                reads=[pst], writes=[hT])

    for t2 in range(2):
        for ns in range(8):
            for c in range(NCC):
                ctx.op("pe", lambda E, t2=t2, ns=ns, c=c: E.matmul(
                    ps_a[0:32, :], wlb[:, t2 * 256 + c * 32:
                                       t2 * 256 + (c + 1) * 32],
                    hT[:, c * SEQ + ns * 512:c * SEQ + (ns + 1) * 512],
                    start=(c == 0), stop=(c == NCC - 1)),
                    reads=[wlb, hT], writes=[ps_a])
            ctx.op("act", lambda E: E.activation(
                tmp1[0:32, 0:512], ps_a[0:32, :], AF.Sigmoid,
                bias=0.0, scale=1.0),
                reads=[ps_a], writes=[tmp1])
            ctx.op("dve", lambda E, t2=t2, ns=ns: E.reduce_sum(
                lrc[0:32, t2 * 64 + ns * 8:t2 * 64 + (ns + 1) * 8],
                tmp1[0:32, 0:512].rearrange("p (t m) -> p t m", t=8),
                axis=AX.X),
                reads=[tmp1], writes=[lrc])
    ctx.op("dve", lambda E: E.tensor_scalar_mul(
        lrc[0:32, 0:64], lrc[0:32, 0:64], 0.001 / 64.0),
        reads=[lrc], writes=[lrc])
    ctx.op("dve", lambda E: E.tensor_scalar_mul(
        lrc[0:32, 64:128], lrc[0:32, 64:128], 0.9 / 64.0),
        reads=[lrc], writes=[lrc])
    ctx.op("dve", lambda E: E.tensor_copy(lrcb[0:32, :], lrc[0:32, :]),
           reads=[lrc], writes=[lrcb])
    for q4 in range(4):
        io = q4 % 2
        srcb = 0 if q4 < 2 else 64
        for j in range(8):
            ctx.op("pe", lambda E, io=io, j=j, srcb=srcb: E.matmul(
                ps_o[:, 0:64], bselb[0:32, (io * 8 + j) * 128:
                                     (io * 8 + j + 1) * 128],
                lrcb[0:32, srcb:srcb + 64], start=True, stop=True),
                reads=[bselb, lrcb], writes=[ps_o])
            ctx.op("dve", lambda E, q4=q4, j=j: E.tensor_copy(
                lwt[:, q4 * 512 + j * 64:q4 * 512 + (j + 1) * 64],
                ps_o[:, 0:64]),
                reads=[ps_o], writes=[lwt])

    for pss in range(4):
        for t3, (fmb, wsrc) in enumerate(((qf, wqT), (kf, wkT), (vf, wvT))):
            ctx.dma("pool", lambda E, wsrc=wsrc: E.dma_start(
                wpa[:, :].rearrange("p (c m) -> p c m", c=NCC),
                wsrc.t.rearrange("(c p) m -> p c m", p=128)),
                reads=[wsrc], writes=[wpa])
            for dchunk in range(NCC):
                for ns2 in range(2):
                    pb = z0 if ns2 == 0 else z1
                    for c in range(NCC):
                        ctx.op("pe", lambda E, dchunk=dchunk, ns2=ns2, c=c,
                               pss=pss, pb=pb: E.matmul(
                            pb[:, :],
                            wpa[:, c * DM + dchunk * 128:
                                c * DM + (dchunk + 1) * 128],
                            hT[:, c * SEQ + pss * 1024 + ns2 * 512:
                               c * SEQ + pss * 1024 + (ns2 + 1) * 512],
                            start=(c == 0), stop=(c == NCC - 1)),
                            reads=[wpa, hT], writes=[pb])
                    if t3 < 2:
                        ctx.op("act", lambda E, dchunk=dchunk, ns2=ns2,
                               fmb=fmb, pb=pb: E.activation(
                            fmb[:, dchunk * 1024 + ns2 * 512:
                                dchunk * 1024 + (ns2 + 1) * 512],
                            pb[:, :], AF.Silu, bias=0.0, scale=1.0),
                            reads=[pb], writes=[fmb])
                    else:
                        ctx.op("dve", lambda E, dchunk=dchunk, ns2=ns2,
                               fmb=fmb, pb=pb: E.tensor_copy(
                            fmb[:, dchunk * 1024 + ns2 * 512:
                                dchunk * 1024 + (ns2 + 1) * 512],
                            pb[:, :]),
                            reads=[pb], writes=[fmb])
        for il in range(16):
            ig = pss * 16 + il
            cb = il * 64
            for st_, stb_ in ((winT, winTb), (woutT, woutTb),
                              (woutD, woutDb)):
                ctx.op("dve", lambda E, st_=st_, stb_=stb_: E.tensor_copy(
                    stb_[:, :], st_[:, :]),
                    reads=[st_], writes=[stb_])
            for j in range(8):
                zp = z0 if j < 4 else z1
                zofs = (j % 4) * 128
                for hh in range(2):
                    hsl = slice(0, 64) if hh == 0 else slice(64, 128)
                    ctx.op("pe", lambda E, j=j, hh=hh, hsl=hsl, zp=zp,
                           zofs=zofs, cb=cb: E.matmul(
                        zp[hsl, zofs:zofs + 64],
                        winTb[hsl, j * 64:(j + 1) * 64],
                        qf[hsl, j * 1024 + cb:j * 1024 + cb + 64],
                        start=True, stop=True),
                        reads=[winTb, qf], writes=[zp])
                    ctx.op("pe", lambda E, j=j, hh=hh, hsl=hsl, zp=zp,
                           zofs=zofs, cb=cb: E.matmul(
                        zp[hsl, zofs + 64:zofs + 128],
                        winTb[hsl, j * 64:(j + 1) * 64],
                        kf[hsl, j * 1024 + cb:j * 1024 + cb + 64],
                        start=True, stop=True),
                        reads=[winTb, kf], writes=[zp])
            ctx.op("act", lambda E: E.activation(
                e_all[:, 0:512], z0[:, :], AF.Exp, bias=0.0, scale=1.0),
                reads=[z0], writes=[e_all])
            ctx.op("act", lambda E: E.activation(
                e_all[:, 512:1024], z1[:, :], AF.Exp, bias=0.0, scale=1.0),
                reads=[z1], writes=[e_all])
            ctx.op("dve", lambda E: E.reduce_sum(
                stat[:, 0:16],
                e_all[:, :].rearrange("p (t m) -> p t m", t=16),
                axis=AX.X),
                reads=[e_all], writes=[stat])
            ctx.op("dve", lambda E: E.reciprocal(stat[:, 0:16],
                                                 stat[:, 0:16]),
                   reads=[stat], writes=[stat])
            ctx.op("dve", lambda E: E.tensor_tensor(
                p_all[:, :].rearrange("p (t m) -> p t m", t=16),
                e_all[:, :].rearrange("p (t m) -> p t m", t=16),
                stat[:, 0:16].unsqueeze(2).broadcast_to((128, 16, 64)),
                ALU.mult),
                reads=[e_all, stat], writes=[p_all])
            for j in range(8):
                for hh in range(2):
                    hsl = slice(0, 64) if hh == 0 else slice(64, 128)
                    ctx.op("pe", lambda E, j=j, hsl=hsl, cb=cb: E.matmul(
                        ps_a[hsl, j * 64:(j + 1) * 64],
                        woutTb[hsl, j * 64:(j + 1) * 64],
                        vf[hsl, j * 1024 + cb:j * 1024 + cb + 64],
                        start=True, stop=True),
                        reads=[woutTb, vf], writes=[ps_a])
                    ctx.op("pe", lambda E, j=j, hsl=hsl: E.matmul(
                        ps_o[hsl, j * 64:(j + 1) * 64],
                        woutDb[hsl, j * 64:(j + 1) * 64],
                        p_all[hsl, j * 128:j * 128 + 64],
                        start=True, stop=True),
                        reads=[woutDb, p_all], writes=[ps_o])
            pk_ap = p_all[:, :].rearrange("p (j t n) -> p j t n",
                                          t=2, n=64)[:, :, 1, :]
            ctx.op("dve", lambda E, pk_ap=pk_ap: E.tensor_tensor(
                t_all[:, :].rearrange("p (j n) -> p j n", n=64),
                pk_ap,
                ps_a[:, :].rearrange("p (j n) -> p j n", n=64),
                ALU.mult),
                reads=[p_all, ps_a], writes=[t_all])
            ctx.op("pe", lambda E: E.matmul(
                ps_s[0:2, :], e2b[:, :], t_all[:, :], start=True, stop=True),
                reads=[e2b, t_all], writes=[ps_s])
            ctx.op("dve", lambda E: E.tensor_copy(sseg[:, :], ps_s[0:2, :]),
                   reads=[ps_s], writes=[sseg])
            ctx.op("pe", lambda E: E.matmul(
                ps_sb[:, :], e2tf[:, :], sseg[:, :], start=True, stop=True),
                reads=[e2tf, sseg], writes=[ps_sb])
            ctx.op("dve", lambda E, pk_ap=pk_ap: E.tensor_tensor(
                u_all[:, :].rearrange("p (j n) -> p j n", n=64),
                pk_ap,
                ps_sb[:, :].rearrange("p (j n) -> p j n", n=64),
                ALU.mult),
                reads=[p_all, ps_sb], writes=[u_all])
            ctx.op("dve", lambda E: E.tensor_sub(
                gz[:, :], u_all[:, :], t_all[:, :]),
                reads=[u_all, t_all], writes=[gz])
            for (srcb, dstb, ofs_fn) in (
                    (kf, ktok, lambda j: j * 1024 + cb),
                    (vf, vtok, lambda j: j * 1024 + cb),
                    (p_all, ptok, lambda j: j * 128 + 64),
                    (gz, gztok, lambda j: j * 64)):
                for j in range(8):
                    ctx.op("pe", lambda E, srcb=srcb, j=j, ofs=ofs_fn(j):
                           E.transpose(pst[0:64, 0:128],
                                       srcb[:, ofs:ofs + 64], ident[:, :]),
                           reads=[srcb, ident], writes=[pst])
                    ctx.op("dve", lambda E, dstb=dstb, j=j: E.tensor_copy(
                        dstb[0:64, j * 128:(j + 1) * 128], pst[0:64, 0:128]),
                        reads=[pst], writes=[dstb])
            for j in range(8):
                for hh in range(2):
                    co = j * 128 + hh * 64
                    go = j * 64
                    rsl = slice(0, 64) if hh == 0 else slice(64, 128)
                    ctx.op("pe", lambda E, co=co, go=go, rsl=rsl: E.matmul(
                        z0[rsl, go:go + 64], ktok[0:64, co:co + 64],
                        gztok[0:64, co:co + 64], start=True, stop=True),
                        reads=[ktok, gztok], writes=[z0])
                    ctx.op("pe", lambda E, co=co, go=go, rsl=rsl: E.matmul(
                        z1[rsl, go:go + 64], vtok[0:64, co:co + 64],
                        ptok[0:64, co:co + 64], start=True, stop=True),
                        reads=[vtok, ptok], writes=[z1])
                    ctx.op("pe", lambda E, co=co, go=go, rsl=rsl: E.matmul(
                        ps_a[rsl, go:go + 64], ptok[0:64, co:co + 64],
                        vtok[0:64, co:co + 64], start=True, stop=True),
                        reads=[ptok, vtok], writes=[ps_a])
            for (st_, gps, qlr, qwd, op1) in (
                    (winT, z0, 0, 2, ALU.subtract),
                    (woutT, z1, 1, 3, ALU.add),
                    (woutD, ps_a, 1, 3, ALU.add)):
                lr_ap = lwt[:, qlr * 512:(qlr + 1) * 512].rearrange(
                    "p (j n) -> p j n", n=64)[:, :, ig:ig + 1].broadcast_to(
                    (128, 8, 64))
                wd_ap = lwt[:, qwd * 512:(qwd + 1) * 512].rearrange(
                    "p (j n) -> p j n", n=64)[:, :, ig:ig + 1].broadcast_to(
                    (128, 8, 64))
                ctx.op("dve", lambda E, gps=gps, lr_ap=lr_ap:
                       E.tensor_tensor(
                           updt[:, :].rearrange("p (j n) -> p j n", n=64),
                           gps[:, :].rearrange("p (j n) -> p j n", n=64),
                           lr_ap, ALU.mult),
                       reads=[gps, lwt], writes=[updt])
                ctx.op("dve", lambda E, st_=st_, wd_ap=wd_ap:
                       E.tensor_tensor(
                           st_[:, :].rearrange("p (j n) -> p j n", n=64),
                           st_[:, :].rearrange("p (j n) -> p j n", n=64),
                           wd_ap, ALU.mult),
                       reads=[st_, lwt], writes=[st_])
                ctx.op("dve", lambda E, st_=st_, op1=op1: E.tensor_tensor(
                    st_[:, :], st_[:, :], updt[:, :], op1),
                    reads=[st_, updt], writes=[st_])
            ctx.op("dve", lambda E: E.tensor_copy(ostage[:, :], ps_o[:, :]),
                   reads=[ps_o], writes=[ostage])
            ctx.dma("sp", lambda E, ig=ig: E.dma_start(
                o_d.t[:, :, ig * 64:(ig + 1) * 64].rearrange(
                    "c p m -> p c m"),
                ostage[:, :].rearrange("p (c m) -> p c m", c=NCC)),
                reads=[ostage], writes=[o_d])

    ctx.dma("pool", lambda E: E.dma_start(
        wpa[:, :].rearrange("p (c m) -> p c m", c=NCC),
        woT.t.rearrange("(c p) m -> p c m", p=128)),
        reads=[woT], writes=[wpa])
    for i in range(NT):
        ctx.dma("sp", lambda E, i=i: E.dma_start(
            lntile[:, :].rearrange("p (c m) -> p c m", c=NCC),
            o_d.t[:, :, i * 128:(i + 1) * 128].rearrange("c p m -> p c m")),
            reads=[o_d], writes=[lntile])
        for half in range(2):
            zp = z0 if half == 0 else z1
            for c in range(NCC):
                ctx.op("pe", lambda E, c=c, half=half, zp=zp: E.matmul(
                    zp[:, :], lntile[:, c * 128:(c + 1) * 128],
                    wpa[:, c * DM + half * 512:c * DM + (half + 1) * 512],
                    start=(c == 0), stop=(c == NCC - 1)),
                    reads=[lntile, wpa], writes=[zp])
        ctx.dma("sp", lambda E, i=i: E.dma_start(
            xtile[:, :], xin.t[i * 128:(i + 1) * 128, :]),
            reads=[xin], writes=[xtile])
        for half in range(2):
            zp = z0 if half == 0 else z1
            ctx.op("dve", lambda E, half=half, zp=zp: E.tensor_copy(
                tmp2[:, half * 512:(half + 1) * 512], zp[:, :]),
                reads=[zp], writes=[tmp2])
        emit_ln(ctx, tmp1, np.s_[:, 0:DM], tmp2, np.s_[:, 0:DM], stat,
                lntile, np.s_[:, 0:DM], DM, res=xtile, res_sl=np.s_[:, :],
                eps=epsc)
        ctx.dma("sp", lambda E, i=i: E.dma_start(
            xout.t[i * 128:(i + 1) * 128, :], tmp1[:, 0:DM]),
            reads=[tmp1], writes=[xout])




def host_consts_b1():
    c = np.zeros((128, 512), dtype=np.float32)
    c[:, 0:128] = np.eye(128, dtype=np.float32)
    c[0:64, 128] = 1.0
    c[64:128, 129] = 1.0
    c[0, 130:194] = 1.0
    c[1, 194:258] = 1.0
    return c


def host_bsel_b1():
    b = np.zeros((32, 16 * 128), dtype=np.float32)
    for io in range(2):
        for j in range(8):
            for p in range(128):
                h = 2 * j + (1 if p >= 64 else 0)
                b[io * 16 + h, (io * 8 + j) * 128 + p] = 1.0
    return b


def host_winit_b1(W_in_init, W_out_init):
    wi = W_in_init[0]
    wo = W_out_init[0]
    winT = np.zeros((128, 512), dtype=np.float32)
    woutT = np.zeros((128, 512), dtype=np.float32)
    woutD = np.zeros((128, 512), dtype=np.float32)
    for p in range(128):
        for j in range(8):
            h = 2 * j + (1 if p >= 64 else 0)
            winT[p, j * 64:(j + 1) * 64] = wi[:, h, p % 64]
            woutT[p, j * 64:(j + 1) * 64] = wo[:, h, p % 64]
            woutD[p, j * 64:(j + 1) * 64] = wo[p % 64, h, :]
    return winT, woutT, woutD


ARENA_COLS = 97280


def com_alloc(ctx):
    from types import SimpleNamespace
    com = SimpleNamespace()
    com.xtile = ctx.sb("xtile", [128, DM], F32)
    com.lntile = ctx.sb("lntile", [128, DM], BF16)
    com.stage = ctx.sb("stage", [128, DM], BF16)
    com.tmp1 = ctx.sb("tmp1", [128, DM], F32)
    com.tmp2 = ctx.sb("tmp2", [128, DM], F32)
    com.stat = ctx.sb("stat", [128, 16], F32)
    com.epsc = ctx.sb("epsc", [128, 2], F32)
    com.ident = ctx.sb("ident", [128, 128], BF16)
    com.onesb = ctx.sb("onesb", [1, 512], BF16)
    com.onesf = ctx.sb("onesf", [1, 128], F32)
    com.pp = [ctx.ps(f"pp{i}", [128, 512], F32) for i in range(6)]
    com.pst = ctx.ps("pst", [128, 1024], BF16)
    return com


def build_solo(emitter, **kw):
    nc = bass.Bass("TRN2", target_bir_lowering=False)
    ctx = Ctx(nc)
    ctx.arena_init(ARENA_COLS)
    com = com_alloc(ctx)
    xin = ctx.din("xin", [SEQ, DM])
    xout = ctx.dout("xout", [SEQ, DM])
    emitter(ctx, com, xin, xout, **kw)
    ctx.barrier_out()
    ctx.finalize()
    return nc


def build_fused(dbg=False):
    nc = bass.Bass("TRN2", target_bir_lowering=False)
    ctx = Ctx(nc)
    ctx.arena_init(ARENA_COLS)
    com = com_alloc(ctx)
    mk = ctx.dout if dbg else ctx.dram
    x0 = ctx.din("x0", [SEQ, DM])
    x1_d = mk("x1_d", [SEQ, DM], F32)
    x2_d = mk("x2_d", [SEQ, DM], F32)
    x3_d = mk("x3_d", [SEQ, DM], F32)
    xout = ctx.dout("xout", [SEQ, DM])
    emit_b1(ctx, com, x0, x1_d)
    ctx.barrier()
    ctx.arena_reset()
    emit_b2(ctx, com, x1_d, x2_d)
    ctx.barrier()
    ctx.arena_reset()
    emit_b3(ctx, com, x2_d, x3_d)
    ctx.barrier()
    ctx.arena_reset()
    emit_b4(ctx, com, x3_d, xout)
    ctx.barrier_out()
    ctx.finalize()
    return nc


def prep_b1(z):
    cstb = host_consts_b1()
    bsel = host_bsel_b1()
    winT, woutT, woutD = host_winit_b1(z["L0_attn_W_in_init"],
                                       z["L0_attn_W_out_init"])
    t = lambda a: np.ascontiguousarray(np.asarray(a, dtype=np.float32).T)
    return {"wqT": t(z["L0_attn_Wq"]), "wkT": t(z["L0_attn_Wk"]),
            "wvT": t(z["L0_attn_Wv"]), "wlT": t(z["L0_attn_Wlr"]),
            "wbT": t(z["L0_attn_Wbeta"]), "woT": t(z["L0_attn_Wo"]),
            "winT_h": winT, "woutT_h": woutT, "woutD_h": woutD,
            "bsel_in": bsel, "cstb": cstb}


def prep_b2(z):
    cos, ss = host_rope_b2()
    cst2 = host_consts_b2()
    f = lambda a: np.ascontiguousarray(np.asarray(a, dtype=np.float32))
    brow = np.concatenate([f(z["L0_ffn_bq"]), f(z["L0_ffn_bk"]),
                           f(z["L0_ffn_bv"]),
                           f(z["L0_ffn_bo"])]).reshape(1, 4 * DM)
    grow = np.concatenate([f(z["L0_ffn_g1"]), f(z["L0_ffn_be1"]),
                           f(z["L0_ffn_g2"]),
                           f(z["L0_ffn_be2"])]).reshape(1, 4 * DM)
    return {"wq": f(z["L0_ffn_Wq"]), "wk": f(z["L0_ffn_Wk"]),
            "wv": f(z["L0_ffn_Wv"]), "wo": f(z["L0_ffn_Wo"]),
            "brow_in": brow, "wg_in": f(z["L0_ffn_Wg"]),
            "bg_in": f(z["L0_ffn_bg"]).reshape(1, 8),
            "w1_in": f(z["L0_ffn_W1"]), "b1_in": f(z["L0_ffn_b1"]),
            "w2_in": f(z["L0_ffn_W2"]), "b2_in": f(z["L0_ffn_b2"]),
            "grow_in": grow, "ropec": cos, "ropes": ss, "cst2": cst2}


def prep_b3(z):
    qtw, ktw, vtw, wtw = prep_b3_weights(
        np.asarray(z["L1_attn_Q"], dtype=np.float32),
        np.asarray(z["L1_attn_K"], dtype=np.float32),
        np.asarray(z["L1_attn_V"], dtype=np.float32),
        np.asarray(z["L1_attn_W"], dtype=np.float32))
    return {"qtw": qtw, "ktw": ktw, "vtw": vtw, "wtw": wtw,
            "consts": host_consts_b3()}


def prep_b4(z):
    cst = host_consts_b2()
    tab = host_tabs_b4()
    msk = host_masks_b4()
    f = lambda a: np.ascontiguousarray(np.asarray(a, dtype=np.float32))
    W1 = f(z["L1_ffn_W1"])
    b1 = f(z["L1_ffn_b1"])
    W2 = f(z["L1_ffn_W2"])
    FF = 2730
    w1a = np.zeros((6, DM, B4_FFP), dtype=np.float32)
    w1g = np.zeros((6, DM, B4_FFP), dtype=np.float32)
    w1a[:, :, 0:FF] = W1[:, :, 0:FF]
    w1g[:, :, 0:FF] = W1[:, :, FF:2 * FF]
    b1a = np.zeros((6, B4_FFP), dtype=np.float32)
    b1g = np.zeros((6, B4_FFP), dtype=np.float32)
    b1a[:, 0:FF] = b1[:, 0:FF]
    b1g[:, 0:FF] = b1[:, FF:2 * FF]
    w2p = np.zeros((6, B4_FFP, DM), dtype=np.float32)
    w2p[:, 0:FF, :] = W2
    return {"wqkv_in": f(z["L1_ffn_Wqkv"]),
            "qs_in": f(z["L1_ffn_q_scale"]), "ks_in": f(z["L1_ffn_k_scale"]),
            "wo_in": f(z["L1_ffn_Wo"]),
            "lng_in": f(z["L1_ffn_ln_g"]), "lnb_in": f(z["L1_ffn_ln_b"]),
            "w1a_in": w1a, "w1g_in": w1g, "b1a_in": b1a, "b1g_in": b1g,
            "w2p_in": w2p, "b2f_in": f(z["L1_ffn_b2"]),
            "tab_in": tab, "msk_in": msk, "cst": cst}


def _run_solo(emitter, prep, x, z, **kw):
    nc = build_solo(emitter, **kw)
    base = prep(z)
    in_maps = [dict(base, xin=np.ascontiguousarray(x[b]))
               for b in range(x.shape[0])]
    res = run_bass_kernel_spmd(nc, in_maps,
                               core_ids=list(range(x.shape[0])))
    if kw.get("dbg"):
        return res.results
    return np.stack([r["xout"] for r in res.results], axis=0)


def run_b1(x0, z):
    return _run_solo(emit_b1, prep_b1, x0, z)


def run_b2(x1, z, dbg=False, nexp=8):
    return _run_solo(emit_b2, prep_b2, x1, z, dbg=dbg, nexp=nexp)


def run_b3(x2, Q, K, V, W):
    z = {"L1_attn_Q": Q, "L1_attn_K": K, "L1_attn_V": V, "L1_attn_W": W}
    return _run_solo(emit_b3, prep_b3, x2, z)


def run_b4(x3, z):
    return _run_solo(emit_b4, prep_b4, x3, z)


def kernel(**inputs):
    z = inputs
    x0 = np.ascontiguousarray(np.asarray(z["L0_attn_x"], dtype=np.float32))
    nc = build_fused()
    base = {}
    base.update(prep_b1(z))
    base.update(prep_b2(z))
    base.update(prep_b3(z))
    base.update(prep_b4(z))
    in_maps = [dict(base, x0=np.ascontiguousarray(x0[b]))
               for b in range(8)]
    res = run_bass_kernel_spmd(nc, in_maps, core_ids=list(range(8)))
    return np.stack([r["xout"] for r in res.results],
                    axis=0).astype(np.float32)


# version 06
# speedup vs baseline: 1.0490x; 1.0490x over previous
import numpy as np
import concourse.bass as bass
import concourse.mybir as mybir
from concourse.bass_utils import run_bass_kernel_spmd
from contextlib import ExitStack

F32 = mybir.dt.float32
BF16 = mybir.dt.bfloat16
AF = mybir.ActivationFunctionType
ALU = mybir.AluOpType
AX = mybir.AxisListType

SEQ = 4096
DM = 1024
NT = SEQ // 128
NCC = DM // 128
B4_FFP = 2816
B4_NFC = 22


ENGS = ("pe", "act", "dve", "pool", "sp")
DMA_SEM = {"sp": "spd", "pool": "poold", "act": "actd"}


class Buf:
    __slots__ = ("name", "t", "w", "r")

    def __init__(self, name, t):
        self.name = name
        self.t = t
        self.w = None
        self.r = {}

    def __getitem__(self, key):
        return self.t[key]


class Ctx:
    def __init__(self, nc):
        self.nc = nc
        self.stack = ExitStack()
        self.q = {e: [] for e in ENGS}
        self.cnt = {s: 0 for s in ("pe", "act", "dve", "pool", "sp",
                                   "spd", "poold", "actd")}
        self.seen = {e: {} for e in ENGS}
        self.sems = {}
        for s in self.cnt:
            self.sems[s] = self.stack.enter_context(nc.semaphore("sem_" + s))

    def sb(self, name, shape, dtype):
        t = self.stack.enter_context(self.nc.sbuf_tensor(name, shape, dtype))
        return Buf(name, t)

    def arena_init(self, cols):
        self.ar = self.stack.enter_context(
            self.nc.sbuf_tensor("ARENA", [128, cols], BF16))
        self.acur = 0
        self.amax = 0

    def arena_reset(self):
        self.acur = 0

    def vsb(self, name, shape, dtype):
        p, c = shape
        cols = c if dtype == BF16 else 2 * c
        acols = (cols + 15) // 16 * 16
        off = self.acur
        self.acur += acols
        self.amax = max(self.amax, self.acur)
        assert self.acur <= self.ar.shape[1], f"arena overflow at {name}"
        ap = self.ar[0:p, off:off + cols]
        if dtype != BF16:
            ap = ap.bitcast(dtype)
        return Buf(name, ap)

    def barrier(self):
        for e in ENGS:
            for s, v in self.cnt.items():
                if v:
                    self._wait(e, s, v)

    def ps(self, name, shape, dtype=F32):
        t = self.stack.enter_context(self.nc.psum_tensor(name, shape, dtype))
        return Buf(name, t)

    def dram(self, name, shape, dtype):
        t = self.nc.dram_tensor(name, shape, dtype, kind="Internal")
        return Buf(name, t.ap())

    def din(self, name, shape, dtype=F32):
        t = self.nc.dram_tensor(name, shape, dtype, kind="ExternalInput")
        return Buf(name, t.ap())

    def dout(self, name, shape, dtype=F32):
        t = self.nc.dram_tensor(name, shape, dtype, kind="ExternalOutput")
        return Buf(name, t.ap())

    def _wait(self, eng, sem, val):
        if eng == "pe" and sem == "pe":
            return
        if self.seen[eng].get(sem, 0) >= val:
            return
        self.q[eng].append(
            lambda E, s=sem, v=val: E.wait_ge(self.sems[s], v))
        self.seen[eng][sem] = val

    def _dval(self, sem, val):
        return self.cnt[sem] if sem.endswith("d") else val

    def _deps(self, eng, reads, writes):
        for b in reads:
            if b.w is not None:
                self._wait(eng, b.w[0], self._dval(b.w[0], b.w[1]))
        for b in writes:
            if b.w is not None:
                self._wait(eng, b.w[0], self._dval(b.w[0], b.w[1]))
            for sem, val in b.r.items():
                self._wait(eng, sem, self._dval(sem, val))

    def op(self, eng, fn, reads=(), writes=()):
        self._deps(eng, reads, writes)
        self.cnt[eng] += 1
        v = self.cnt[eng]
        self.q[eng].append(
            lambda E, f=fn, s=eng: f(E).then_inc(self.sems[s], 1))
        tok = (eng, v)
        for b in reads:
            b.r[eng] = v
        for b in writes:
            b.w = tok
            b.r = {}
        return tok

    def dma(self, queue, fn, reads=(), writes=()):
        self._deps(queue, reads, writes)
        dsem = DMA_SEM[queue]
        self.cnt[dsem] += 16
        v = self.cnt[dsem]
        self.q[queue].append(
            lambda E, f=fn, s=dsem: f(E).then_inc(self.sems[s], 16))
        tok = (dsem, v)
        for b in reads:
            b.r[dsem] = v
        for b in writes:
            b.w = tok
            b.r = {}
        return tok

    def barrier_out(self):
        for s in ("spd", "poold", "actd"):
            if self.cnt[s]:
                self.q["sp"].append(
                    lambda E, ss=s, v=self.cnt[s]: E.wait_ge(self.sems[ss], v))

    def finalize(self):
        nc = self.nc
        qs = self.q
        with nc.Block() as blk:
            @blk.tensor
            def _(e):
                for f in qs["pe"]:
                    f(e)

            @blk.scalar
            def _(e):
                for f in qs["act"]:
                    f(e)

            @blk.vector
            def _(e):
                for f in qs["dve"]:
                    f(e)

            @blk.gpsimd
            def _(e):
                for f in qs["pool"]:
                    f(e)

            @blk.sync
            def _(e):
                for f in qs["sp"]:
                    f(e)
        self.stack.close()



def emit_ln(ctx, out, out_sl, x, x_sl, stat, sqbuf, sq_sl, D,
            res=None, res_sl=None, eps=None):
    xap = x[x_sl]
    sqap = sqbuf[sq_sl]
    ctx.op("dve", lambda E: E.reduce_sum(stat[:, 0:1], xap, axis=AX.X),
           reads=[x], writes=[stat])
    ctx.op("act", lambda E: E.activation(sqap, xap, AF.Square,
                                         bias=0.0, scale=1.0),
           reads=[x], writes=[sqbuf])
    ctx.op("dve", lambda E: E.reduce_sum(stat[:, 1:2], sqap, axis=AX.X),
           reads=[sqbuf], writes=[stat])
    ctx.op("dve", lambda E: E.tensor_scalar_mul(stat[:, 2:3], stat[:, 0:1],
                                                1.0 / D),
           reads=[stat], writes=[stat])
    ctx.op("dve", lambda E: E.tensor_tensor(stat[:, 3:4], stat[:, 2:3],
                                            stat[:, 2:3], ALU.mult),
           reads=[stat], writes=[stat])
    ctx.op("dve", lambda E: E.scalar_tensor_tensor(
        stat[:, 4:5], stat[:, 1:2], 1.0 / D, stat[:, 3:4],
        ALU.mult, ALU.subtract),
        reads=[stat], writes=[stat])
    ctx.op("act", lambda E: E.activation(stat[:, 5:6], stat[:, 4:5], AF.Sqrt,
                                         bias=eps[:, 0:1], scale=1.0),
           reads=[stat, eps], writes=[stat])
    ctx.op("dve", lambda E: E.reciprocal(stat[:, 6:7], stat[:, 5:6]),
           reads=[stat], writes=[stat])
    ctx.op("dve", lambda E: E.tensor_scalar_mul(stat[:, 7:8], stat[:, 2:3],
                                                -1.0),
           reads=[stat], writes=[stat])
    oap = out[out_sl]
    ctx.op("dve", lambda E: E.tensor_scalar(oap, xap, stat[:, 7:8],
                                            stat[:, 6:7], ALU.add, ALU.mult),
           reads=[x, stat], writes=[out])
    if res is not None:
        ctx.op("dve", lambda E: E.tensor_tensor(oap, oap, res[res_sl],
                                                ALU.add),
               reads=[out, res], writes=[out])



def emit_b3(ctx, com, xin, xout):
    nc = ctx.nc
    qtw = ctx.din("qtw", [8, DM, 128])
    ktw = ctx.din("ktw", [8, DM, 128])
    vtw = ctx.din("vtw", [8, DM, 128])
    wtw = ctx.din("wtw", [DM, DM])
    consts = ctx.din("consts", [128, 128 + 128 + 4 * 512])

    yfm_d = ctx.dram("b3yfm_d", [NCC, 128, SEQ], BF16)

    xt2 = ctx.vsb("xt2", [128, DM], F32)
    xtile = [com.xtile, xt2]
    lntile, stat, ident = com.lntile, com.stat, com.ident
    epsc, onesc = com.epsc, com.onesf
    pp, pst = com.pp, com.pst

    ht = ctx.vsb("ht", [128, NCC * SEQ], BF16)
    sq = ctx.vsb("sq", [128, DM], F32)
    tri = ctx.vsb("tri", [128, 4 * 512], BF16)
    qw = ctx.vsb("qw", [128, NCC * 128], BF16)
    kw = ctx.vsb("kw", [128, NCC * 128], BF16)
    vw = ctx.vsb("vw", [128, NCC * 128], BF16)
    wt = ctx.vsb("wt", [128, NCC * DM], BF16)
    qt = ctx.vsb("qt", [128, SEQ], BF16)
    kt = ctx.vsb("kt", [128, SEQ], BF16)
    vfm = ctx.vsb("vfm", [128, SEQ], BF16)
    va = ctx.vsb("va", [128, NT * 65], BF16)
    vb = ctx.vsb("vb", [128, NT * 65], BF16)
    pt2 = [ctx.vsb(f"pt{i}", [128, 512], BF16) for i in range(2)]
    recip = ctx.vsb("recip", [1, 512], F32)
    ybc = ctx.vsb("ybc", [128, 512], F32)
    ystage = ctx.vsb("ystage", [128, 512], BF16)
    yfmt = [ctx.vsb(f"yfmt{i}", [128, DM], BF16) for i in range(2)]
    outln = ctx.vsb("outln", [128, DM], F32)

    ctx.dma("pool", lambda E: E.dma_start(ident[:, :], consts[:, 0:128]),
            reads=[consts], writes=[ident])
    ctx.dma("sp", lambda E: E.dma_start(onesc[:, :], consts[0:1, 128:256]),
            reads=[consts], writes=[onesc])
    ctx.dma("pool", lambda E: E.dma_start(tri[:, :],
                                          consts[:, 256:256 + 4 * 512]),
            reads=[consts], writes=[tri])
    ctx.dma("pool", lambda E: E.dma_start(
        wt[:, :].rearrange("p (c m) -> p c m", c=NCC),
        wtw.t.rearrange("(c p) m -> p c m", p=128)),
        reads=[wtw], writes=[wt])

    ctx.op("dve", lambda E: E.memset(epsc[:, :], 1e-5),
           reads=[], writes=[epsc])

    for i in range(NT):
        xt = xtile[i % 2]
        ctx.dma("sp", lambda E, i=i, xt=xt: E.dma_start(
            xt[:, :], xin.t[i * 128:(i + 1) * 128, :]),
            reads=[xin], writes=[xt])
        emit_ln(ctx, lntile, np.s_[:, :], xt, np.s_[:, :], stat, sq,
                np.s_[:, :], DM, eps=epsc)
        for c in range(NCC):
            ctx.op("pe", lambda E, c=c: E.transpose(
                pst[:, 0:128], lntile[:, c * 128:(c + 1) * 128], ident[:, :]),
                reads=[lntile, ident], writes=[pst])
            ctx.op("dve", lambda E, c=c, i=i: E.tensor_copy(
                ht[:, c * SEQ + i * 128: c * SEQ + (i + 1) * 128],
                pst[:, 0:128]),
                reads=[pst], writes=[ht])

    SC_QK = float(np.sqrt(64.0 / 1024.0) / 8.0)
    SC_V = float(np.sqrt(64.0 / 1024.0))
    for pair in range(8):
        for wbuf, src in ((qw, qtw), (kw, ktw), (vw, vtw)):
            ctx.dma("pool", lambda E, wbuf=wbuf, src=src, pair=pair:
                    E.dma_start(
                        wbuf[:, :].rearrange("p (c m) -> p c m", c=NCC),
                        src.t[pair].rearrange("(c p) m -> p c m", p=128)),
                    reads=[src], writes=[wbuf])
        for tens, wbuf, scl in ((qt, qw, SC_QK), (kt, kw, SC_QK),
                                (vfm, vw, SC_V)):
            for ns in range(8):
                pb = pp[ns % 2]
                for c in range(NCC):
                    ctx.op("pe", lambda E, c=c, ns=ns, wbuf=wbuf, pb=pb:
                           E.matmul(pb[:, :],
                                    wbuf[:, c * 128:(c + 1) * 128],
                                    ht[:, c * SEQ + ns * 512:
                                       c * SEQ + (ns + 1) * 512],
                                    start=(c == 0), stop=(c == NCC - 1)),
                           reads=[wbuf, ht], writes=[pb])
                ctx.op("dve", lambda E, ns=ns, tens=tens, scl=scl, pb=pb:
                       E.tensor_scalar_mul(
                           tens[:, ns * 512:(ns + 1) * 512], pb[:, :], scl),
                       reads=[pb], writes=[tens])
        for ktn in range(NT):
            ctx.op("pe", lambda E, ktn=ktn: E.transpose(
                pst[:, 0:128], vfm[:, ktn * 128:(ktn + 1) * 128], ident[:, :]),
                reads=[vfm, ident], writes=[pst])
            ctx.op("dve", lambda E, ktn=ktn: E.tensor_copy(
                va[:, ktn * 65:ktn * 65 + 64], pst[:, 0:64]),
                reads=[pst], writes=[va])
            ctx.op("dve", lambda E, ktn=ktn: E.tensor_copy(
                vb[:, ktn * 65:ktn * 65 + 64], pst[:, 64:128]),
                reads=[pst], writes=[vb])
        ctx.op("dve", lambda E: E.memset(
            va[:, :].rearrange("p (n c) -> p n c", c=65)[:, :, 64:65], 1.0),
            reads=[], writes=[va])
        ctx.op("dve", lambda E: E.memset(
            vb[:, :].rearrange("p (n c) -> p n c", c=65)[:, :, 64:65], 1.0),
            reads=[], writes=[vb])

        for hh in range(2):
            hsl = slice(0, 64) if hh == 0 else slice(64, 128)
            vx = va if hh == 0 else vb
            for qs in range(8):
                outp = pp[2]
                ktmax = 4 * qs + 3
                for ktn in range(ktmax + 1):
                    sb_ = pp[ktn % 2]
                    pt = pt2[ktn % 2]
                    ctx.op("pe", lambda E, ktn=ktn, qs=qs, hsl=hsl, sb_=sb_:
                           E.matmul(sb_[:, :],
                                    kt[hsl, ktn * 128:(ktn + 1) * 128],
                                    qt[hsl, qs * 512:(qs + 1) * 512],
                                    start=True, stop=True),
                           reads=[kt, qt], writes=[sb_])
                    band_j = ktn - 4 * qs
                    if band_j < 0:
                        ctx.op("act", lambda E, sb_=sb_, pt=pt: E.activation(
                            pt[:, :], sb_[:, :], AF.Exp, bias=0.0, scale=1.0),
                            reads=[sb_], writes=[pt])
                    else:
                        ctx.op("act", lambda E, sb_=sb_: E.activation(
                            sq[:, 0:512], sb_[:, :], AF.Exp, bias=0.0,
                            scale=1.0),
                            reads=[sb_], writes=[sq])
                        ctx.op("dve", lambda E, j=band_j, pt=pt:
                               E.tensor_tensor(
                                   pt[:, :], sq[:, 0:512],
                                   tri[:, j * 512:(j + 1) * 512], ALU.mult),
                               reads=[sq, tri], writes=[pt])
                    ctx.op("pe", lambda E, ktn=ktn, vx=vx, outp=outp,
                           pt=pt, st=(ktn == 0), sp_=(ktn == ktmax):
                           E.matmul(outp[0:65, :],
                                    vx[:, ktn * 65:(ktn + 1) * 65],
                                    pt[:, :], start=st, stop=sp_),
                           reads=[vx, pt], writes=[outp])
                ctx.op("dve", lambda E, outp=outp: E.reciprocal(
                    recip[:, :], outp[64:65, :]),
                    reads=[outp], writes=[recip])
                bc = pp[3]
                ctx.op("pe", lambda E, bc=bc: E.matmul(
                    bc[:, :], onesc[:, :], recip[:, :], start=True, stop=True),
                    reads=[onesc, recip], writes=[bc])
                ctx.op("dve", lambda E, bc=bc: E.tensor_copy(
                    ybc[:, :], bc[:, :]), reads=[bc], writes=[ybc])
                ctx.op("dve", lambda E, outp=outp: E.tensor_tensor(
                    ystage[0:64, :], outp[0:64, :], ybc[0:64, :], ALU.mult),
                    reads=[outp, ybc], writes=[ystage])
                ctx.dma("act", lambda E, pair=pair, hh=hh, qs=qs: E.dma_start(
                    yfm_d.t[pair, hh * 64:(hh + 1) * 64,
                            qs * 512:(qs + 1) * 512],
                    ystage[0:64, :]),
                    reads=[ystage], writes=[yfm_d])

    WSC = float(np.sqrt(1024.0 / 64.0) / 16.0)
    for i in range(NT):
        yb = yfmt[i % 2]
        ctx.dma("sp", lambda E, i=i, yb=yb: E.dma_start(
            yb[:, :].rearrange("p (c m) -> p c m", c=NCC),
            yfm_d.t[:, :, i * 128:(i + 1) * 128].rearrange(
                "c p m -> p c m")),
            reads=[yfm_d], writes=[yb])
        for half in range(2):
            zp = pp[4 + half]
            for c in range(NCC):
                ctx.op("pe", lambda E, c=c, half=half, zp=zp, yb=yb:
                       E.matmul(zp[:, :],
                                yb[:, c * 128:(c + 1) * 128],
                                wt[:, c * DM + half * 512:
                                   c * DM + (half + 1) * 512],
                                start=(c == 0), stop=(c == NCC - 1)),
                       reads=[yb, wt], writes=[zp])
        xt = xtile[i % 2]
        ctx.dma("sp", lambda E, i=i, xt=xt: E.dma_start(
            xt[:, :], xin.t[i * 128:(i + 1) * 128, :]),
            reads=[xin], writes=[xt])
        for half in range(2):
            zp = pp[4 + half]
            ctx.op("dve", lambda E, half=half, zp=zp: E.tensor_scalar_mul(
                sq[:, half * 512:(half + 1) * 512], zp[:, :], WSC),
                reads=[zp], writes=[sq])
        emit_ln(ctx, outln, np.s_[:, 0:DM], sq, np.s_[:, 0:DM], stat,
                outln, np.s_[:, 0:DM], DM, res=xt, res_sl=np.s_[:, :],
                eps=epsc)
        ctx.dma("act", lambda E, i=i: E.dma_start(
            xout.t[i * 128:(i + 1) * 128, :], outln[:, 0:DM]),
            reads=[outln], writes=[xout])




def host_consts_b3():
    c = np.zeros((128, 128 + 128 + 4 * 512), dtype=np.float32)
    c[:, 0:128] = np.eye(128, dtype=np.float32)
    c[0, 128:256] = 1.0
    for j in range(4):
        m = np.zeros((128, 512), dtype=np.float32)
        for sub in range(4):
            if sub > j:
                m[:, sub * 128:(sub + 1) * 128] = 1.0
            elif sub == j:
                m[:, sub * 128:(sub + 1) * 128] = np.tril(
                    np.ones((128, 128), dtype=np.float32)).T
        c[:, 256 + j * 512:256 + (j + 1) * 512] = m
    return c


def prep_b3_weights(Q, K, V, W):
    def pairs(q):
        t = np.ascontiguousarray(np.transpose(q, (2, 1, 0)))
        t = t.reshape(8, 2, DM, 64).transpose(0, 2, 1, 3).reshape(8, DM, 128)
        return np.ascontiguousarray(t)
    wtw = np.ascontiguousarray(np.transpose(W, (2, 1, 0)).reshape(DM, DM))
    return pairs(Q), pairs(K), pairs(V), wtw


def emit_b2(ctx, com, xin, xout, dbg=False, nexp=8):
    nc = ctx.nc
    wq = ctx.din("wq", [DM, DM])
    wk = ctx.din("wk", [DM, DM])
    wv = ctx.din("wv", [DM, DM])
    wo = ctx.din("wo", [DM, DM])
    brow_in = ctx.din("brow_in", [1, 4 * DM])
    wg_in = ctx.din("wg_in", [DM, 8])
    bg_in = ctx.din("bg_in", [1, 8])
    w1_in = ctx.din("w1_in", [8, DM, 4096])
    b1_in = ctx.din("b1_in", [8, 4096])
    w2_in = ctx.din("w2_in", [8, 4096, DM])
    b2_in = ctx.din("b2_in", [8, DM])
    grow_in = ctx.din("grow_in", [1, 4 * DM])
    ropec = ctx.din("ropec", [SEQ, 64])
    ropes = ctx.din("ropes", [SEQ, 64])
    cst2 = ctx.din("cst2", [128, 768])

    mk = ctx.dout if dbg else ctx.dram
    h_d = mk("b2h_d", [SEQ, DM], F32)
    qT_d = mk("b2qT_d", [NCC, 128, SEQ], BF16)
    kT_d = mk("b2kT_d", [NCC, 128, SEQ], BF16)
    yfm_d = mk("b2yfm_d", [NCC, 128, SEQ], BF16)
    x1p_d = mk("b2x1p_d", [SEQ, DM], F32)
    x1pT_d = mk("b2x1pT_d", [NCC, 128, SEQ], BF16)
    moe_d = mk("b2moe_d", [SEQ, DM], F32)

    xtile, lntile, stage = com.xtile, com.lntile, com.stage
    tmp1, tmp2, stat, epsc = com.tmp1, com.tmp2, com.stat, com.epsc
    ident, onesb, onesf = com.ident, com.onesb, com.onesf
    pp, pst = com.pp, com.pst

    big0 = ctx.vsb("big0", [128, NCC * SEQ], BF16)
    big1 = ctx.vsb("big1", [128, NT * 1040], BF16)
    ar0 = ctx.vsb("ar0", [128, 2 * NCC * DM], BF16)
    argb = ctx.vsb("argb", [128, 2 * NT * 64], BF16)
    rowbuf = ctx.vsb("rowbuf", [65, 4608], BF16)
    wgw = ctx.vsb("wgw", [128, NCC * 8], BF16)
    pt2 = [ctx.vsb(f"pt{i}", [128, 512], BF16) for i in range(2)]
    ystage = ctx.vsb("ystage", [128, 512], BF16)
    comb = ctx.vsb("comb", [128, NT * 8], F32)
    mtile = ctx.vsb("mtile", [128, DM], F32)

    qhalf = big0
    gbufv = argb.t[:, :].bitcast(F32)

    def wp(i3, sl0, sl1):
        return ar0[:, i3 * NCC * DM + sl0:i3 * NCC * DM + sl1]

    ctx.dma("pool", lambda E: E.dma_start(ident[:, :], cst2[:, 0:128]),
            reads=[cst2], writes=[ident])
    ctx.dma("pool", lambda E: E.dma_start(onesb[:, :], cst2[0:1, 128:640]),
            reads=[cst2], writes=[onesb])
    ctx.dma("sp", lambda E: E.dma_start(onesf[:, :], cst2[0:1, 128:256]),
            reads=[cst2], writes=[onesf])
    ctx.dma("pool", lambda E: E.dma_start(rowbuf[0:1, 0:4096], brow_in.t),
            reads=[brow_in], writes=[rowbuf])
    ctx.dma("pool", lambda E: E.dma_start(rowbuf[0:1, 4096:4104], bg_in.t),
            reads=[bg_in], writes=[rowbuf])
    ctx.op("dve", lambda E: E.memset(rowbuf[32:33, 4096:4608], 1.0),
           reads=[], writes=[rowbuf])
    ctx.op("dve", lambda E: E.memset(rowbuf[64:65, 4096:4224], 1.0),
           reads=[], writes=[rowbuf])
    ctx.dma("pool", lambda E: E.dma_start(
        wgw[:, :].rearrange("p (c m) -> p c m", c=NCC),
        wg_in.t.rearrange("(c p) m -> p c m", p=128)),
        reads=[wg_in], writes=[wgw])
    for i3, w in enumerate((wq, wk)):
        ctx.dma("pool", lambda E, i3=i3, w=w: E.dma_start(
            ar0[:, i3 * NCC * DM:(i3 + 1) * NCC * DM].rearrange(
                "p (c m) -> p c m", c=NCC),
            w.t.rearrange("(c p) m -> p c m", p=128)),
            reads=[w], writes=[ar0])
    ctx.dma("pool", lambda E: E.dma_start(
        argb[:, 0:NT * 64].rearrange("p (t m) -> p t m", t=NT),
        ropec.t.rearrange("(t p) m -> p t m", p=128)),
        reads=[ropec], writes=[argb])
    ctx.dma("pool", lambda E: E.dma_start(
        argb[:, NT * 64:2 * NT * 64].rearrange("p (t m) -> p t m", t=NT),
        ropes.t.rearrange("(t p) m -> p t m", p=128)),
        reads=[ropes], writes=[argb])
    ctx.op("dve", lambda E: E.memset(epsc[:, 0:1], 1e-5),
           reads=[], writes=[epsc])
    ctx.op("dve", lambda E: E.memset(
        big1[:, :].rearrange("p (n c) -> p n c", c=65)[:, :, 64:65], 1.0),
        reads=[], writes=[big1])

    for i in range(NT):
        ctx.dma("sp", lambda E, i=i: E.dma_start(
            xtile[:, :], xin.t[i * 128:(i + 1) * 128, :]),
            reads=[xin], writes=[xtile])
        emit_ln(ctx, tmp1, np.s_[:, :], xtile, np.s_[:, :], stat, tmp2,
                np.s_[:, :], DM, eps=epsc)
        ctx.dma("act", lambda E, i=i: E.dma_start(
            h_d.t[i * 128:(i + 1) * 128, :], tmp1[:, :]),
            reads=[tmp1], writes=[h_d])
        ctx.op("dve", lambda E: E.tensor_copy(lntile[:, :], tmp1[:, :]),
               reads=[tmp1], writes=[lntile])
        for c in range(NCC):
            ctx.op("pe", lambda E, c=c: E.transpose(
                pst[:, 0:128], lntile[:, c * 128:(c + 1) * 128], ident[:, :]),
                reads=[lntile, ident], writes=[pst])
            ctx.op("dve", lambda E, c=c, i=i: E.tensor_copy(
                big0[:, c * SEQ + i * 128: c * SEQ + (i + 1) * 128],
                pst[:, 0:128]),
                reads=[pst], writes=[big0])

    for i in range(NT):
        for t3 in range(2):
            for half in range(2):
                pb = pp[t3 * 2 + half]
                ctx.op("pe", lambda E, t3=t3, half=half, pb=pb: E.matmul(
                    pb[:, :], onesb[0:1, 0:128],
                    rowbuf[0:1, t3 * DM + half * 512:
                           t3 * DM + (half + 1) * 512],
                    start=True, stop=False),
                    reads=[onesb, rowbuf], writes=[pb])
                for c in range(NCC):
                    ctx.op("pe", lambda E, t3=t3, half=half, c=c, i=i, pb=pb:
                           E.matmul(pb[:, :],
                                    big0[:, c * SEQ + i * 128:
                                         c * SEQ + (i + 1) * 128],
                                    wp(t3, c * DM + half * 512,
                                       c * DM + (half + 1) * 512),
                                    start=False, stop=(c == NCC - 1)),
                           reads=[big0, ar0], writes=[pb])
        cosap = argb[:, i * 64:(i + 1) * 64]
        sinap = argb[:, NT * 64 + i * 64:NT * 64 + (i + 1) * 64]
        for t3 in range(2):
            for half in range(2):
                pb = pp[t3 * 2 + half]
                hofs = half * 512
                pb3 = pb[:, :].rearrange("p (h j) -> p h j", j=64)
                ctx.op("dve", lambda E, pb3=pb3, hofs=hofs, cosap=cosap:
                       E.tensor_tensor(
                           tmp1[:, hofs:hofs + 512].rearrange(
                               "p (h j) -> p h j", j=64),
                           pb3,
                           cosap.unsqueeze(1).broadcast_to(
                               (128, 8, 64)), ALU.mult),
                       reads=[pb, argb], writes=[tmp1])
                ctx.op("dve", lambda E, pb3=pb3, hofs=hofs, sinap=sinap:
                       E.tensor_tensor(
                           tmp2[:, hofs:hofs + 512].rearrange(
                               "p (h j) -> p h j", j=64)[:, :, 0:32],
                           pb3[:, :, 32:64],
                           sinap[:, 0:32].unsqueeze(1).broadcast_to((128, 8, 32)),
                           ALU.mult),
                       reads=[pb, argb], writes=[tmp2])
                ctx.op("dve", lambda E, pb3=pb3, hofs=hofs, sinap=sinap:
                       E.tensor_tensor(
                           tmp2[:, hofs:hofs + 512].rearrange(
                               "p (h j) -> p h j", j=64)[:, :, 32:64],
                           pb3[:, :, 0:32],
                           sinap[:, 32:64].unsqueeze(1).broadcast_to((128, 8, 32)),
                           ALU.mult),
                       reads=[pb, argb], writes=[tmp2])
            ctx.op("dve", lambda E: E.tensor_add(stage[:, :], tmp1[:, :],
                                                 tmp2[:, :]),
                   reads=[tmp1, tmp2], writes=[stage])
            dst = qT_d if t3 == 0 else kT_d
            for c in range(NCC):
                ctx.op("pe", lambda E, c=c: E.transpose(
                    pst[:, 0:128], stage[:, c * 128:(c + 1) * 128],
                    ident[:, :]),
                    reads=[stage, ident], writes=[pst])
                ctx.op("dve", lambda E, c=c: E.tensor_copy(
                    lntile[:, c * 128:(c + 1) * 128], pst[:, 0:128]),
                    reads=[pst], writes=[lntile])
            ctx.dma("sp", lambda E, i=i, dst=dst: E.dma_start(
                dst.t[:, :, i * 128:(i + 1) * 128].rearrange(
                    "c p m -> p c m"),
                lntile[:, :].rearrange("p (c m) -> p c m", c=NCC)),
                reads=[lntile], writes=[dst])

    ctx.dma("pool", lambda E: E.dma_start(
        ar0[:, 0:NCC * DM].rearrange("p (c m) -> p c m", c=NCC),
        wv.t.rearrange("(c p) m -> p c m", p=128)),
        reads=[wv], writes=[ar0])
    for i in range(NT):
        for half in range(2):
            pb = pp[4 + half]
            ctx.op("pe", lambda E, half=half, pb=pb: E.matmul(
                pb[:, :], onesb[0:1, 0:128],
                rowbuf[0:1, 2 * DM + half * 512:2 * DM + (half + 1) * 512],
                start=True, stop=False),
                reads=[onesb, rowbuf], writes=[pb])
            for c in range(NCC):
                ctx.op("pe", lambda E, half=half, c=c, i=i, pb=pb:
                       E.matmul(pb[:, :],
                                big0[:, c * SEQ + i * 128:
                                     c * SEQ + (i + 1) * 128],
                                wp(0, c * DM + half * 512,
                                   c * DM + (half + 1) * 512),
                                start=False, stop=(c == NCC - 1)),
                       reads=[big0, ar0], writes=[pb])
            ctx.op("dve", lambda E, i=i, half=half, pb=pb: E.tensor_copy(
                big1[:, :].rearrange("p (n h j) -> p n h j", h=16, j=65)
                [:, i, half * 8:(half + 1) * 8, 0:64],
                pb[:, :].rearrange("p (h j) -> p h j", j=64)),
                reads=[pb], writes=[big1])

    for pair in range(8):
        ctx.dma("sp", lambda E, pair=pair: E.dma_start(
            qhalf[:, 0:SEQ], qT_d.t[pair]), reads=[qT_d], writes=[qhalf])
        ctx.dma("sp", lambda E, pair=pair: E.dma_start(
            qhalf[:, SEQ:2 * SEQ], kT_d.t[pair]), reads=[kT_d],
            writes=[qhalf])
        for hh in range(2):
            hsl = slice(0, 64) if hh == 0 else slice(64, 128)
            habs = pair * 2 + hh
            for qs in range(8):
                outp = pp[2]
                for ktn in range(NT):
                    sb_ = pp[ktn % 2]
                    pt = pt2[ktn % 2]
                    ctx.op("pe", lambda E, ktn=ktn, qs=qs, hsl=hsl, sb_=sb_:
                           E.matmul(sb_[:, :],
                                    qhalf[hsl, SEQ + ktn * 128:
                                          SEQ + (ktn + 1) * 128],
                                    qhalf[hsl, qs * 512:(qs + 1) * 512],
                                    start=True, stop=True),
                           reads=[qhalf], writes=[sb_])
                    ctx.op("act", lambda E, sb_=sb_, pt=pt: E.activation(
                        pt[:, :], sb_[:, :], AF.Exp, bias=0.0, scale=1.0),
                        reads=[sb_], writes=[pt])
                    ctx.op("pe", lambda E, ktn=ktn, habs=habs, outp=outp,
                           pt=pt, st=(ktn == 0), sp_=(ktn == NT - 1):
                           E.matmul(outp[0:65, :],
                                    big1[:, :].rearrange(
                                        "p (n h j) -> p n h j", h=16, j=65)
                                    [:, ktn, habs, :],
                                    pt[:, :], start=st, stop=sp_),
                           reads=[big1, pt], writes=[outp])
                ctx.op("dve", lambda E, outp=outp: E.reciprocal(
                    tmp2[0:1, 0:512], outp[64:65, :]),
                    reads=[outp], writes=[tmp2])
                bc = pp[3]
                ctx.op("pe", lambda E, bc=bc: E.matmul(
                    bc[:, :], onesf[:, :], tmp2[0:1, 0:512], start=True,
                    stop=True),
                    reads=[onesf, tmp2], writes=[bc])
                ctx.op("dve", lambda E, bc=bc: E.tensor_copy(
                    tmp1[:, 0:512], bc[:, :]), reads=[bc], writes=[tmp1])
                ctx.op("dve", lambda E, outp=outp: E.tensor_tensor(
                    ystage[0:64, 0:512], outp[0:64, :], tmp1[0:64, 0:512],
                    ALU.mult),
                    reads=[outp, tmp1], writes=[ystage])
                ctx.dma("act", lambda E, pair=pair, hh=hh, qs=qs: E.dma_start(
                    yfm_d.t[pair, hh * 64:(hh + 1) * 64,
                            qs * 512:(qs + 1) * 512],
                    ystage[0:64, 0:512]),
                    reads=[ystage], writes=[yfm_d])

    ctx.dma("sp", lambda E: E.dma_start(
        gbufv[:, 0:DM], grow_in.t[0:1, 0:DM].broadcast_to((128, DM))),
        reads=[grow_in], writes=[argb])
    ctx.dma("sp", lambda E: E.dma_start(
        gbufv[:, DM:2 * DM],
        grow_in.t[0:1, DM:2 * DM].broadcast_to((128, DM))),
        reads=[grow_in], writes=[argb])
    ctx.dma("pool", lambda E: E.dma_start(
        ar0[:, 0:NCC * DM].rearrange("p (c m) -> p c m", c=NCC),
        wo.t.rearrange("(c p) m -> p c m", p=128)),
        reads=[wo], writes=[ar0])
    for i in range(NT):
        yb = lntile
        ctx.dma("sp", lambda E, i=i, yb=yb: E.dma_start(
            yb[:, :].rearrange("p (c m) -> p c m", c=NCC),
            yfm_d.t[:, :, i * 128:(i + 1) * 128].rearrange("c p m -> p c m")),
            reads=[yfm_d], writes=[yb])
        for half in range(2):
            zp = pp[4 + half]
            ctx.op("pe", lambda E, half=half, zp=zp: E.matmul(
                zp[:, :], onesb[0:1, 0:128],
                rowbuf[0:1, 3 * DM + half * 512:3 * DM + (half + 1) * 512],
                start=True, stop=False),
                reads=[onesb, rowbuf], writes=[zp])
            for c in range(NCC):
                ctx.op("pe", lambda E, c=c, half=half, zp=zp, yb=yb:
                       E.matmul(zp[:, :],
                                yb[:, c * 128:(c + 1) * 128],
                                wp(0, c * DM + half * 512,
                                   c * DM + (half + 1) * 512),
                                start=False, stop=(c == NCC - 1)),
                       reads=[yb, ar0], writes=[zp])
        ctx.dma("sp", lambda E, i=i: E.dma_start(
            xtile[:, :], h_d.t[i * 128:(i + 1) * 128, :]),
            reads=[h_d], writes=[xtile])
        for half in range(2):
            zp = pp[4 + half]
            ctx.op("dve", lambda E, half=half, zp=zp: E.tensor_tensor(
                tmp2[:, half * 512:(half + 1) * 512], zp[:, :],
                xtile[:, half * 512:(half + 1) * 512], ALU.add),
                reads=[zp, xtile], writes=[tmp2])
        emit_ln(ctx, mtile, np.s_[:, 0:DM], tmp2, np.s_[:, 0:DM], stat,
                tmp1, np.s_[:, 0:DM], DM, eps=epsc)
        ctx.op("dve", lambda E: E.tensor_tensor(
            mtile[:, 0:DM], mtile[:, 0:DM], gbufv[:, 0:DM], ALU.mult),
            reads=[mtile, argb], writes=[mtile])
        ctx.op("dve", lambda E: E.tensor_tensor(
            mtile[:, 0:DM], mtile[:, 0:DM], gbufv[:, DM:2 * DM], ALU.add),
            reads=[mtile, argb], writes=[mtile])
        ctx.dma("act", lambda E, i=i: E.dma_start(
            x1p_d.t[i * 128:(i + 1) * 128, :], mtile[:, 0:DM]),
            reads=[mtile], writes=[x1p_d])
        ctx.op("dve", lambda E: E.tensor_copy(lntile[:, :], mtile[:, 0:DM]),
               reads=[mtile], writes=[lntile])
        for c in range(NCC):
            ctx.op("pe", lambda E, c=c: E.transpose(
                pst[:, 0:128], lntile[:, c * 128:(c + 1) * 128], ident[:, :]),
                reads=[lntile, ident], writes=[pst])
            ctx.op("dve", lambda E, c=c: E.tensor_copy(
                stage[:, c * 128:(c + 1) * 128], pst[:, 0:128]),
                reads=[pst], writes=[stage])
        ctx.dma("act", lambda E, i=i: E.dma_start(
            x1pT_d.t[:, :, i * 128:(i + 1) * 128].rearrange("c p m -> p c m"),
            stage[:, :].rearrange("p (c m) -> p c m", c=NCC)),
            reads=[stage], writes=[x1pT_d])
        gp = pp[3]
        ctx.op("pe", lambda E, gp=gp: E.matmul(
            gp[:, 0:8], onesb[0:1, 0:128], rowbuf[0:1, 4096:4104],
            start=True, stop=False),
            reads=[onesb, rowbuf], writes=[gp])
        for c in range(NCC):
            ctx.op("pe", lambda E, c=c, gp=gp: E.matmul(
                gp[:, 0:8], stage[:, c * 128:(c + 1) * 128],
                wgw[:, c * 8:(c + 1) * 8],
                start=False, stop=(c == NCC - 1)),
                reads=[stage, wgw], writes=[gp])
        ctx.op("act", lambda E, gp=gp: E.activation(
            tmp2[:, 0:8], gp[:, 0:8], AF.Exp, bias=0.0, scale=1.0,
            accum_out=stat[:, 8:9]),
            reads=[gp], writes=[tmp2, stat])
        ctx.op("dve", lambda E: E.reciprocal(stat[:, 9:10], stat[:, 8:9]),
               reads=[stat], writes=[stat])
        ctx.op("dve", lambda E: E.tensor_scalar_mul(
            tmp2[:, 0:8], tmp2[:, 0:8], stat[:, 9:10]),
            reads=[stat, tmp2], writes=[tmp2])
        ctx.op("dve", lambda E: E.reduce_max(stat[:, 10:11], tmp2[:, 0:8],
                                             axis=AX.X),
               reads=[tmp2], writes=[stat])
        ctx.op("dve", lambda E: E.tensor_tensor(
            tmp2[:, 8:16], tmp2[:, 0:8],
            stat[:, 10:11].broadcast_to((128, 8)), ALU.is_ge),
            reads=[tmp2, stat], writes=[tmp2])
        ctx.op("dve", lambda E: E.scalar_tensor_tensor(
            tmp1[:, 8:16], tmp2[:, 8:16], -1e9, tmp2[:, 0:8],
            ALU.mult, ALU.add),
            reads=[tmp2], writes=[tmp1])
        ctx.op("dve", lambda E: E.reduce_max(stat[:, 12:13], tmp1[:, 8:16],
                                             axis=AX.X),
               reads=[tmp1], writes=[stat])
        ctx.op("dve", lambda E: E.tensor_tensor(
            tmp1[:, 0:8], tmp2[:, 0:8],
            stat[:, 12:13].broadcast_to((128, 8)), ALU.is_ge),
            reads=[tmp2, stat], writes=[tmp1])
        ctx.op("dve", lambda E: E.tensor_tensor(
            tmp1[:, 8:16], tmp1[:, 0:8], tmp2[:, 0:8], ALU.mult),
            reads=[tmp1, tmp2], writes=[tmp1])
        ctx.op("dve", lambda E: E.reduce_sum(stat[:, 13:14], tmp1[:, 8:16],
                                             axis=AX.X),
               reads=[tmp1], writes=[stat])
        ctx.op("dve", lambda E: E.reciprocal(stat[:, 14:15], stat[:, 13:14]),
               reads=[stat], writes=[stat])
        ctx.op("dve", lambda E, i=i: E.tensor_scalar_mul(
            comb[:, i * 8:(i + 1) * 8], tmp1[:, 8:16], stat[:, 14:15]),
            reads=[tmp1, stat], writes=[comb])

    if dbg:
        comb_d = ctx.dout("comb_d", [128, NT * 8], F32)
        ctx.dma("act", lambda E: E.dma_start(comb_d.t, comb[:, :]),
                reads=[comb], writes=[comb_d])

    HF = 8192
    for e in range(nexp):
        ctx.dma("pool", lambda E, e=e: E.dma_start(
            big0[:, :].rearrange("p (c m) -> p c m", c=NCC),
            w1_in.t[e].rearrange("(c p) m -> p c m", p=128)),
            reads=[w1_in], writes=[big0])
        ctx.dma("pool", lambda E, e=e: E.dma_start(
            big1[:, 0:32 * DM].rearrange("p (c m) -> p c m", c=32),
            w2_in.t[e].rearrange("(c p) m -> p c m", p=128)),
            reads=[w2_in], writes=[big1])
        ctx.dma("pool", lambda E, e=e: E.dma_start(
            rowbuf[32:33, 0:4096], b1_in.t[e:e + 1, :]),
            reads=[b1_in], writes=[rowbuf])
        ctx.dma("pool", lambda E, e=e: E.dma_start(
            rowbuf[64:65, 0:DM], b2_in.t[e:e + 1, :]),
            reads=[b2_in], writes=[rowbuf])
        for ns in range(8):
            ctx.dma("sp", lambda E, ns=ns: E.dma_start(
                argb[:, 0:4096].rearrange("p (c m) -> p c m", c=NCC),
                x1pT_d.t[:, :, ns * 512:(ns + 1) * 512].rearrange(
                    "c p m -> p c m")),
                reads=[x1pT_d], writes=[argb])
            for fc in range(32):
                pb = pp[fc % 2]
                ctx.op("pe", lambda E, fc=fc, pb=pb: E.matmul(
                    pb[:, :], rowbuf[32:33, fc * 128:(fc + 1) * 128],
                    rowbuf[32:33, 4096:4608], start=True, stop=False),
                    reads=[rowbuf], writes=[pb])
                for c in range(NCC):
                    ctx.op("pe", lambda E, fc=fc, c=c, pb=pb:
                           E.matmul(pb[:, :],
                                    big0[:, c * SEQ + fc * 128:
                                         c * SEQ + (fc + 1) * 128],
                                    argb[:, c * 512:(c + 1) * 512],
                                    start=False, stop=(c == NCC - 1)),
                           reads=[big0, argb], writes=[pb])
                ctx.op("act", lambda E, fc=fc, pb=pb: E.activation(
                    ar0[:, fc * 512:(fc + 1) * 512], pb[:, :], AF.Gelu,
                    bias=0.0, scale=1.0),
                    reads=[pb], writes=[ar0])
            if dbg and e == 0 and ns == 0:
                hfm_d = ctx.dout("hfm_d", [128, 32 * 512], BF16)
                ctx.dma("act", lambda E: E.dma_start(
                    hfm_d.t, ar0[:, 0:16384]), reads=[ar0], writes=[hfm_d])
            for nsub in range(4):
                i_abs = ns * 4 + nsub
                for half in range(2):
                    yp = pp[(nsub * 2 + half) % 6]
                    ctx.op("pe", lambda E, half=half, yp=yp: E.matmul(
                        yp[:, :], rowbuf[64:65, 4096:4224],
                        rowbuf[64:65, half * 512:(half + 1) * 512],
                        start=True, stop=False),
                        reads=[rowbuf], writes=[yp])
                    for fc in range(32):
                        ctx.op("pe", lambda E, fc=fc, nsub=nsub, half=half,
                               yp=yp:
                               E.matmul(yp[:, :],
                                        ar0[:, fc * 512 + nsub * 128:
                                            fc * 512 + (nsub + 1) * 128],
                                        big1[:, fc * DM + half * 512:
                                             fc * DM + (half + 1) * 512],
                                        start=False, stop=(fc == 31)),
                               reads=[ar0, big1], writes=[yp])
                    ctx.op("dve", lambda E, i_abs=i_abs, half=half, e=e,
                           yp=yp:
                           E.tensor_scalar_mul(
                               mtile[:, half * 512:(half + 1) * 512],
                               yp[:, :], comb[:, i_abs * 8 + e:
                                              i_abs * 8 + e + 1]),
                           reads=[yp, comb], writes=[mtile])
                ctx.dma("pool", lambda E, i_abs=i_abs, e=e: E.dma_start(
                    moe_d.t[i_abs * 128:(i_abs + 1) * 128, :],
                    mtile[:, 0:DM],
                    accum_op=(ALU.bypass if e == 0 else ALU.add)),
                    reads=[mtile], writes=[moe_d])

    ctx.dma("sp", lambda E: E.dma_start(
        gbufv[:, 0:DM],
        grow_in.t[0:1, 2 * DM:3 * DM].broadcast_to((128, DM))),
        reads=[grow_in], writes=[argb])
    ctx.dma("sp", lambda E: E.dma_start(
        gbufv[:, DM:2 * DM],
        grow_in.t[0:1, 3 * DM:4 * DM].broadcast_to((128, DM))),
        reads=[grow_in], writes=[argb])
    for i in range(NT):
        ctx.dma("sp", lambda E, i=i: E.dma_start(
            xtile[:, :], x1p_d.t[i * 128:(i + 1) * 128, :]),
            reads=[x1p_d], writes=[xtile])
        ctx.dma("sp", lambda E, i=i: E.dma_start(
            mtile[:, :], moe_d.t[i * 128:(i + 1) * 128, :]),
            reads=[moe_d], writes=[mtile])
        ctx.op("dve", lambda E: E.tensor_tensor(
            tmp2[:, 0:DM], xtile[:, :], mtile[:, :], ALU.add),
            reads=[xtile, mtile], writes=[tmp2])
        emit_ln(ctx, tmp1, np.s_[:, 0:DM], tmp2, np.s_[:, 0:DM], stat,
                mtile, np.s_[:, 0:DM], DM, eps=epsc)
        ctx.op("dve", lambda E: E.tensor_tensor(
            tmp1[:, 0:DM], tmp1[:, 0:DM], gbufv[:, 0:DM], ALU.mult),
            reads=[tmp1, argb], writes=[tmp1])
        ctx.op("dve", lambda E: E.tensor_tensor(
            tmp1[:, 0:DM], tmp1[:, 0:DM], gbufv[:, DM:2 * DM], ALU.add),
            reads=[tmp1, argb], writes=[tmp1])
        ctx.dma("sp", lambda E, i=i: E.dma_start(
            xtile[:, :], xin.t[i * 128:(i + 1) * 128, :]),
            reads=[xin], writes=[xtile])
        emit_ln(ctx, mtile, np.s_[:, 0:DM], tmp1, np.s_[:, 0:DM], stat,
                tmp2, np.s_[:, 0:DM], DM, res=xtile, res_sl=np.s_[:, :],
                eps=epsc)
        ctx.dma("act", lambda E, i=i: E.dma_start(
            xout.t[i * 128:(i + 1) * 128, :], mtile[:, 0:DM]),
            reads=[mtile], writes=[xout])




def host_consts_b2():
    c = np.zeros((128, 768), dtype=np.float32)
    c[:, 0:128] = np.eye(128, dtype=np.float32)
    c[0, 128:640] = 1.0
    return c


def host_rope_b2():
    HD = 64
    SC = float(1.0 / (2.0 * np.sqrt(2.0)))
    inv = 1.0 / 10000.0 ** (np.arange(0, HD, 2, dtype=np.float32) / HD)
    t = np.arange(SEQ, dtype=np.float32)
    fr = (t[:, None] * inv[None, :]).astype(np.float32)
    emb = np.concatenate([fr, fr], axis=-1)
    cos = (np.cos(emb) * SC).astype(np.float32)
    sin = np.sin(emb).astype(np.float32)
    ss = np.empty((SEQ, HD), dtype=np.float32)
    ss[:, 0:32] = -sin[:, 0:32] * SC
    ss[:, 32:64] = sin[:, 32:64] * SC
    return cos, ss


def emit_b4(ctx, com, xin, xout):
    nc = ctx.nc
    wqkv_in = ctx.din("wqkv_in", [6, DM, 3072])
    qs_in = ctx.din("qs_in", [6, 64])
    ks_in = ctx.din("ks_in", [6, 64])
    wo_in = ctx.din("wo_in", [6, DM, DM])
    lng_in = ctx.din("lng_in", [6, DM])
    lnb_in = ctx.din("lnb_in", [6, DM])
    w1a_in = ctx.din("w1a_in", [6, DM, B4_FFP])
    w1g_in = ctx.din("w1g_in", [6, DM, B4_FFP])
    b1a_in = ctx.din("b1a_in", [6, B4_FFP])
    b1g_in = ctx.din("b1g_in", [6, B4_FFP])
    w2p_in = ctx.din("w2p_in", [6, B4_FFP, DM])
    b2f_in = ctx.din("b2f_in", [6, DM])
    tab_in = ctx.din("tab_in", [SEQ, 4 * 64])
    msk_in = ctx.din("msk_in", [128, 4 * 256])
    cst = ctx.din("cst", [128, 768])

    xA_d = ctx.dram("b4xA_d", [SEQ, DM], F32)
    xB_d = ctx.dram("b4xB_d", [SEQ, DM], F32)
    qT_d = ctx.dram("b4qT_d", [NCC, 128, SEQ], BF16)
    kT_d = ctx.dram("b4kT_d", [NCC, 128, SEQ], BF16)
    yfm_d = ctx.dram("b4yfm_d", [NCC, 128, SEQ], BF16)

    xtile, lntile, stage = com.xtile, com.lntile, com.stage
    tmp1, tmp2, stat, epsc = com.tmp1, com.tmp2, com.stat, com.epsc
    ident, onesf = com.ident, com.onesf
    pp, pst = com.pp, com.pst

    arA = ctx.vsb("arA", [128, 24576], BF16)
    arB = ctx.vsb("arB", [128, 33792], BF16)
    arC = ctx.vsb("arC", [128, 22528], BF16)
    tabs = ctx.vsb("tabs", [128, 4 * NT * 64], BF16)
    qsb = ctx.vsb("qsb", [128, 4 * 64], BF16)
    xnslab = ctx.vsb("xnslab", [128, NCC * 512], BF16)
    shbuf = ctx.vsb("shbuf", [128, 3328], BF16)
    gg = ctx.vsb("gg", [128, 512], BF16)
    masks = shbuf
    rowbuf = shbuf
    pt = gg
    glbuf = tabs
    b2_in = b2f_in

    gbv = tabs.t[:, :].bitcast(F32)

    ctx.dma("pool", lambda E: E.dma_start(ident[:, :], cst[:, 0:128]),
            reads=[cst], writes=[ident])
    ctx.dma("sp", lambda E: E.dma_start(onesf[:, :], cst[0:1, 128:256]),
            reads=[cst], writes=[onesf])
    ctx.op("dve", lambda E: E.memset(epsc[:, 0:1], 1e-5),
           reads=[], writes=[epsc])
    ctx.op("dve", lambda E: E.memset(epsc[:, 1:2], -12.0),
           reads=[], writes=[epsc])

    for i in range(NT):
        ctx.dma("sp", lambda E, i=i: E.dma_start(
            xtile[:, :], xin.t[i * 128:(i + 1) * 128, :]),
            reads=[xin], writes=[xtile])
        emit_ln(ctx, tmp1, np.s_[:, :], xtile, np.s_[:, :], stat, tmp2,
                np.s_[:, :], DM, eps=epsc)
        ctx.dma("act", lambda E, i=i: E.dma_start(
            xA_d.t[i * 128:(i + 1) * 128, :], tmp1[:, :]),
            reads=[tmp1], writes=[xA_d])

    def shifted_load(xsrc, i):
        ctx.dma("sp", lambda E, i=i: E.dma_start(
            xtile[:, 0:512], xsrc.t[i * 128:(i + 1) * 128, 0:512]),
            reads=[xsrc], writes=[xtile])
        if i == 0:
            ctx.op("dve", lambda E: E.memset(xtile[0:1, 512:1024], 0.0),
                   reads=[], writes=[xtile])
            ctx.dma("sp", lambda E: E.dma_start(
                xtile[1:128, 512:1024], xsrc.t[0:127, 512:1024]),
                reads=[xsrc], writes=[xtile])
        else:
            ctx.dma("sp", lambda E, i=i: E.dma_start(
                xtile[:, 512:1024],
                xsrc.t[i * 128 - 1:(i + 1) * 128 - 1, 512:1024]),
                reads=[xsrc], writes=[xtile])

    for lyr in range(6):
        xCur, xNxt = (xA_d, xB_d)
        ctx.dma("pool", lambda E, lyr=lyr: E.dma_start(
            arA[:, :].rearrange("p (c m) -> p c m", c=NCC),
            wqkv_in.t[lyr].rearrange("(c p) m -> p c m", p=128)),
            reads=[wqkv_in], writes=[arA])
        for base, srcv in ((0, qs_in), (128, ks_in)):
            ctx.dma("pool", lambda E, base=base, srcv=srcv, lyr=lyr:
                    E.dma_start(qsb[:, base:base + 64],
                                srcv.t[lyr:lyr + 1, :].broadcast_to(
                                    (128, 64))),
                    reads=[srcv], writes=[qsb])
            ctx.dma("pool", lambda E, base=base, srcv=srcv, lyr=lyr:
                    E.dma_start(qsb[:, base + 64:base + 96],
                                srcv.t[lyr:lyr + 1, 32:64].broadcast_to(
                                    (128, 32))),
                    reads=[srcv], writes=[qsb])
            ctx.dma("pool", lambda E, base=base, srcv=srcv, lyr=lyr:
                    E.dma_start(qsb[:, base + 96:base + 128],
                                srcv.t[lyr:lyr + 1, 0:32].broadcast_to(
                                    (128, 32))),
                    reads=[srcv], writes=[qsb])
        for t4 in range(4):
            qv = (0, 64, 128, 192)[t4]
            for g2 in range(2):
                ctx.dma("sp", lambda E, t4=t4, g2=g2: E.dma_start(
                    tmp1[:, :].rearrange("p (t m) -> p t m", t=16),
                    tab_in.t[g2 * 2048:(g2 + 1) * 2048,
                             t4 * 64:(t4 + 1) * 64].rearrange(
                        "(t p) m -> p t m", p=128)),
                    reads=[tab_in], writes=[tmp1])
                ctx.op("dve", lambda E, t4=t4, g2=g2, qv=qv: E.tensor_tensor(
                    tabs[:, t4 * 2048 + g2 * 1024:
                         t4 * 2048 + (g2 + 1) * 1024].rearrange(
                        "p (t m) -> p t m", t=16),
                    tmp1[:, :].rearrange("p (t m) -> p t m", t=16),
                    qsb[:, qv:qv + 64].unsqueeze(1).broadcast_to(
                        (128, 16, 64)),
                    ALU.mult),
                    reads=[tmp1, qsb], writes=[tabs])
        ctx.op("dve", lambda E: E.memset(
            arB[:, 0:NT * 1040].rearrange("p (n c) -> p n c", c=65)
            [:, :, 64:65], 1.0),
            reads=[], writes=[arB])

        for i in range(NT):
            shifted_load(xCur, i)
            ctx.op("dve", lambda E: E.tensor_copy(lntile[:, :], xtile[:, :]),
                   reads=[xtile], writes=[lntile])
            for c in range(NCC):
                ctx.op("pe", lambda E, c=c: E.transpose(
                    pst[:, 0:128], lntile[:, c * 128:(c + 1) * 128],
                    ident[:, :]),
                    reads=[lntile, ident], writes=[pst])
                ctx.op("dve", lambda E, c=c: E.tensor_copy(
                    stage[:, c * 128:(c + 1) * 128], pst[:, 0:128]),
                    reads=[pst], writes=[stage])
            for fb in range(6):
                pb = pp[fb]
                for c in range(NCC):
                    ctx.op("pe", lambda E, fb=fb, c=c, pb=pb: E.matmul(
                        pb[:, :], stage[:, c * 128:(c + 1) * 128],
                        arA[:, c * 3072 + fb * 512:c * 3072 + (fb + 1) * 512],
                        start=(c == 0), stop=(c == NCC - 1)),
                        reads=[stage, arA], writes=[pb])
            for tq in range(2):
                tb = tq * 2048
                cqb = (0 if tq == 0 else 2) * 2048
                sqb = (1 if tq == 0 else 3) * 2048
                for hb in range(2):
                    pb = pp[tq * 2 + hb]
                    hofs = hb * 512
                    ctx.op("act", lambda E, pb=pb, hofs=hofs: E.activation(
                        tmp1[:, hofs:hofs + 512], pb[:, :], AF.Square,
                        bias=0.0, scale=1.0),
                        reads=[pb], writes=[tmp1])
                    ctx.op("dve", lambda E, hofs=hofs, hb=hb: E.reduce_sum(
                        stat[:, hb * 8:hb * 8 + 8],
                        tmp1[:, hofs:hofs + 512].rearrange(
                            "p (h j) -> p h j", j=64),
                        axis=AX.X),
                        reads=[tmp1], writes=[stat])
                ctx.op("act", lambda E: E.activation(
                    stat[:, 0:16], stat[:, 0:16], AF.Sqrt, bias=0.0,
                    scale=1.0),
                    reads=[stat], writes=[stat])
                ctx.op("dve", lambda E: E.tensor_scalar_max(
                    stat[:, 0:16], stat[:, 0:16], 1e-12),
                    reads=[stat], writes=[stat])
                ctx.op("dve", lambda E: E.reciprocal(stat[:, 0:16],
                                                     stat[:, 0:16]),
                       reads=[stat], writes=[stat])
                for hb in range(2):
                    pb = pp[tq * 2 + hb]
                    hofs = hb * 512
                    ctx.op("dve", lambda E, pb=pb, hofs=hofs, hb=hb:
                           E.tensor_tensor(
                               tmp2[:, hofs:hofs + 512].rearrange(
                                   "p (h j) -> p h j", j=64),
                               pb[:, :].rearrange("p (h j) -> p h j", j=64),
                               stat[:, hb * 8:hb * 8 + 8].unsqueeze(
                                   2).broadcast_to((128, 8, 64)),
                               ALU.mult),
                           reads=[pb, stat], writes=[tmp2])
                cosap = tabs[:, cqb + i * 64:cqb + (i + 1) * 64]
                sinap = tabs[:, sqb + i * 64:sqb + (i + 1) * 64]
                t23 = tmp2[:, :].rearrange("p (h j) -> p h j", j=64)
                ctx.op("dve", lambda E, sinap=sinap, t23=t23:
                       E.tensor_tensor(
                           tmp1[:, :].rearrange(
                               "p (h j) -> p h j", j=64)[:, :, 0:32],
                           t23[:, :, 32:64],
                           sinap[:, 0:32].unsqueeze(1).broadcast_to(
                               (128, 16, 32)),
                           ALU.mult),
                       reads=[tmp2, tabs], writes=[tmp1])
                ctx.op("dve", lambda E, sinap=sinap, t23=t23:
                       E.tensor_tensor(
                           tmp1[:, :].rearrange(
                               "p (h j) -> p h j", j=64)[:, :, 32:64],
                           t23[:, :, 0:32],
                           sinap[:, 32:64].unsqueeze(1).broadcast_to(
                               (128, 16, 32)),
                           ALU.mult),
                       reads=[tmp2, tabs], writes=[tmp1])
                ctx.op("dve", lambda E, cosap=cosap, t23=t23:
                       E.tensor_tensor(
                           lntile[:, :].rearrange("p (h j) -> p h j", j=64),
                           t23,
                           cosap.unsqueeze(1).broadcast_to((128, 16, 64)),
                           ALU.mult),
                       reads=[tmp2, tabs], writes=[lntile])
                ctx.op("dve", lambda E: E.tensor_add(
                    lntile[:, :], lntile[:, :], tmp1[:, :]),
                    reads=[lntile, tmp1], writes=[lntile])
                for c in range(NCC):
                    ctx.op("pe", lambda E, c=c: E.transpose(
                        pst[:, 0:128], lntile[:, c * 128:(c + 1) * 128],
                        ident[:, :]),
                        reads=[lntile, ident], writes=[pst])
                    ctx.op("dve", lambda E, c=c: E.tensor_copy(
                        stage[:, c * 128:(c + 1) * 128], pst[:, 0:128]),
                        reads=[pst], writes=[stage])
                dst = qT_d if tq == 0 else kT_d
                ctx.dma("sp", lambda E, i=i, dst=dst: E.dma_start(
                    dst.t[:, :, i * 128:(i + 1) * 128].rearrange(
                        "c p m -> p c m"),
                    stage[:, :].rearrange("p (c m) -> p c m", c=NCC)),
                    reads=[stage], writes=[dst])
            for hb in range(2):
                pb = pp[4 + hb]
                ctx.op("dve", lambda E, i=i, hb=hb, pb=pb: E.tensor_copy(
                    arB[:, 0:NT * 1040].rearrange(
                        "p (n h j) -> p n h j", h=16, j=65)
                    [:, i, hb * 8:(hb + 1) * 8, 0:64],
                    pb[:, :].rearrange("p (h j) -> p h j", j=64)),
                    reads=[pb], writes=[arB])

        ctx.dma("pool", lambda E: E.dma_start(shbuf[:, 0:1024], msk_in.t),
                reads=[msk_in], writes=[shbuf])
        for pair in range(8):
            ctx.dma("sp", lambda E, pair=pair: E.dma_start(
                arC[:, 0:SEQ], qT_d.t[pair]), reads=[qT_d], writes=[arC])
            ctx.dma("sp", lambda E, pair=pair: E.dma_start(
                arC[:, SEQ:2 * SEQ], kT_d.t[pair]), reads=[kT_d],
                writes=[arC])
            for hh in range(2):
                hsl = slice(0, 64) if hh == 0 else slice(64, 128)
                habs = pair * 2 + hh
                for w in range(16):
                    outp = pp[2]
                    j0 = 2 if w == 0 else 0
                    for j in range(j0, 4):
                        ktn = 2 * w - 2 + j
                        sb_ = pp[j % 2]
                        est = tmp1
                        ptb = gg
                        ctx.op("pe", lambda E, ktn=ktn, w=w, hsl=hsl,
                               sb_=sb_:
                               E.matmul(sb_[:, 0:256],
                                        arC[hsl, SEQ + ktn * 128:
                                            SEQ + (ktn + 1) * 128],
                                        arC[hsl, w * 256:(w + 1) * 256],
                                        start=True, stop=True),
                               reads=[arC], writes=[sb_])
                        ctx.op("act", lambda E, sb_=sb_, est=est:
                               E.activation(
                                   est[:, 0:256], sb_[:, 0:256], AF.Exp,
                                   bias=epsc[:, 1:2], scale=1.0),
                               reads=[sb_, epsc], writes=[est])
                        ctx.op("dve", lambda E, j=j, est=est, ptb=ptb:
                               E.tensor_tensor(
                                   ptb[:, 0:256], est[:, 0:256],
                                   masks[:, j * 256:(j + 1) * 256],
                                   ALU.mult),
                               reads=[est, masks], writes=[ptb])
                        ctx.op("pe", lambda E, ktn=ktn, habs=habs, ptb=ptb,
                               outp=outp, st=(j == j0), sp_=(j == 3):
                               E.matmul(outp[0:65, 0:256],
                                        arB[:, ktn * 1040 + habs * 65:
                                            ktn * 1040 + habs * 65 + 65],
                                        ptb[:, 0:256], start=st, stop=sp_),
                               reads=[arB, ptb], writes=[outp])
                    ctx.op("dve", lambda E, outp=outp: E.reciprocal(
                        tmp2[0:1, 0:256], outp[64:65, 0:256]),
                        reads=[outp], writes=[tmp2])
                    bc = pp[3]
                    ctx.op("pe", lambda E, bc=bc: E.matmul(
                        bc[:, 0:256], onesf[:, :], tmp2[0:1, 0:256],
                        start=True, stop=True),
                        reads=[onesf, tmp2], writes=[bc])
                    ctx.op("dve", lambda E, bc=bc: E.tensor_copy(
                        tmp1[:, 0:256], bc[:, 0:256]),
                        reads=[bc], writes=[tmp1])
                    ctx.op("dve", lambda E, outp=outp: E.tensor_tensor(
                        xnslab[0:64, 0:256], outp[0:64, 0:256],
                        tmp1[0:64, 0:256], ALU.mult),
                        reads=[outp, tmp1], writes=[xnslab])
                    ctx.dma("act", lambda E, pair=pair, hh=hh, w=w:
                            E.dma_start(
                                yfm_d.t[pair, hh * 64:(hh + 1) * 64,
                                        w * 256:(w + 1) * 256],
                                xnslab[0:64, 0:256]),
                            reads=[xnslab], writes=[yfm_d])

        ctx.dma("pool", lambda E, lyr=lyr: E.dma_start(
            arC[:, 2 * SEQ:2 * SEQ + NCC * DM].rearrange(
                "p (c m) -> p c m", c=NCC),
            wo_in.t[lyr].rearrange("(c p) m -> p c m", p=128)),
            reads=[wo_in], writes=[arC])
        for i in range(NT):
            ctx.dma("sp", lambda E, i=i: E.dma_start(
                lntile[:, :].rearrange("p (c m) -> p c m", c=NCC),
                yfm_d.t[:, :, i * 128:(i + 1) * 128].rearrange(
                    "c p m -> p c m")),
                reads=[yfm_d], writes=[lntile])
            for half in range(2):
                zp = pp[4 + half]
                for c in range(NCC):
                    ctx.op("pe", lambda E, c=c, half=half, zp=zp: E.matmul(
                        zp[:, :], lntile[:, c * 128:(c + 1) * 128],
                        arC[:, 2 * SEQ + c * DM + half * 512:
                            2 * SEQ + c * DM + (half + 1) * 512],
                        start=(c == 0), stop=(c == NCC - 1)),
                        reads=[lntile, arC], writes=[zp])
            ctx.dma("sp", lambda E, i=i: E.dma_start(
                xtile[:, :], xCur.t[i * 128:(i + 1) * 128, :]),
                reads=[xCur], writes=[xtile])
            for half in range(2):
                zp = pp[4 + half]
                ctx.op("dve", lambda E, half=half, zp=zp: E.tensor_tensor(
                    tmp1[:, half * 512:(half + 1) * 512], zp[:, :],
                    xtile[:, half * 512:(half + 1) * 512], ALU.add),
                    reads=[zp, xtile], writes=[tmp1])
            ctx.dma("act", lambda E, i=i: E.dma_start(
                xNxt.t[i * 128:(i + 1) * 128, :], tmp1[:, :]),
                reads=[tmp1], writes=[xNxt])

        ctx.dma("pool", lambda E, lyr=lyr: E.dma_start(
            arA[:, 0:NCC * B4_FFP].rearrange("p (c m) -> p c m", c=NCC),
            w1a_in.t[lyr].rearrange("(c p) m -> p c m", p=128)),
            reads=[w1a_in], writes=[arA])
        ctx.dma("pool", lambda E, lyr=lyr: E.dma_start(
            arB[:, 0:NCC * B4_FFP].rearrange("p (c m) -> p c m", c=NCC),
            w1g_in.t[lyr].rearrange("(c p) m -> p c m", p=128)),
            reads=[w1g_in], writes=[arB])
        ctx.dma("pool", lambda E, lyr=lyr: E.dma_start(
            arC[:, :].rearrange("p (c m) -> p c m", c=B4_NFC),
            w2p_in.t[lyr].rearrange("(c p) m -> p c m", p=128)),
            reads=[w2p_in], writes=[arC])
        ctx.dma("pool", lambda E, lyr=lyr: E.dma_start(
            rowbuf[0:1, 0:B4_FFP], b1a_in.t[lyr:lyr + 1, :]),
            reads=[b1a_in], writes=[rowbuf])
        ctx.dma("pool", lambda E, lyr=lyr: E.dma_start(
            rowbuf[32:33, 0:B4_FFP], b1g_in.t[lyr:lyr + 1, :]),
            reads=[b1g_in], writes=[rowbuf])
        ctx.dma("pool", lambda E, lyr=lyr: E.dma_start(
            rowbuf[64:65, 0:DM], b2_in.t[lyr:lyr + 1, :]),
            reads=[b2_in], writes=[rowbuf])
        ctx.op("dve", lambda E: E.memset(rowbuf[0:1, 2816:3328], 1.0),
               reads=[], writes=[rowbuf])
        ctx.op("dve", lambda E: E.memset(rowbuf[32:33, 2816:3328], 1.0),
               reads=[], writes=[rowbuf])
        ctx.op("dve", lambda E: E.memset(rowbuf[64:65, 1024:1152], 1.0),
               reads=[], writes=[rowbuf])
        ctx.dma("sp", lambda E, lyr=lyr: E.dma_start(
            gbv[:, 0:DM],
            lng_in.t[lyr:lyr + 1, :].broadcast_to((128, DM))),
            reads=[lng_in], writes=[tabs])
        ctx.dma("sp", lambda E, lyr=lyr: E.dma_start(
            gbv[:, DM:2 * DM],
            lnb_in.t[lyr:lyr + 1, :].broadcast_to((128, DM))),
            reads=[lnb_in], writes=[tabs])
        for ns in range(8):
            for sub in range(4):
                i = ns * 4 + sub
                shifted_load(xNxt, i)
                emit_ln(ctx, tmp1, np.s_[:, :], xtile, np.s_[:, :], stat,
                        tmp2, np.s_[:, :], DM, eps=epsc)
                ctx.op("dve", lambda E: E.tensor_tensor(
                    tmp1[:, :], tmp1[:, :], gbv[:, 0:DM], ALU.mult),
                    reads=[tmp1, tabs], writes=[tmp1])
                ctx.op("dve", lambda E: E.tensor_tensor(
                    tmp1[:, :], tmp1[:, :], gbv[:, DM:2 * DM], ALU.add),
                    reads=[tmp1, tabs], writes=[tmp1])
                ctx.op("dve", lambda E: E.tensor_copy(lntile[:, :],
                                                      tmp1[:, :]),
                       reads=[tmp1], writes=[lntile])
                for c in range(NCC):
                    ctx.op("pe", lambda E, c=c: E.transpose(
                        pst[:, 0:128], lntile[:, c * 128:(c + 1) * 128],
                        ident[:, :]),
                        reads=[lntile, ident], writes=[pst])
                    ctx.op("dve", lambda E, c=c, sub=sub: E.tensor_copy(
                        xnslab[:, c * 512 + sub * 128:
                               c * 512 + (sub + 1) * 128],
                        pst[:, 0:128]),
                        reads=[pst], writes=[xnslab])
            for fc in range(B4_NFC):
                gp_ = pp[0]
                ctx.op("pe", lambda E, fc=fc, gp_=gp_: E.matmul(
                    gp_[:, :], rowbuf[32:33, fc * 128:(fc + 1) * 128],
                    rowbuf[32:33, 2816:3328], start=True, stop=False),
                    reads=[rowbuf], writes=[gp_])
                for c in range(NCC):
                    ctx.op("pe", lambda E, fc=fc, c=c, gp_=gp_: E.matmul(
                        gp_[:, :],
                        arB[:, c * B4_FFP + fc * 128:
                            c * B4_FFP + (fc + 1) * 128],
                        xnslab[:, c * 512:(c + 1) * 512],
                        start=False, stop=(c == NCC - 1)),
                        reads=[arB, xnslab], writes=[gp_])
                ctx.op("act", lambda E, gp_=gp_: E.activation(
                    gg[:, :], gp_[:, :], AF.Gelu, bias=0.0, scale=1.0),
                    reads=[gp_], writes=[gg])
                ap_ = pp[1]
                ctx.op("pe", lambda E, fc=fc, ap_=ap_: E.matmul(
                    ap_[:, :], rowbuf[0:1, fc * 128:(fc + 1) * 128],
                    rowbuf[0:1, 2816:3328], start=True, stop=False),
                    reads=[rowbuf], writes=[ap_])
                for c in range(NCC):
                    ctx.op("pe", lambda E, fc=fc, c=c, ap_=ap_: E.matmul(
                        ap_[:, :],
                        arA[:, c * B4_FFP + fc * 128:
                            c * B4_FFP + (fc + 1) * 128],
                        xnslab[:, c * 512:(c + 1) * 512],
                        start=False, stop=(c == NCC - 1)),
                        reads=[arA, xnslab], writes=[ap_])
                ctx.op("dve", lambda E, fc=fc, ap_=ap_: E.tensor_tensor(
                    arB[:, NCC * B4_FFP + fc * 512:
                        NCC * B4_FFP + (fc + 1) * 512],
                    ap_[:, :], gg[:, :], ALU.mult),
                    reads=[ap_, gg], writes=[arB])
            for nsub in range(4):
                i = ns * 4 + nsub
                ctx.dma("sp", lambda E, i=i: E.dma_start(
                    xtile[:, :], xNxt.t[i * 128:(i + 1) * 128, :]),
                    reads=[xNxt], writes=[xtile])
                for half in range(2):
                    yp = pp[2 + half]
                    ctx.op("pe", lambda E, half=half, yp=yp: E.matmul(
                        yp[:, :], rowbuf[64:65, 1024:1152],
                        rowbuf[64:65, half * 512:(half + 1) * 512],
                        start=True, stop=False),
                        reads=[rowbuf], writes=[yp])
                    for fc in range(B4_NFC):
                        ctx.op("pe", lambda E, fc=fc, nsub=nsub, half=half,
                               yp=yp: E.matmul(
                            yp[:, :],
                            arB[:, NCC * B4_FFP + fc * 512 + nsub * 128:
                                NCC * B4_FFP + fc * 512 + (nsub + 1) * 128],
                            arC[:, fc * DM + half * 512:
                                fc * DM + (half + 1) * 512],
                            start=False, stop=(fc == B4_NFC - 1)),
                            reads=[arB, arC], writes=[yp])
                    ctx.op("dve", lambda E, half=half, yp=yp:
                           E.tensor_tensor(
                               tmp1[:, half * 512:(half + 1) * 512],
                               yp[:, :],
                               xtile[:, half * 512:(half + 1) * 512],
                               ALU.add),
                           reads=[yp, xtile], writes=[tmp1])
                ctx.dma("act", lambda E, i=i: E.dma_start(
                    xCur.t[i * 128:(i + 1) * 128, :], tmp1[:, :]),
                    reads=[tmp1], writes=[xCur])

    for i in range(NT):
        ctx.dma("sp", lambda E, i=i: E.dma_start(
            tmp2[:, :], xA_d.t[i * 128:(i + 1) * 128, :]),
            reads=[xA_d], writes=[tmp2])
        ctx.dma("sp", lambda E, i=i: E.dma_start(
            xtile[:, :], xin.t[i * 128:(i + 1) * 128, :]),
            reads=[xin], writes=[xtile])
        emit_ln(ctx, tmp1, np.s_[:, :], tmp2, np.s_[:, :], stat,
                lntile, np.s_[:, :], DM, res=xtile, res_sl=np.s_[:, :],
                eps=epsc)
        ctx.dma("act", lambda E, i=i: E.dma_start(
            xout.t[i * 128:(i + 1) * 128, :], tmp1[:, :]),
            reads=[tmp1], writes=[xout])




def host_tabs_b4():
    DIM_HEAD, WINDOW, n = 64, 256, SEQ
    QKS = float(np.sqrt(8.0))
    half = np.arange(0, DIM_HEAD, 2, dtype=np.float64)
    inv_freq = 1.0 / 10000.0 ** (half / DIM_HEAD)
    t = np.arange(n, dtype=np.float64)
    freqs = t[:, None] * inv_freq[None, :]
    freqs = np.concatenate([freqs, freqs], -1)
    base_scale = (half + 0.4 * DIM_HEAD) / (1.4 * DIM_HEAD)
    power = (t - n // 2) / (WINDOW // 2)
    scale = base_scale[None, :] ** power[:, None]
    scale = np.concatenate([scale, scale], -1)
    cos, sin = np.cos(freqs), np.sin(freqs)
    sgn = np.empty((n, 64))
    sgn[:, 0:32] = -sin[:, 0:32]
    sgn[:, 32:64] = sin[:, 32:64]
    inv = 1.0 / scale
    tab = np.concatenate([cos * scale * QKS, sgn * scale * QKS,
                          cos * inv * QKS, sgn * inv * QKS],
                         axis=1).astype(np.float32)
    return tab


def host_masks_b4():
    m = np.zeros((128, 4 * 256), dtype=np.float32)
    pc = np.arange(128)[:, None]
    pf = np.arange(256)[None, :]
    m[:, 0 * 256:1 * 256] = ((pf < 128) & (pc >= pf)).astype(np.float32)
    m[:, 1 * 256:2 * 256] = ((pf < 128) | (pc >= pf - 128)).astype(
        np.float32)
    m[:, 2 * 256:3 * 256] = ((pf >= 128) | (pc <= pf)).astype(np.float32)
    m[:, 3 * 256:4 * 256] = ((pf >= 128) & (pc <= pf - 128)).astype(
        np.float32)
    return m


def emit_b1(ctx, com, xin, xout):
    nc = ctx.nc
    wqT = ctx.din("wqT", [DM, DM])
    wkT = ctx.din("wkT", [DM, DM])
    wvT = ctx.din("wvT", [DM, DM])
    wlT = ctx.din("wlT", [DM, 32])
    wbT = ctx.din("wbT", [DM, 32])
    woT = ctx.din("woT", [DM, DM])
    winT_h = ctx.din("winT_h", [128, 512])
    woutT_h = ctx.din("woutT_h", [128, 512])
    woutD_h = ctx.din("woutD_h", [128, 512])
    bsel_in = ctx.din("bsel_in", [32, 16 * 128])
    cstb = ctx.din("cstb", [128, 512])

    o_d = ctx.dram("b1o_d", [NCC, 128, SEQ], BF16)

    xtile, lntile = com.xtile, com.lntile
    tmp1, tmp2, stat, epsc = com.tmp1, com.tmp2, com.stat, com.epsc
    ident = com.ident
    pst = com.pst
    z0, z1 = com.pp[0], com.pp[1]
    ps_a, ps_o, ps_s, ps_sb = com.pp[2], com.pp[3], com.pp[4], com.pp[5]

    hT = ctx.vsb("hT", [128, NCC * SEQ], BF16)
    qf = ctx.vsb("qf", [128, 8 * 1024], BF16)
    kf = ctx.vsb("kf", [128, 8 * 1024], BF16)
    vf = ctx.vsb("vf", [128, 8 * 1024], BF16)
    wpa = ctx.vsb("wpa", [128, NCC * DM], BF16)
    winT = ctx.vsb("winT", [128, 512], F32)
    woutT = ctx.vsb("woutT", [128, 512], F32)
    woutD = ctx.vsb("woutD", [128, 512], F32)
    winTb = ctx.vsb("winTb", [128, 512], BF16)
    woutTb = ctx.vsb("woutTb", [128, 512], BF16)
    woutDb = ctx.vsb("woutDb", [128, 512], BF16)
    lwt = ctx.vsb("lwt", [128, 2048], F32)
    lrc = ctx.vsb("lrc", [32, 128], F32)
    lrcb = ctx.vsb("lrcb", [32, 128], BF16)
    bselb = ctx.vsb("bselb", [32, 2048], BF16)
    wlb = ctx.vsb("wlb", [128, 512], BF16)
    e_all = ctx.vsb("e_all", [128, 1024], F32)
    p_all = ctx.vsb("p_all", [128, 1024], BF16)
    t_all = ctx.vsb("t_all", [128, 512], BF16)
    u_all = ctx.vsb("u_all", [128, 512], F32)
    gz = ctx.vsb("gz", [128, 512], BF16)
    ktok = ctx.vsb("ktok", [64, 1024], BF16)
    vtok = ctx.vsb("vtok", [64, 1024], BF16)
    ptok = ctx.vsb("ptok", [64, 1024], BF16)
    gztok = ctx.vsb("gztok", [64, 1024], BF16)
    ostage = ctx.vsb("ostage", [128, 512], BF16)
    updt = ctx.vsb("updt", [128, 512], F32)
    sseg = ctx.vsb("sseg", [2, 512], F32)
    e2b = ctx.vsb("e2b", [128, 2], BF16)
    e2tf = ctx.vsb("e2tf", [2, 128], F32)

    ctx.dma("pool", lambda E: E.dma_start(ident[:, :], cstb[:, 0:128]),
            reads=[cstb], writes=[ident])
    ctx.dma("pool", lambda E: E.dma_start(e2b[:, :], cstb[:, 128:130]),
            reads=[cstb], writes=[e2b])
    ctx.dma("sp", lambda E: E.dma_start(e2tf[:, :], cstb[0:2, 130:258]),
            reads=[cstb], writes=[e2tf])
    ctx.dma("pool", lambda E: E.dma_start(bselb[:, :], bsel_in.t),
            reads=[bsel_in], writes=[bselb])
    ctx.dma("pool", lambda E: E.dma_start(
        wlb[:, 0:256].rearrange("p (c m) -> p c m", c=NCC),
        wlT.t.rearrange("(c p) m -> p c m", p=128)),
        reads=[wlT], writes=[wlb])
    ctx.dma("pool", lambda E: E.dma_start(
        wlb[:, 256:512].rearrange("p (c m) -> p c m", c=NCC),
        wbT.t.rearrange("(c p) m -> p c m", p=128)),
        reads=[wbT], writes=[wlb])
    ctx.dma("sp", lambda E: E.dma_start(winT[:, :], winT_h.t),
            reads=[winT_h], writes=[winT])
    ctx.dma("sp", lambda E: E.dma_start(woutT[:, :], woutT_h.t),
            reads=[woutT_h], writes=[woutT])
    ctx.dma("sp", lambda E: E.dma_start(woutD[:, :], woutD_h.t),
            reads=[woutD_h], writes=[woutD])
    ctx.op("dve", lambda E: E.memset(epsc[:, 0:1], 1e-5),
           reads=[], writes=[epsc])

    for i in range(NT):
        ctx.dma("sp", lambda E, i=i: E.dma_start(
            xtile[:, :], xin.t[i * 128:(i + 1) * 128, :]),
            reads=[xin], writes=[xtile])
        emit_ln(ctx, tmp1, np.s_[:, :], xtile, np.s_[:, :], stat, tmp2,
                np.s_[:, :], DM, eps=epsc)
        ctx.op("dve", lambda E: E.tensor_copy(lntile[:, :], tmp1[:, :]),
               reads=[tmp1], writes=[lntile])
        for c in range(NCC):
            ctx.op("pe", lambda E, c=c: E.transpose(
                pst[:, 0:128], lntile[:, c * 128:(c + 1) * 128], ident[:, :]),
                reads=[lntile, ident], writes=[pst])
            ctx.op("dve", lambda E, c=c, i=i: E.tensor_copy(
                hT[:, c * SEQ + i * 128: c * SEQ + (i + 1) * 128],
                pst[:, 0:128]),
                reads=[pst], writes=[hT])

    for t2 in range(2):
        for ns in range(8):
            for c in range(NCC):
                ctx.op("pe", lambda E, t2=t2, ns=ns, c=c: E.matmul(
                    ps_a[0:32, :], wlb[:, t2 * 256 + c * 32:
                                       t2 * 256 + (c + 1) * 32],
                    hT[:, c * SEQ + ns * 512:c * SEQ + (ns + 1) * 512],
                    start=(c == 0), stop=(c == NCC - 1)),
                    reads=[wlb, hT], writes=[ps_a])
            ctx.op("act", lambda E: E.activation(
                tmp1[0:32, 0:512], ps_a[0:32, :], AF.Sigmoid,
                bias=0.0, scale=1.0),
                reads=[ps_a], writes=[tmp1])
            ctx.op("dve", lambda E, t2=t2, ns=ns: E.reduce_sum(
                lrc[0:32, t2 * 64 + ns * 8:t2 * 64 + (ns + 1) * 8],
                tmp1[0:32, 0:512].rearrange("p (t m) -> p t m", t=8),
                axis=AX.X),
                reads=[tmp1], writes=[lrc])
    ctx.op("dve", lambda E: E.tensor_scalar_mul(
        lrc[0:32, 0:64], lrc[0:32, 0:64], 0.001 / 64.0),
        reads=[lrc], writes=[lrc])
    ctx.op("dve", lambda E: E.tensor_scalar_mul(
        lrc[0:32, 64:128], lrc[0:32, 64:128], 0.9 / 64.0),
        reads=[lrc], writes=[lrc])
    ctx.op("dve", lambda E: E.tensor_copy(lrcb[0:32, :], lrc[0:32, :]),
           reads=[lrc], writes=[lrcb])
    for q4 in range(4):
        io = q4 % 2
        srcb = 0 if q4 < 2 else 64
        for j in range(8):
            ctx.op("pe", lambda E, io=io, j=j, srcb=srcb: E.matmul(
                ps_o[:, 0:64], bselb[0:32, (io * 8 + j) * 128:
                                     (io * 8 + j + 1) * 128],
                lrcb[0:32, srcb:srcb + 64], start=True, stop=True),
                reads=[bselb, lrcb], writes=[ps_o])
            ctx.op("dve", lambda E, q4=q4, j=j: E.tensor_copy(
                lwt[:, q4 * 512 + j * 64:q4 * 512 + (j + 1) * 64],
                ps_o[:, 0:64]),
                reads=[ps_o], writes=[lwt])

    for pss in range(4):
        for t3, (fmb, wsrc) in enumerate(((qf, wqT), (kf, wkT), (vf, wvT))):
            ctx.dma("pool", lambda E, wsrc=wsrc: E.dma_start(
                wpa[:, :].rearrange("p (c m) -> p c m", c=NCC),
                wsrc.t.rearrange("(c p) m -> p c m", p=128)),
                reads=[wsrc], writes=[wpa])
            for dchunk in range(NCC):
                for ns2 in range(2):
                    pb = z0 if ns2 == 0 else z1
                    for c in range(NCC):
                        ctx.op("pe", lambda E, dchunk=dchunk, ns2=ns2, c=c,
                               pss=pss, pb=pb: E.matmul(
                            pb[:, :],
                            wpa[:, c * DM + dchunk * 128:
                                c * DM + (dchunk + 1) * 128],
                            hT[:, c * SEQ + pss * 1024 + ns2 * 512:
                               c * SEQ + pss * 1024 + (ns2 + 1) * 512],
                            start=(c == 0), stop=(c == NCC - 1)),
                            reads=[wpa, hT], writes=[pb])
                    if t3 < 2:
                        ctx.op("act", lambda E, dchunk=dchunk, ns2=ns2,
                               fmb=fmb, pb=pb: E.activation(
                            fmb[:, dchunk * 1024 + ns2 * 512:
                                dchunk * 1024 + (ns2 + 1) * 512],
                            pb[:, :], AF.Silu, bias=0.0, scale=1.0),
                            reads=[pb], writes=[fmb])
                    else:
                        ctx.op("dve", lambda E, dchunk=dchunk, ns2=ns2,
                               fmb=fmb, pb=pb: E.tensor_copy(
                            fmb[:, dchunk * 1024 + ns2 * 512:
                                dchunk * 1024 + (ns2 + 1) * 512],
                            pb[:, :]),
                            reads=[pb], writes=[fmb])
        for il in range(16):
            ig = pss * 16 + il
            cb = il * 64
            for st_, stb_ in ((winT, winTb), (woutT, woutTb),
                              (woutD, woutDb)):
                ctx.op("dve", lambda E, st_=st_, stb_=stb_: E.tensor_copy(
                    stb_[:, :], st_[:, :]),
                    reads=[st_], writes=[stb_])
            for j in range(8):
                zp = z0 if j < 4 else z1
                zofs = (j % 4) * 128
                for hh in range(2):
                    hsl = slice(0, 64) if hh == 0 else slice(64, 128)
                    ctx.op("pe", lambda E, j=j, hh=hh, hsl=hsl, zp=zp,
                           zofs=zofs, cb=cb: E.matmul(
                        zp[hsl, zofs:zofs + 64],
                        winTb[hsl, j * 64:(j + 1) * 64],
                        qf[hsl, j * 1024 + cb:j * 1024 + cb + 64],
                        start=True, stop=True),
                        reads=[winTb, qf], writes=[zp])
                    ctx.op("pe", lambda E, j=j, hh=hh, hsl=hsl, zp=zp,
                           zofs=zofs, cb=cb: E.matmul(
                        zp[hsl, zofs + 64:zofs + 128],
                        winTb[hsl, j * 64:(j + 1) * 64],
                        kf[hsl, j * 1024 + cb:j * 1024 + cb + 64],
                        start=True, stop=True),
                        reads=[winTb, kf], writes=[zp])
            ctx.op("act", lambda E: E.activation(
                e_all[:, 0:512], z0[:, :], AF.Exp, bias=0.0, scale=1.0),
                reads=[z0], writes=[e_all])
            ctx.op("act", lambda E: E.activation(
                e_all[:, 512:1024], z1[:, :], AF.Exp, bias=0.0, scale=1.0),
                reads=[z1], writes=[e_all])
            ctx.op("dve", lambda E: E.reduce_sum(
                stat[:, 0:16],
                e_all[:, :].rearrange("p (t m) -> p t m", t=16),
                axis=AX.X),
                reads=[e_all], writes=[stat])
            ctx.op("dve", lambda E: E.reciprocal(stat[:, 0:16],
                                                 stat[:, 0:16]),
                   reads=[stat], writes=[stat])
            ctx.op("dve", lambda E: E.tensor_tensor(
                p_all[:, :].rearrange("p (t m) -> p t m", t=16),
                e_all[:, :].rearrange("p (t m) -> p t m", t=16),
                stat[:, 0:16].unsqueeze(2).broadcast_to((128, 16, 64)),
                ALU.mult),
                reads=[e_all, stat], writes=[p_all])
            for j in range(8):
                for hh in range(2):
                    hsl = slice(0, 64) if hh == 0 else slice(64, 128)
                    ctx.op("pe", lambda E, j=j, hsl=hsl, cb=cb: E.matmul(
                        ps_a[hsl, j * 64:(j + 1) * 64],
                        woutTb[hsl, j * 64:(j + 1) * 64],
                        vf[hsl, j * 1024 + cb:j * 1024 + cb + 64],
                        start=True, stop=True),
                        reads=[woutTb, vf], writes=[ps_a])
                    ctx.op("pe", lambda E, j=j, hsl=hsl: E.matmul(
                        ps_o[hsl, j * 64:(j + 1) * 64],
                        woutDb[hsl, j * 64:(j + 1) * 64],
                        p_all[hsl, j * 128:j * 128 + 64],
                        start=True, stop=True),
                        reads=[woutDb, p_all], writes=[ps_o])
            pk_ap = p_all[:, :].rearrange("p (j t n) -> p j t n",
                                          t=2, n=64)[:, :, 1, :]
            ctx.op("dve", lambda E, pk_ap=pk_ap: E.tensor_tensor(
                t_all[:, :].rearrange("p (j n) -> p j n", n=64),
                pk_ap,
                ps_a[:, :].rearrange("p (j n) -> p j n", n=64),
                ALU.mult),
                reads=[p_all, ps_a], writes=[t_all])
            ctx.op("pe", lambda E: E.matmul(
                ps_s[0:2, :], e2b[:, :], t_all[:, :], start=True, stop=True),
                reads=[e2b, t_all], writes=[ps_s])
            ctx.op("dve", lambda E: E.tensor_copy(sseg[:, :], ps_s[0:2, :]),
                   reads=[ps_s], writes=[sseg])
            ctx.op("pe", lambda E: E.matmul(
                ps_sb[:, :], e2tf[:, :], sseg[:, :], start=True, stop=True),
                reads=[e2tf, sseg], writes=[ps_sb])
            ctx.op("dve", lambda E, pk_ap=pk_ap: E.tensor_tensor(
                u_all[:, :].rearrange("p (j n) -> p j n", n=64),
                pk_ap,
                ps_sb[:, :].rearrange("p (j n) -> p j n", n=64),
                ALU.mult),
                reads=[p_all, ps_sb], writes=[u_all])
            ctx.op("dve", lambda E: E.tensor_sub(
                gz[:, :], u_all[:, :], t_all[:, :]),
                reads=[u_all, t_all], writes=[gz])
            for (srcb, dstb, ofs_fn) in (
                    (kf, ktok, lambda j: j * 1024 + cb),
                    (vf, vtok, lambda j: j * 1024 + cb),
                    (p_all, ptok, lambda j: j * 128 + 64),
                    (gz, gztok, lambda j: j * 64)):
                for j in range(8):
                    ctx.op("pe", lambda E, srcb=srcb, j=j, ofs=ofs_fn(j):
                           E.transpose(pst[0:64, 0:128],
                                       srcb[:, ofs:ofs + 64], ident[:, :]),
                           reads=[srcb, ident], writes=[pst])
                    ctx.op("dve", lambda E, dstb=dstb, j=j: E.tensor_copy(
                        dstb[0:64, j * 128:(j + 1) * 128], pst[0:64, 0:128]),
                        reads=[pst], writes=[dstb])
            for j in range(8):
                for hh in range(2):
                    co = j * 128 + hh * 64
                    go = j * 64
                    rsl = slice(0, 64) if hh == 0 else slice(64, 128)
                    ctx.op("pe", lambda E, co=co, go=go, rsl=rsl: E.matmul(
                        z0[rsl, go:go + 64], ktok[0:64, co:co + 64],
                        gztok[0:64, co:co + 64], start=True, stop=True),
                        reads=[ktok, gztok], writes=[z0])
                    ctx.op("pe", lambda E, co=co, go=go, rsl=rsl: E.matmul(
                        z1[rsl, go:go + 64], vtok[0:64, co:co + 64],
                        ptok[0:64, co:co + 64], start=True, stop=True),
                        reads=[vtok, ptok], writes=[z1])
                    ctx.op("pe", lambda E, co=co, go=go, rsl=rsl: E.matmul(
                        ps_a[rsl, go:go + 64], ptok[0:64, co:co + 64],
                        vtok[0:64, co:co + 64], start=True, stop=True),
                        reads=[ptok, vtok], writes=[ps_a])
            for (st_, gps, qlr, qwd, op1) in (
                    (winT, z0, 0, 2, ALU.subtract),
                    (woutT, z1, 1, 3, ALU.add),
                    (woutD, ps_a, 1, 3, ALU.add)):
                lr_ap = lwt[:, qlr * 512:(qlr + 1) * 512].rearrange(
                    "p (j n) -> p j n", n=64)[:, :, ig:ig + 1].broadcast_to(
                    (128, 8, 64))
                wd_ap = lwt[:, qwd * 512:(qwd + 1) * 512].rearrange(
                    "p (j n) -> p j n", n=64)[:, :, ig:ig + 1].broadcast_to(
                    (128, 8, 64))
                ctx.op("dve", lambda E, gps=gps, lr_ap=lr_ap:
                       E.tensor_tensor(
                           updt[:, :].rearrange("p (j n) -> p j n", n=64),
                           gps[:, :].rearrange("p (j n) -> p j n", n=64),
                           lr_ap, ALU.mult),
                       reads=[gps, lwt], writes=[updt])
                ctx.op("dve", lambda E, st_=st_, wd_ap=wd_ap:
                       E.tensor_tensor(
                           st_[:, :].rearrange("p (j n) -> p j n", n=64),
                           st_[:, :].rearrange("p (j n) -> p j n", n=64),
                           wd_ap, ALU.mult),
                       reads=[st_, lwt], writes=[st_])
                ctx.op("dve", lambda E, st_=st_, op1=op1: E.tensor_tensor(
                    st_[:, :], st_[:, :], updt[:, :], op1),
                    reads=[st_, updt], writes=[st_])
            ctx.op("dve", lambda E: E.tensor_copy(ostage[:, :], ps_o[:, :]),
                   reads=[ps_o], writes=[ostage])
            ctx.dma("act", lambda E, ig=ig: E.dma_start(
                o_d.t[:, :, ig * 64:(ig + 1) * 64].rearrange(
                    "c p m -> p c m"),
                ostage[:, :].rearrange("p (c m) -> p c m", c=NCC)),
                reads=[ostage], writes=[o_d])

    ctx.dma("pool", lambda E: E.dma_start(
        wpa[:, :].rearrange("p (c m) -> p c m", c=NCC),
        woT.t.rearrange("(c p) m -> p c m", p=128)),
        reads=[woT], writes=[wpa])
    for i in range(NT):
        ctx.dma("sp", lambda E, i=i: E.dma_start(
            lntile[:, :].rearrange("p (c m) -> p c m", c=NCC),
            o_d.t[:, :, i * 128:(i + 1) * 128].rearrange("c p m -> p c m")),
            reads=[o_d], writes=[lntile])
        for half in range(2):
            zp = z0 if half == 0 else z1
            for c in range(NCC):
                ctx.op("pe", lambda E, c=c, half=half, zp=zp: E.matmul(
                    zp[:, :], lntile[:, c * 128:(c + 1) * 128],
                    wpa[:, c * DM + half * 512:c * DM + (half + 1) * 512],
                    start=(c == 0), stop=(c == NCC - 1)),
                    reads=[lntile, wpa], writes=[zp])
        ctx.dma("sp", lambda E, i=i: E.dma_start(
            xtile[:, :], xin.t[i * 128:(i + 1) * 128, :]),
            reads=[xin], writes=[xtile])
        for half in range(2):
            zp = z0 if half == 0 else z1
            ctx.op("dve", lambda E, half=half, zp=zp: E.tensor_copy(
                tmp2[:, half * 512:(half + 1) * 512], zp[:, :]),
                reads=[zp], writes=[tmp2])
        emit_ln(ctx, tmp1, np.s_[:, 0:DM], tmp2, np.s_[:, 0:DM], stat,
                lntile, np.s_[:, 0:DM], DM, res=xtile, res_sl=np.s_[:, :],
                eps=epsc)
        ctx.dma("act", lambda E, i=i: E.dma_start(
            xout.t[i * 128:(i + 1) * 128, :], tmp1[:, 0:DM]),
            reads=[tmp1], writes=[xout])




def host_consts_b1():
    c = np.zeros((128, 512), dtype=np.float32)
    c[:, 0:128] = np.eye(128, dtype=np.float32)
    c[0:64, 128] = 1.0
    c[64:128, 129] = 1.0
    c[0, 130:194] = 1.0
    c[1, 194:258] = 1.0
    return c


def host_bsel_b1():
    b = np.zeros((32, 16 * 128), dtype=np.float32)
    for io in range(2):
        for j in range(8):
            for p in range(128):
                h = 2 * j + (1 if p >= 64 else 0)
                b[io * 16 + h, (io * 8 + j) * 128 + p] = 1.0
    return b


def host_winit_b1(W_in_init, W_out_init):
    wi = W_in_init[0]
    wo = W_out_init[0]
    winT = np.zeros((128, 512), dtype=np.float32)
    woutT = np.zeros((128, 512), dtype=np.float32)
    woutD = np.zeros((128, 512), dtype=np.float32)
    for p in range(128):
        for j in range(8):
            h = 2 * j + (1 if p >= 64 else 0)
            winT[p, j * 64:(j + 1) * 64] = wi[:, h, p % 64]
            woutT[p, j * 64:(j + 1) * 64] = wo[:, h, p % 64]
            woutD[p, j * 64:(j + 1) * 64] = wo[p % 64, h, :]
    return winT, woutT, woutD


ARENA_COLS = 97280


def com_alloc(ctx):
    from types import SimpleNamespace
    com = SimpleNamespace()
    com.xtile = ctx.sb("xtile", [128, DM], F32)
    com.lntile = ctx.sb("lntile", [128, DM], BF16)
    com.stage = ctx.sb("stage", [128, DM], BF16)
    com.tmp1 = ctx.sb("tmp1", [128, DM], F32)
    com.tmp2 = ctx.sb("tmp2", [128, DM], F32)
    com.stat = ctx.sb("stat", [128, 16], F32)
    com.epsc = ctx.sb("epsc", [128, 2], F32)
    com.ident = ctx.sb("ident", [128, 128], BF16)
    com.onesb = ctx.sb("onesb", [1, 512], BF16)
    com.onesf = ctx.sb("onesf", [1, 128], F32)
    com.pp = [ctx.ps(f"pp{i}", [128, 512], F32) for i in range(6)]
    com.pst = ctx.ps("pst", [128, 1024], BF16)
    return com


def build_solo(emitter, **kw):
    nc = bass.Bass("TRN2", target_bir_lowering=False)
    ctx = Ctx(nc)
    ctx.arena_init(ARENA_COLS)
    com = com_alloc(ctx)
    xin = ctx.din("xin", [SEQ, DM])
    xout = ctx.dout("xout", [SEQ, DM])
    emitter(ctx, com, xin, xout, **kw)
    ctx.barrier_out()
    ctx.finalize()
    return nc


def build_fused(dbg=False):
    nc = bass.Bass("TRN2", target_bir_lowering=False)
    ctx = Ctx(nc)
    ctx.arena_init(ARENA_COLS)
    com = com_alloc(ctx)
    mk = ctx.dout if dbg else ctx.dram
    x0 = ctx.din("x0", [SEQ, DM])
    x1_d = mk("x1_d", [SEQ, DM], F32)
    x2_d = mk("x2_d", [SEQ, DM], F32)
    x3_d = mk("x3_d", [SEQ, DM], F32)
    xout = ctx.dout("xout", [SEQ, DM])
    emit_b1(ctx, com, x0, x1_d)
    ctx.barrier()
    ctx.arena_reset()
    emit_b2(ctx, com, x1_d, x2_d)
    ctx.barrier()
    ctx.arena_reset()
    emit_b3(ctx, com, x2_d, x3_d)
    ctx.barrier()
    ctx.arena_reset()
    emit_b4(ctx, com, x3_d, xout)
    ctx.barrier_out()
    ctx.finalize()
    return nc


def prep_b1(z):
    cstb = host_consts_b1()
    bsel = host_bsel_b1()
    winT, woutT, woutD = host_winit_b1(z["L0_attn_W_in_init"],
                                       z["L0_attn_W_out_init"])
    t = lambda a: np.ascontiguousarray(np.asarray(a, dtype=np.float32).T)
    return {"wqT": t(z["L0_attn_Wq"]), "wkT": t(z["L0_attn_Wk"]),
            "wvT": t(z["L0_attn_Wv"]), "wlT": t(z["L0_attn_Wlr"]),
            "wbT": t(z["L0_attn_Wbeta"]), "woT": t(z["L0_attn_Wo"]),
            "winT_h": winT, "woutT_h": woutT, "woutD_h": woutD,
            "bsel_in": bsel, "cstb": cstb}


def prep_b2(z):
    cos, ss = host_rope_b2()
    cst2 = host_consts_b2()
    f = lambda a: np.ascontiguousarray(np.asarray(a, dtype=np.float32))
    brow = np.concatenate([f(z["L0_ffn_bq"]), f(z["L0_ffn_bk"]),
                           f(z["L0_ffn_bv"]),
                           f(z["L0_ffn_bo"])]).reshape(1, 4 * DM)
    grow = np.concatenate([f(z["L0_ffn_g1"]), f(z["L0_ffn_be1"]),
                           f(z["L0_ffn_g2"]),
                           f(z["L0_ffn_be2"])]).reshape(1, 4 * DM)
    return {"wq": f(z["L0_ffn_Wq"]), "wk": f(z["L0_ffn_Wk"]),
            "wv": f(z["L0_ffn_Wv"]), "wo": f(z["L0_ffn_Wo"]),
            "brow_in": brow, "wg_in": f(z["L0_ffn_Wg"]),
            "bg_in": f(z["L0_ffn_bg"]).reshape(1, 8),
            "w1_in": f(z["L0_ffn_W1"]), "b1_in": f(z["L0_ffn_b1"]),
            "w2_in": f(z["L0_ffn_W2"]), "b2_in": f(z["L0_ffn_b2"]),
            "grow_in": grow, "ropec": cos, "ropes": ss, "cst2": cst2}


def prep_b3(z):
    qtw, ktw, vtw, wtw = prep_b3_weights(
        np.asarray(z["L1_attn_Q"], dtype=np.float32),
        np.asarray(z["L1_attn_K"], dtype=np.float32),
        np.asarray(z["L1_attn_V"], dtype=np.float32),
        np.asarray(z["L1_attn_W"], dtype=np.float32))
    return {"qtw": qtw, "ktw": ktw, "vtw": vtw, "wtw": wtw,
            "consts": host_consts_b3()}


def prep_b4(z):
    cst = host_consts_b2()
    tab = host_tabs_b4()
    msk = host_masks_b4()
    f = lambda a: np.ascontiguousarray(np.asarray(a, dtype=np.float32))
    W1 = f(z["L1_ffn_W1"])
    b1 = f(z["L1_ffn_b1"])
    W2 = f(z["L1_ffn_W2"])
    FF = 2730
    w1a = np.zeros((6, DM, B4_FFP), dtype=np.float32)
    w1g = np.zeros((6, DM, B4_FFP), dtype=np.float32)
    w1a[:, :, 0:FF] = W1[:, :, 0:FF]
    w1g[:, :, 0:FF] = W1[:, :, FF:2 * FF]
    b1a = np.zeros((6, B4_FFP), dtype=np.float32)
    b1g = np.zeros((6, B4_FFP), dtype=np.float32)
    b1a[:, 0:FF] = b1[:, 0:FF]
    b1g[:, 0:FF] = b1[:, FF:2 * FF]
    w2p = np.zeros((6, B4_FFP, DM), dtype=np.float32)
    w2p[:, 0:FF, :] = W2
    return {"wqkv_in": f(z["L1_ffn_Wqkv"]),
            "qs_in": f(z["L1_ffn_q_scale"]), "ks_in": f(z["L1_ffn_k_scale"]),
            "wo_in": f(z["L1_ffn_Wo"]),
            "lng_in": f(z["L1_ffn_ln_g"]), "lnb_in": f(z["L1_ffn_ln_b"]),
            "w1a_in": w1a, "w1g_in": w1g, "b1a_in": b1a, "b1g_in": b1g,
            "w2p_in": w2p, "b2f_in": f(z["L1_ffn_b2"]),
            "tab_in": tab, "msk_in": msk, "cst": cst}


def _run_solo(emitter, prep, x, z, **kw):
    nc = build_solo(emitter, **kw)
    base = prep(z)
    in_maps = [dict(base, xin=np.ascontiguousarray(x[b]))
               for b in range(x.shape[0])]
    res = run_bass_kernel_spmd(nc, in_maps,
                               core_ids=list(range(x.shape[0])))
    if kw.get("dbg"):
        return res.results
    return np.stack([r["xout"] for r in res.results], axis=0)


def run_b1(x0, z):
    return _run_solo(emit_b1, prep_b1, x0, z)


def run_b2(x1, z, dbg=False, nexp=8):
    return _run_solo(emit_b2, prep_b2, x1, z, dbg=dbg, nexp=nexp)


def run_b3(x2, Q, K, V, W):
    z = {"L1_attn_Q": Q, "L1_attn_K": K, "L1_attn_V": V, "L1_attn_W": W}
    return _run_solo(emit_b3, prep_b3, x2, z)


def run_b4(x3, z):
    return _run_solo(emit_b4, prep_b4, x3, z)


def kernel(**inputs):
    z = inputs
    x0 = np.ascontiguousarray(np.asarray(z["L0_attn_x"], dtype=np.float32))
    nc = build_fused()
    base = {}
    base.update(prep_b1(z))
    base.update(prep_b2(z))
    base.update(prep_b3(z))
    base.update(prep_b4(z))
    in_maps = [dict(base, x0=np.ascontiguousarray(x0[b]))
               for b in range(8)]
    res = run_bass_kernel_spmd(nc, in_maps, core_ids=list(range(8)))
    return np.stack([r["xout"] for r in res.results],
                    axis=0).astype(np.float32)


# version 07
# speedup vs baseline: 1.0565x; 1.0071x over previous
import numpy as np
import concourse.bass as bass
import concourse.mybir as mybir
from concourse.bass_utils import run_bass_kernel_spmd
from contextlib import ExitStack

F32 = mybir.dt.float32
BF16 = mybir.dt.bfloat16
AF = mybir.ActivationFunctionType
ALU = mybir.AluOpType
AX = mybir.AxisListType

SEQ = 4096
DM = 1024
NT = SEQ // 128
NCC = DM // 128
B4_FFP = 2816
B4_NFC = 22


ENGS = ("pe", "act", "dve", "pool", "sp")
DMA_SEM = {"sp": "spd", "pool": "poold", "act": "actd"}


class Buf:
    __slots__ = ("name", "t", "w", "r")

    def __init__(self, name, t):
        self.name = name
        self.t = t
        self.w = None
        self.r = {}

    def __getitem__(self, key):
        return self.t[key]


class Ctx:
    def __init__(self, nc):
        self.nc = nc
        self.stack = ExitStack()
        self.q = {e: [] for e in ENGS}
        self.cnt = {s: 0 for s in ("pe", "act", "dve", "pool", "sp",
                                   "spd", "poold", "actd")}
        self.seen = {e: {} for e in ENGS}
        self.sems = {}
        for s in self.cnt:
            self.sems[s] = self.stack.enter_context(nc.semaphore("sem_" + s))

    def sb(self, name, shape, dtype):
        t = self.stack.enter_context(self.nc.sbuf_tensor(name, shape, dtype))
        return Buf(name, t)

    def arena_init(self, cols):
        self.ar = self.stack.enter_context(
            self.nc.sbuf_tensor("ARENA", [128, cols], BF16))
        self.acur = 0
        self.amax = 0

    def arena_reset(self):
        self.acur = 0

    def vsb(self, name, shape, dtype):
        p, c = shape
        cols = c if dtype == BF16 else 2 * c
        acols = (cols + 15) // 16 * 16
        off = self.acur
        self.acur += acols
        self.amax = max(self.amax, self.acur)
        assert self.acur <= self.ar.shape[1], f"arena overflow at {name}"
        ap = self.ar[0:p, off:off + cols]
        if dtype != BF16:
            ap = ap.bitcast(dtype)
        return Buf(name, ap)

    def barrier(self):
        for e in ENGS:
            for s, v in self.cnt.items():
                if v:
                    self._wait(e, s, v)

    def ps(self, name, shape, dtype=F32):
        t = self.stack.enter_context(self.nc.psum_tensor(name, shape, dtype))
        return Buf(name, t)

    def dram(self, name, shape, dtype):
        t = self.nc.dram_tensor(name, shape, dtype, kind="Internal")
        return Buf(name, t.ap())

    def din(self, name, shape, dtype=F32):
        t = self.nc.dram_tensor(name, shape, dtype, kind="ExternalInput")
        return Buf(name, t.ap())

    def dout(self, name, shape, dtype=F32):
        t = self.nc.dram_tensor(name, shape, dtype, kind="ExternalOutput")
        return Buf(name, t.ap())

    def _wait(self, eng, sem, val):
        if eng == "pe" and sem == "pe":
            return
        if sem == eng and self.cnt[eng] - val >= 1:
            return
        if self.seen[eng].get(sem, 0) >= val:
            return
        self.q[eng].append(
            lambda E, s=sem, v=val: E.wait_ge(self.sems[s], v))
        self.seen[eng][sem] = val

    def _dval(self, sem, val):
        return self.cnt[sem] if sem.endswith("d") else val

    def _deps(self, eng, reads, writes):
        for b in reads:
            if b.w is not None:
                self._wait(eng, b.w[0], self._dval(b.w[0], b.w[1]))
        for b in writes:
            if b.w is not None:
                self._wait(eng, b.w[0], self._dval(b.w[0], b.w[1]))
            for sem, val in b.r.items():
                self._wait(eng, sem, self._dval(sem, val))

    def op(self, eng, fn, reads=(), writes=()):
        self._deps(eng, reads, writes)
        self.cnt[eng] += 1
        v = self.cnt[eng]
        self.q[eng].append(
            lambda E, f=fn, s=eng: f(E).then_inc(self.sems[s], 1))
        tok = (eng, v)
        for b in reads:
            b.r[eng] = v
        for b in writes:
            b.w = tok
            b.r = {}
        return tok

    def dma(self, queue, fn, reads=(), writes=()):
        self._deps(queue, reads, writes)
        dsem = DMA_SEM[queue]
        self.cnt[dsem] += 16
        v = self.cnt[dsem]
        self.q[queue].append(
            lambda E, f=fn, s=dsem: f(E).then_inc(self.sems[s], 16))
        tok = (dsem, v)
        for b in reads:
            b.r[dsem] = v
        for b in writes:
            b.w = tok
            b.r = {}
        return tok

    def barrier_out(self):
        for s in ("spd", "poold", "actd"):
            if self.cnt[s]:
                self.q["sp"].append(
                    lambda E, ss=s, v=self.cnt[s]: E.wait_ge(self.sems[ss], v))

    def finalize(self):
        nc = self.nc
        qs = self.q
        with nc.Block() as blk:
            @blk.tensor
            def _(e):
                for f in qs["pe"]:
                    f(e)

            @blk.scalar
            def _(e):
                for f in qs["act"]:
                    f(e)

            @blk.vector
            def _(e):
                for f in qs["dve"]:
                    f(e)

            @blk.gpsimd
            def _(e):
                for f in qs["pool"]:
                    f(e)

            @blk.sync
            def _(e):
                for f in qs["sp"]:
                    f(e)
        self.stack.close()



def emit_ln(ctx, out, out_sl, x, x_sl, stat, sqbuf, sq_sl, D,
            res=None, res_sl=None, eps=None):
    xap = x[x_sl]
    sqap = sqbuf[sq_sl]
    ctx.op("dve", lambda E: E.reduce_sum(stat[:, 0:1], xap, axis=AX.X),
           reads=[x], writes=[stat])
    ctx.op("act", lambda E: E.activation(sqap, xap, AF.Square,
                                         bias=0.0, scale=1.0),
           reads=[x], writes=[sqbuf])
    ctx.op("dve", lambda E: E.reduce_sum(stat[:, 1:2], sqap, axis=AX.X),
           reads=[sqbuf], writes=[stat])
    ctx.op("dve", lambda E: E.tensor_scalar_mul(stat[:, 2:3], stat[:, 0:1],
                                                1.0 / D),
           reads=[stat], writes=[stat])
    ctx.op("dve", lambda E: E.tensor_tensor(stat[:, 3:4], stat[:, 2:3],
                                            stat[:, 2:3], ALU.mult),
           reads=[stat], writes=[stat])
    ctx.op("dve", lambda E: E.scalar_tensor_tensor(
        stat[:, 4:5], stat[:, 1:2], 1.0 / D, stat[:, 3:4],
        ALU.mult, ALU.subtract),
        reads=[stat], writes=[stat])
    ctx.op("act", lambda E: E.activation(stat[:, 5:6], stat[:, 4:5], AF.Sqrt,
                                         bias=eps[:, 0:1], scale=1.0),
           reads=[stat, eps], writes=[stat])
    ctx.op("dve", lambda E: E.reciprocal(stat[:, 6:7], stat[:, 5:6]),
           reads=[stat], writes=[stat])
    ctx.op("dve", lambda E: E.tensor_scalar_mul(stat[:, 7:8], stat[:, 2:3],
                                                -1.0),
           reads=[stat], writes=[stat])
    oap = out[out_sl]
    ctx.op("dve", lambda E: E.tensor_scalar(oap, xap, stat[:, 7:8],
                                            stat[:, 6:7], ALU.add, ALU.mult),
           reads=[x, stat], writes=[out])
    if res is not None:
        ctx.op("dve", lambda E: E.tensor_tensor(oap, oap, res[res_sl],
                                                ALU.add),
               reads=[out, res], writes=[out])



def emit_b3(ctx, com, xin, xout):
    nc = ctx.nc
    qtw = ctx.din("qtw", [8, DM, 128])
    ktw = ctx.din("ktw", [8, DM, 128])
    vtw = ctx.din("vtw", [8, DM, 128])
    wtw = ctx.din("wtw", [DM, DM])
    consts = ctx.din("consts", [128, 128 + 128 + 4 * 512])

    yfm_d = ctx.dram("b3yfm_d", [NCC, 128, SEQ], BF16)

    xt2 = ctx.vsb("xt2", [128, DM], F32)
    xtile = [com.xtile, xt2]
    lntile, stat, ident = com.lntile, com.stat, com.ident
    epsc, onesc = com.epsc, com.onesf
    pp, pst = com.pp, com.pst

    ht = ctx.vsb("ht", [128, NCC * SEQ], BF16)
    sq = ctx.vsb("sq", [128, DM], F32)
    tri = ctx.vsb("tri", [128, 4 * 512], BF16)
    qw = ctx.vsb("qw", [128, NCC * 128], BF16)
    kw = ctx.vsb("kw", [128, NCC * 128], BF16)
    vw = ctx.vsb("vw", [128, NCC * 128], BF16)
    wt = ctx.vsb("wt", [128, NCC * DM], BF16)
    qt = ctx.vsb("qt", [128, SEQ], BF16)
    kt = ctx.vsb("kt", [128, SEQ], BF16)
    vfm = ctx.vsb("vfm", [128, SEQ], BF16)
    va = ctx.vsb("va", [128, NT * 65], BF16)
    vb = ctx.vsb("vb", [128, NT * 65], BF16)
    pt2 = [ctx.vsb(f"pt{i}", [128, 512], BF16) for i in range(2)]
    recip = ctx.vsb("recip", [1, 512], F32)
    ybc = ctx.vsb("ybc", [128, 512], F32)
    ystage = ctx.vsb("ystage", [128, 512], BF16)
    yfmt = [ctx.vsb(f"yfmt{i}", [128, DM], BF16) for i in range(2)]
    outln = ctx.vsb("outln", [128, DM], F32)

    ctx.dma("pool", lambda E: E.dma_start(ident[:, :], consts[:, 0:128]),
            reads=[consts], writes=[ident])
    ctx.dma("sp", lambda E: E.dma_start(onesc[:, :], consts[0:1, 128:256]),
            reads=[consts], writes=[onesc])
    ctx.dma("pool", lambda E: E.dma_start(tri[:, :],
                                          consts[:, 256:256 + 4 * 512]),
            reads=[consts], writes=[tri])
    ctx.dma("pool", lambda E: E.dma_start(
        wt[:, :].rearrange("p (c m) -> p c m", c=NCC),
        wtw.t.rearrange("(c p) m -> p c m", p=128)),
        reads=[wtw], writes=[wt])

    ctx.op("dve", lambda E: E.memset(epsc[:, :], 1e-5),
           reads=[], writes=[epsc])

    for i in range(NT):
        xt = xtile[i % 2]
        ctx.dma("sp", lambda E, i=i, xt=xt: E.dma_start(
            xt[:, :], xin.t[i * 128:(i + 1) * 128, :]),
            reads=[xin], writes=[xt])
        emit_ln(ctx, lntile, np.s_[:, :], xt, np.s_[:, :], stat, sq,
                np.s_[:, :], DM, eps=epsc)
        for c in range(NCC):
            ctx.op("pe", lambda E, c=c: E.transpose(
                pst[:, 0:128], lntile[:, c * 128:(c + 1) * 128], ident[:, :]),
                reads=[lntile, ident], writes=[pst])
            ctx.op("dve", lambda E, c=c, i=i: E.tensor_copy(
                ht[:, c * SEQ + i * 128: c * SEQ + (i + 1) * 128],
                pst[:, 0:128]),
                reads=[pst], writes=[ht])

    SC_QK = float(np.sqrt(64.0 / 1024.0) / 8.0)
    SC_V = float(np.sqrt(64.0 / 1024.0))
    for pair in range(8):
        for wbuf, src in ((qw, qtw), (kw, ktw), (vw, vtw)):
            ctx.dma("pool", lambda E, wbuf=wbuf, src=src, pair=pair:
                    E.dma_start(
                        wbuf[:, :].rearrange("p (c m) -> p c m", c=NCC),
                        src.t[pair].rearrange("(c p) m -> p c m", p=128)),
                    reads=[src], writes=[wbuf])
        for tens, wbuf, scl in ((qt, qw, SC_QK), (kt, kw, SC_QK),
                                (vfm, vw, SC_V)):
            for ns in range(8):
                pb = pp[ns % 2]
                for c in range(NCC):
                    ctx.op("pe", lambda E, c=c, ns=ns, wbuf=wbuf, pb=pb:
                           E.matmul(pb[:, :],
                                    wbuf[:, c * 128:(c + 1) * 128],
                                    ht[:, c * SEQ + ns * 512:
                                       c * SEQ + (ns + 1) * 512],
                                    start=(c == 0), stop=(c == NCC - 1)),
                           reads=[wbuf, ht], writes=[pb])
                ctx.op("dve", lambda E, ns=ns, tens=tens, scl=scl, pb=pb:
                       E.tensor_scalar_mul(
                           tens[:, ns * 512:(ns + 1) * 512], pb[:, :], scl),
                       reads=[pb], writes=[tens])
        for ktn in range(NT):
            ctx.op("pe", lambda E, ktn=ktn: E.transpose(
                pst[:, 0:128], vfm[:, ktn * 128:(ktn + 1) * 128], ident[:, :]),
                reads=[vfm, ident], writes=[pst])
            ctx.op("dve", lambda E, ktn=ktn: E.tensor_copy(
                va[:, ktn * 65:ktn * 65 + 64], pst[:, 0:64]),
                reads=[pst], writes=[va])
            ctx.op("dve", lambda E, ktn=ktn: E.tensor_copy(
                vb[:, ktn * 65:ktn * 65 + 64], pst[:, 64:128]),
                reads=[pst], writes=[vb])
        ctx.op("dve", lambda E: E.memset(
            va[:, :].rearrange("p (n c) -> p n c", c=65)[:, :, 64:65], 1.0),
            reads=[], writes=[va])
        ctx.op("dve", lambda E: E.memset(
            vb[:, :].rearrange("p (n c) -> p n c", c=65)[:, :, 64:65], 1.0),
            reads=[], writes=[vb])

        for hh in range(2):
            hsl = slice(0, 64) if hh == 0 else slice(64, 128)
            vx = va if hh == 0 else vb
            for qs in range(8):
                outp = pp[2]
                ktmax = 4 * qs + 3
                for ktn in range(ktmax + 1):
                    sb_ = pp[ktn % 2]
                    pt = pt2[ktn % 2]
                    ctx.op("pe", lambda E, ktn=ktn, qs=qs, hsl=hsl, sb_=sb_:
                           E.matmul(sb_[:, :],
                                    kt[hsl, ktn * 128:(ktn + 1) * 128],
                                    qt[hsl, qs * 512:(qs + 1) * 512],
                                    start=True, stop=True),
                           reads=[kt, qt], writes=[sb_])
                    band_j = ktn - 4 * qs
                    if band_j < 0:
                        ctx.op("act", lambda E, sb_=sb_, pt=pt: E.activation(
                            pt[:, :], sb_[:, :], AF.Exp, bias=0.0, scale=1.0),
                            reads=[sb_], writes=[pt])
                    else:
                        ctx.op("act", lambda E, sb_=sb_: E.activation(
                            sq[:, 0:512], sb_[:, :], AF.Exp, bias=0.0,
                            scale=1.0),
                            reads=[sb_], writes=[sq])
                        ctx.op("dve", lambda E, j=band_j, pt=pt:
                               E.tensor_tensor(
                                   pt[:, :], sq[:, 0:512],
                                   tri[:, j * 512:(j + 1) * 512], ALU.mult),
                               reads=[sq, tri], writes=[pt])
                    ctx.op("pe", lambda E, ktn=ktn, vx=vx, outp=outp,
                           pt=pt, st=(ktn == 0), sp_=(ktn == ktmax):
                           E.matmul(outp[0:65, :],
                                    vx[:, ktn * 65:(ktn + 1) * 65],
                                    pt[:, :], start=st, stop=sp_),
                           reads=[vx, pt], writes=[outp])
                ctx.op("dve", lambda E, outp=outp: E.reciprocal(
                    recip[:, :], outp[64:65, :]),
                    reads=[outp], writes=[recip])
                bc = pp[3]
                ctx.op("pe", lambda E, bc=bc: E.matmul(
                    bc[:, :], onesc[:, :], recip[:, :], start=True, stop=True),
                    reads=[onesc, recip], writes=[bc])
                ctx.op("dve", lambda E, bc=bc: E.tensor_copy(
                    ybc[:, :], bc[:, :]), reads=[bc], writes=[ybc])
                ctx.op("dve", lambda E, outp=outp: E.tensor_tensor(
                    ystage[0:64, :], outp[0:64, :], ybc[0:64, :], ALU.mult),
                    reads=[outp, ybc], writes=[ystage])
                ctx.dma("act", lambda E, pair=pair, hh=hh, qs=qs: E.dma_start(
                    yfm_d.t[pair, hh * 64:(hh + 1) * 64,
                            qs * 512:(qs + 1) * 512],
                    ystage[0:64, :]),
                    reads=[ystage], writes=[yfm_d])

    WSC = float(np.sqrt(1024.0 / 64.0) / 16.0)
    for i in range(NT):
        yb = yfmt[i % 2]
        ctx.dma("sp", lambda E, i=i, yb=yb: E.dma_start(
            yb[:, :].rearrange("p (c m) -> p c m", c=NCC),
            yfm_d.t[:, :, i * 128:(i + 1) * 128].rearrange(
                "c p m -> p c m")),
            reads=[yfm_d], writes=[yb])
        for half in range(2):
            zp = pp[4 + half]
            for c in range(NCC):
                ctx.op("pe", lambda E, c=c, half=half, zp=zp, yb=yb:
                       E.matmul(zp[:, :],
                                yb[:, c * 128:(c + 1) * 128],
                                wt[:, c * DM + half * 512:
                                   c * DM + (half + 1) * 512],
                                start=(c == 0), stop=(c == NCC - 1)),
                       reads=[yb, wt], writes=[zp])
        xt = xtile[i % 2]
        ctx.dma("sp", lambda E, i=i, xt=xt: E.dma_start(
            xt[:, :], xin.t[i * 128:(i + 1) * 128, :]),
            reads=[xin], writes=[xt])
        for half in range(2):
            zp = pp[4 + half]
            ctx.op("dve", lambda E, half=half, zp=zp: E.tensor_scalar_mul(
                sq[:, half * 512:(half + 1) * 512], zp[:, :], WSC),
                reads=[zp], writes=[sq])
        emit_ln(ctx, outln, np.s_[:, 0:DM], sq, np.s_[:, 0:DM], stat,
                outln, np.s_[:, 0:DM], DM, res=xt, res_sl=np.s_[:, :],
                eps=epsc)
        ctx.dma("act", lambda E, i=i: E.dma_start(
            xout.t[i * 128:(i + 1) * 128, :], outln[:, 0:DM]),
            reads=[outln], writes=[xout])




def host_consts_b3():
    c = np.zeros((128, 128 + 128 + 4 * 512), dtype=np.float32)
    c[:, 0:128] = np.eye(128, dtype=np.float32)
    c[0, 128:256] = 1.0
    for j in range(4):
        m = np.zeros((128, 512), dtype=np.float32)
        for sub in range(4):
            if sub > j:
                m[:, sub * 128:(sub + 1) * 128] = 1.0
            elif sub == j:
                m[:, sub * 128:(sub + 1) * 128] = np.tril(
                    np.ones((128, 128), dtype=np.float32)).T
        c[:, 256 + j * 512:256 + (j + 1) * 512] = m
    return c


def prep_b3_weights(Q, K, V, W):
    def pairs(q):
        t = np.ascontiguousarray(np.transpose(q, (2, 1, 0)))
        t = t.reshape(8, 2, DM, 64).transpose(0, 2, 1, 3).reshape(8, DM, 128)
        return np.ascontiguousarray(t)
    wtw = np.ascontiguousarray(np.transpose(W, (2, 1, 0)).reshape(DM, DM))
    return pairs(Q), pairs(K), pairs(V), wtw


def emit_b2(ctx, com, xin, xout, dbg=False, nexp=8):
    nc = ctx.nc
    wq = ctx.din("wq", [DM, DM])
    wk = ctx.din("wk", [DM, DM])
    wv = ctx.din("wv", [DM, DM])
    wo = ctx.din("wo", [DM, DM])
    brow_in = ctx.din("brow_in", [1, 4 * DM])
    wg_in = ctx.din("wg_in", [DM, 8])
    bg_in = ctx.din("bg_in", [1, 8])
    w1_in = ctx.din("w1_in", [8, DM, 4096])
    b1_in = ctx.din("b1_in", [8, 4096])
    w2_in = ctx.din("w2_in", [8, 4096, DM])
    b2_in = ctx.din("b2_in", [8, DM])
    grow_in = ctx.din("grow_in", [1, 4 * DM])
    ropec = ctx.din("ropec", [SEQ, 64])
    ropes = ctx.din("ropes", [SEQ, 64])
    cst2 = ctx.din("cst2", [128, 768])

    mk = ctx.dout if dbg else ctx.dram
    h_d = mk("b2h_d", [SEQ, DM], F32)
    qT_d = mk("b2qT_d", [NCC, 128, SEQ], BF16)
    kT_d = mk("b2kT_d", [NCC, 128, SEQ], BF16)
    yfm_d = mk("b2yfm_d", [NCC, 128, SEQ], BF16)
    x1p_d = mk("b2x1p_d", [SEQ, DM], F32)
    x1pT_d = mk("b2x1pT_d", [NCC, 128, SEQ], BF16)
    moe_d = mk("b2moe_d", [SEQ, DM], F32)

    xtile, lntile, stage = com.xtile, com.lntile, com.stage
    tmp1, tmp2, stat, epsc = com.tmp1, com.tmp2, com.stat, com.epsc
    ident, onesb, onesf = com.ident, com.onesb, com.onesf
    pp, pst = com.pp, com.pst

    big0 = ctx.vsb("big0", [128, NCC * SEQ], BF16)
    big1 = ctx.vsb("big1", [128, NT * 1040], BF16)
    ar0 = ctx.vsb("ar0", [128, 2 * NCC * DM], BF16)
    argb = ctx.vsb("argb", [128, 2 * NT * 64], BF16)
    rowbuf = ctx.vsb("rowbuf", [65, 4608], BF16)
    wgw = ctx.vsb("wgw", [128, NCC * 8], BF16)
    pt2 = [ctx.vsb(f"pt{i}", [128, 512], BF16) for i in range(2)]
    ystage = ctx.vsb("ystage", [128, 512], BF16)
    comb = ctx.vsb("comb", [128, NT * 8], F32)
    mtile = ctx.vsb("mtile", [128, DM], F32)

    qhalf = big0
    gbufv = argb.t[:, :].bitcast(F32)

    def wp(i3, sl0, sl1):
        return ar0[:, i3 * NCC * DM + sl0:i3 * NCC * DM + sl1]

    ctx.dma("pool", lambda E: E.dma_start(ident[:, :], cst2[:, 0:128]),
            reads=[cst2], writes=[ident])
    ctx.dma("pool", lambda E: E.dma_start(onesb[:, :], cst2[0:1, 128:640]),
            reads=[cst2], writes=[onesb])
    ctx.dma("sp", lambda E: E.dma_start(onesf[:, :], cst2[0:1, 128:256]),
            reads=[cst2], writes=[onesf])
    ctx.dma("pool", lambda E: E.dma_start(rowbuf[0:1, 0:4096], brow_in.t),
            reads=[brow_in], writes=[rowbuf])
    ctx.dma("pool", lambda E: E.dma_start(rowbuf[0:1, 4096:4104], bg_in.t),
            reads=[bg_in], writes=[rowbuf])
    ctx.op("dve", lambda E: E.memset(rowbuf[32:33, 4096:4608], 1.0),
           reads=[], writes=[rowbuf])
    ctx.op("dve", lambda E: E.memset(rowbuf[64:65, 4096:4224], 1.0),
           reads=[], writes=[rowbuf])
    ctx.dma("pool", lambda E: E.dma_start(
        wgw[:, :].rearrange("p (c m) -> p c m", c=NCC),
        wg_in.t.rearrange("(c p) m -> p c m", p=128)),
        reads=[wg_in], writes=[wgw])
    for i3, w in enumerate((wq, wk)):
        ctx.dma("pool", lambda E, i3=i3, w=w: E.dma_start(
            ar0[:, i3 * NCC * DM:(i3 + 1) * NCC * DM].rearrange(
                "p (c m) -> p c m", c=NCC),
            w.t.rearrange("(c p) m -> p c m", p=128)),
            reads=[w], writes=[ar0])
    ctx.dma("pool", lambda E: E.dma_start(
        argb[:, 0:NT * 64].rearrange("p (t m) -> p t m", t=NT),
        ropec.t.rearrange("(t p) m -> p t m", p=128)),
        reads=[ropec], writes=[argb])
    ctx.dma("pool", lambda E: E.dma_start(
        argb[:, NT * 64:2 * NT * 64].rearrange("p (t m) -> p t m", t=NT),
        ropes.t.rearrange("(t p) m -> p t m", p=128)),
        reads=[ropes], writes=[argb])
    ctx.op("dve", lambda E: E.memset(epsc[:, 0:1], 1e-5),
           reads=[], writes=[epsc])
    ctx.op("dve", lambda E: E.memset(
        big1[:, :].rearrange("p (n c) -> p n c", c=65)[:, :, 64:65], 1.0),
        reads=[], writes=[big1])

    for i in range(NT):
        ctx.dma("sp", lambda E, i=i: E.dma_start(
            xtile[:, :], xin.t[i * 128:(i + 1) * 128, :]),
            reads=[xin], writes=[xtile])
        emit_ln(ctx, tmp1, np.s_[:, :], xtile, np.s_[:, :], stat, tmp2,
                np.s_[:, :], DM, eps=epsc)
        ctx.dma("act", lambda E, i=i: E.dma_start(
            h_d.t[i * 128:(i + 1) * 128, :], tmp1[:, :]),
            reads=[tmp1], writes=[h_d])
        ctx.op("dve", lambda E: E.tensor_copy(lntile[:, :], tmp1[:, :]),
               reads=[tmp1], writes=[lntile])
        for c in range(NCC):
            ctx.op("pe", lambda E, c=c: E.transpose(
                pst[:, 0:128], lntile[:, c * 128:(c + 1) * 128], ident[:, :]),
                reads=[lntile, ident], writes=[pst])
            ctx.op("dve", lambda E, c=c, i=i: E.tensor_copy(
                big0[:, c * SEQ + i * 128: c * SEQ + (i + 1) * 128],
                pst[:, 0:128]),
                reads=[pst], writes=[big0])

    for i in range(NT):
        for t3 in range(2):
            for half in range(2):
                pb = pp[t3 * 2 + half]
                ctx.op("pe", lambda E, t3=t3, half=half, pb=pb: E.matmul(
                    pb[:, :], onesb[0:1, 0:128],
                    rowbuf[0:1, t3 * DM + half * 512:
                           t3 * DM + (half + 1) * 512],
                    start=True, stop=False),
                    reads=[onesb, rowbuf], writes=[pb])
                for c in range(NCC):
                    ctx.op("pe", lambda E, t3=t3, half=half, c=c, i=i, pb=pb:
                           E.matmul(pb[:, :],
                                    big0[:, c * SEQ + i * 128:
                                         c * SEQ + (i + 1) * 128],
                                    wp(t3, c * DM + half * 512,
                                       c * DM + (half + 1) * 512),
                                    start=False, stop=(c == NCC - 1)),
                           reads=[big0, ar0], writes=[pb])
        cosap = argb[:, i * 64:(i + 1) * 64]
        sinap = argb[:, NT * 64 + i * 64:NT * 64 + (i + 1) * 64]
        for t3 in range(2):
            for half in range(2):
                pb = pp[t3 * 2 + half]
                hofs = half * 512
                pb3 = pb[:, :].rearrange("p (h j) -> p h j", j=64)
                ctx.op("dve", lambda E, pb3=pb3, hofs=hofs, cosap=cosap:
                       E.tensor_tensor(
                           tmp1[:, hofs:hofs + 512].rearrange(
                               "p (h j) -> p h j", j=64),
                           pb3,
                           cosap.unsqueeze(1).broadcast_to(
                               (128, 8, 64)), ALU.mult),
                       reads=[pb, argb], writes=[tmp1])
                ctx.op("dve", lambda E, pb3=pb3, hofs=hofs, sinap=sinap:
                       E.tensor_tensor(
                           tmp2[:, hofs:hofs + 512].rearrange(
                               "p (h j) -> p h j", j=64)[:, :, 0:32],
                           pb3[:, :, 32:64],
                           sinap[:, 0:32].unsqueeze(1).broadcast_to((128, 8, 32)),
                           ALU.mult),
                       reads=[pb, argb], writes=[tmp2])
                ctx.op("dve", lambda E, pb3=pb3, hofs=hofs, sinap=sinap:
                       E.tensor_tensor(
                           tmp2[:, hofs:hofs + 512].rearrange(
                               "p (h j) -> p h j", j=64)[:, :, 32:64],
                           pb3[:, :, 0:32],
                           sinap[:, 32:64].unsqueeze(1).broadcast_to((128, 8, 32)),
                           ALU.mult),
                       reads=[pb, argb], writes=[tmp2])
            ctx.op("dve", lambda E: E.tensor_add(stage[:, :], tmp1[:, :],
                                                 tmp2[:, :]),
                   reads=[tmp1, tmp2], writes=[stage])
            dst = qT_d if t3 == 0 else kT_d
            for c in range(NCC):
                ctx.op("pe", lambda E, c=c: E.transpose(
                    pst[:, 0:128], stage[:, c * 128:(c + 1) * 128],
                    ident[:, :]),
                    reads=[stage, ident], writes=[pst])
                ctx.op("dve", lambda E, c=c: E.tensor_copy(
                    lntile[:, c * 128:(c + 1) * 128], pst[:, 0:128]),
                    reads=[pst], writes=[lntile])
            ctx.dma("sp", lambda E, i=i, dst=dst: E.dma_start(
                dst.t[:, :, i * 128:(i + 1) * 128].rearrange(
                    "c p m -> p c m"),
                lntile[:, :].rearrange("p (c m) -> p c m", c=NCC)),
                reads=[lntile], writes=[dst])

    ctx.dma("pool", lambda E: E.dma_start(
        ar0[:, 0:NCC * DM].rearrange("p (c m) -> p c m", c=NCC),
        wv.t.rearrange("(c p) m -> p c m", p=128)),
        reads=[wv], writes=[ar0])
    for i in range(NT):
        for half in range(2):
            pb = pp[4 + half]
            ctx.op("pe", lambda E, half=half, pb=pb: E.matmul(
                pb[:, :], onesb[0:1, 0:128],
                rowbuf[0:1, 2 * DM + half * 512:2 * DM + (half + 1) * 512],
                start=True, stop=False),
                reads=[onesb, rowbuf], writes=[pb])
            for c in range(NCC):
                ctx.op("pe", lambda E, half=half, c=c, i=i, pb=pb:
                       E.matmul(pb[:, :],
                                big0[:, c * SEQ + i * 128:
                                     c * SEQ + (i + 1) * 128],
                                wp(0, c * DM + half * 512,
                                   c * DM + (half + 1) * 512),
                                start=False, stop=(c == NCC - 1)),
                       reads=[big0, ar0], writes=[pb])
            ctx.op("dve", lambda E, i=i, half=half, pb=pb: E.tensor_copy(
                big1[:, :].rearrange("p (n h j) -> p n h j", h=16, j=65)
                [:, i, half * 8:(half + 1) * 8, 0:64],
                pb[:, :].rearrange("p (h j) -> p h j", j=64)),
                reads=[pb], writes=[big1])

    for pair in range(8):
        ctx.dma("sp", lambda E, pair=pair: E.dma_start(
            qhalf[:, 0:SEQ], qT_d.t[pair]), reads=[qT_d], writes=[qhalf])
        ctx.dma("sp", lambda E, pair=pair: E.dma_start(
            qhalf[:, SEQ:2 * SEQ], kT_d.t[pair]), reads=[kT_d],
            writes=[qhalf])
        for hh in range(2):
            hsl = slice(0, 64) if hh == 0 else slice(64, 128)
            habs = pair * 2 + hh
            for qs in range(8):
                outp = pp[2]
                for ktn in range(NT):
                    sb_ = pp[ktn % 2]
                    pt = pt2[ktn % 2]
                    ctx.op("pe", lambda E, ktn=ktn, qs=qs, hsl=hsl, sb_=sb_:
                           E.matmul(sb_[:, :],
                                    qhalf[hsl, SEQ + ktn * 128:
                                          SEQ + (ktn + 1) * 128],
                                    qhalf[hsl, qs * 512:(qs + 1) * 512],
                                    start=True, stop=True),
                           reads=[qhalf], writes=[sb_])
                    ctx.op("act", lambda E, sb_=sb_, pt=pt: E.activation(
                        pt[:, :], sb_[:, :], AF.Exp, bias=0.0, scale=1.0),
                        reads=[sb_], writes=[pt])
                    ctx.op("pe", lambda E, ktn=ktn, habs=habs, outp=outp,
                           pt=pt, st=(ktn == 0), sp_=(ktn == NT - 1):
                           E.matmul(outp[0:65, :],
                                    big1[:, :].rearrange(
                                        "p (n h j) -> p n h j", h=16, j=65)
                                    [:, ktn, habs, :],
                                    pt[:, :], start=st, stop=sp_),
                           reads=[big1, pt], writes=[outp])
                ctx.op("dve", lambda E, outp=outp: E.reciprocal(
                    tmp2[0:1, 0:512], outp[64:65, :]),
                    reads=[outp], writes=[tmp2])
                bc = pp[3]
                ctx.op("pe", lambda E, bc=bc: E.matmul(
                    bc[:, :], onesf[:, :], tmp2[0:1, 0:512], start=True,
                    stop=True),
                    reads=[onesf, tmp2], writes=[bc])
                ctx.op("dve", lambda E, bc=bc: E.tensor_copy(
                    tmp1[:, 0:512], bc[:, :]), reads=[bc], writes=[tmp1])
                ctx.op("dve", lambda E, outp=outp: E.tensor_tensor(
                    ystage[0:64, 0:512], outp[0:64, :], tmp1[0:64, 0:512],
                    ALU.mult),
                    reads=[outp, tmp1], writes=[ystage])
                ctx.dma("act", lambda E, pair=pair, hh=hh, qs=qs: E.dma_start(
                    yfm_d.t[pair, hh * 64:(hh + 1) * 64,
                            qs * 512:(qs + 1) * 512],
                    ystage[0:64, 0:512]),
                    reads=[ystage], writes=[yfm_d])

    ctx.dma("sp", lambda E: E.dma_start(
        gbufv[:, 0:DM], grow_in.t[0:1, 0:DM].broadcast_to((128, DM))),
        reads=[grow_in], writes=[argb])
    ctx.dma("sp", lambda E: E.dma_start(
        gbufv[:, DM:2 * DM],
        grow_in.t[0:1, DM:2 * DM].broadcast_to((128, DM))),
        reads=[grow_in], writes=[argb])
    ctx.dma("pool", lambda E: E.dma_start(
        ar0[:, 0:NCC * DM].rearrange("p (c m) -> p c m", c=NCC),
        wo.t.rearrange("(c p) m -> p c m", p=128)),
        reads=[wo], writes=[ar0])
    for i in range(NT):
        yb = lntile
        ctx.dma("sp", lambda E, i=i, yb=yb: E.dma_start(
            yb[:, :].rearrange("p (c m) -> p c m", c=NCC),
            yfm_d.t[:, :, i * 128:(i + 1) * 128].rearrange("c p m -> p c m")),
            reads=[yfm_d], writes=[yb])
        for half in range(2):
            zp = pp[4 + half]
            ctx.op("pe", lambda E, half=half, zp=zp: E.matmul(
                zp[:, :], onesb[0:1, 0:128],
                rowbuf[0:1, 3 * DM + half * 512:3 * DM + (half + 1) * 512],
                start=True, stop=False),
                reads=[onesb, rowbuf], writes=[zp])
            for c in range(NCC):
                ctx.op("pe", lambda E, c=c, half=half, zp=zp, yb=yb:
                       E.matmul(zp[:, :],
                                yb[:, c * 128:(c + 1) * 128],
                                wp(0, c * DM + half * 512,
                                   c * DM + (half + 1) * 512),
                                start=False, stop=(c == NCC - 1)),
                       reads=[yb, ar0], writes=[zp])
        ctx.dma("sp", lambda E, i=i: E.dma_start(
            xtile[:, :], h_d.t[i * 128:(i + 1) * 128, :]),
            reads=[h_d], writes=[xtile])
        for half in range(2):
            zp = pp[4 + half]
            ctx.op("dve", lambda E, half=half, zp=zp: E.tensor_tensor(
                tmp2[:, half * 512:(half + 1) * 512], zp[:, :],
                xtile[:, half * 512:(half + 1) * 512], ALU.add),
                reads=[zp, xtile], writes=[tmp2])
        emit_ln(ctx, mtile, np.s_[:, 0:DM], tmp2, np.s_[:, 0:DM], stat,
                tmp1, np.s_[:, 0:DM], DM, eps=epsc)
        ctx.op("dve", lambda E: E.tensor_tensor(
            mtile[:, 0:DM], mtile[:, 0:DM], gbufv[:, 0:DM], ALU.mult),
            reads=[mtile, argb], writes=[mtile])
        ctx.op("dve", lambda E: E.tensor_tensor(
            mtile[:, 0:DM], mtile[:, 0:DM], gbufv[:, DM:2 * DM], ALU.add),
            reads=[mtile, argb], writes=[mtile])
        ctx.dma("act", lambda E, i=i: E.dma_start(
            x1p_d.t[i * 128:(i + 1) * 128, :], mtile[:, 0:DM]),
            reads=[mtile], writes=[x1p_d])
        ctx.op("dve", lambda E: E.tensor_copy(lntile[:, :], mtile[:, 0:DM]),
               reads=[mtile], writes=[lntile])
        for c in range(NCC):
            ctx.op("pe", lambda E, c=c: E.transpose(
                pst[:, 0:128], lntile[:, c * 128:(c + 1) * 128], ident[:, :]),
                reads=[lntile, ident], writes=[pst])
            ctx.op("dve", lambda E, c=c: E.tensor_copy(
                stage[:, c * 128:(c + 1) * 128], pst[:, 0:128]),
                reads=[pst], writes=[stage])
        ctx.dma("act", lambda E, i=i: E.dma_start(
            x1pT_d.t[:, :, i * 128:(i + 1) * 128].rearrange("c p m -> p c m"),
            stage[:, :].rearrange("p (c m) -> p c m", c=NCC)),
            reads=[stage], writes=[x1pT_d])
        gp = pp[3]
        ctx.op("pe", lambda E, gp=gp: E.matmul(
            gp[:, 0:8], onesb[0:1, 0:128], rowbuf[0:1, 4096:4104],
            start=True, stop=False),
            reads=[onesb, rowbuf], writes=[gp])
        for c in range(NCC):
            ctx.op("pe", lambda E, c=c, gp=gp: E.matmul(
                gp[:, 0:8], stage[:, c * 128:(c + 1) * 128],
                wgw[:, c * 8:(c + 1) * 8],
                start=False, stop=(c == NCC - 1)),
                reads=[stage, wgw], writes=[gp])
        ctx.op("act", lambda E, gp=gp: E.activation(
            tmp2[:, 0:8], gp[:, 0:8], AF.Exp, bias=0.0, scale=1.0,
            accum_out=stat[:, 8:9]),
            reads=[gp], writes=[tmp2, stat])
        ctx.op("dve", lambda E: E.reciprocal(stat[:, 9:10], stat[:, 8:9]),
               reads=[stat], writes=[stat])
        ctx.op("dve", lambda E: E.tensor_scalar_mul(
            tmp2[:, 0:8], tmp2[:, 0:8], stat[:, 9:10]),
            reads=[stat, tmp2], writes=[tmp2])
        ctx.op("dve", lambda E: E.reduce_max(stat[:, 10:11], tmp2[:, 0:8],
                                             axis=AX.X),
               reads=[tmp2], writes=[stat])
        ctx.op("dve", lambda E: E.tensor_tensor(
            tmp2[:, 8:16], tmp2[:, 0:8],
            stat[:, 10:11].broadcast_to((128, 8)), ALU.is_ge),
            reads=[tmp2, stat], writes=[tmp2])
        ctx.op("dve", lambda E: E.scalar_tensor_tensor(
            tmp1[:, 8:16], tmp2[:, 8:16], -1e9, tmp2[:, 0:8],
            ALU.mult, ALU.add),
            reads=[tmp2], writes=[tmp1])
        ctx.op("dve", lambda E: E.reduce_max(stat[:, 12:13], tmp1[:, 8:16],
                                             axis=AX.X),
               reads=[tmp1], writes=[stat])
        ctx.op("dve", lambda E: E.tensor_tensor(
            tmp1[:, 0:8], tmp2[:, 0:8],
            stat[:, 12:13].broadcast_to((128, 8)), ALU.is_ge),
            reads=[tmp2, stat], writes=[tmp1])
        ctx.op("dve", lambda E: E.tensor_tensor(
            tmp1[:, 8:16], tmp1[:, 0:8], tmp2[:, 0:8], ALU.mult),
            reads=[tmp1, tmp2], writes=[tmp1])
        ctx.op("dve", lambda E: E.reduce_sum(stat[:, 13:14], tmp1[:, 8:16],
                                             axis=AX.X),
               reads=[tmp1], writes=[stat])
        ctx.op("dve", lambda E: E.reciprocal(stat[:, 14:15], stat[:, 13:14]),
               reads=[stat], writes=[stat])
        ctx.op("dve", lambda E, i=i: E.tensor_scalar_mul(
            comb[:, i * 8:(i + 1) * 8], tmp1[:, 8:16], stat[:, 14:15]),
            reads=[tmp1, stat], writes=[comb])

    if dbg:
        comb_d = ctx.dout("comb_d", [128, NT * 8], F32)
        ctx.dma("act", lambda E: E.dma_start(comb_d.t, comb[:, :]),
                reads=[comb], writes=[comb_d])

    HF = 8192
    for e in range(nexp):
        ctx.dma("pool", lambda E, e=e: E.dma_start(
            big0[:, :].rearrange("p (c m) -> p c m", c=NCC),
            w1_in.t[e].rearrange("(c p) m -> p c m", p=128)),
            reads=[w1_in], writes=[big0])
        ctx.dma("pool", lambda E, e=e: E.dma_start(
            big1[:, 0:32 * DM].rearrange("p (c m) -> p c m", c=32),
            w2_in.t[e].rearrange("(c p) m -> p c m", p=128)),
            reads=[w2_in], writes=[big1])
        ctx.dma("pool", lambda E, e=e: E.dma_start(
            rowbuf[32:33, 0:4096], b1_in.t[e:e + 1, :]),
            reads=[b1_in], writes=[rowbuf])
        ctx.dma("pool", lambda E, e=e: E.dma_start(
            rowbuf[64:65, 0:DM], b2_in.t[e:e + 1, :]),
            reads=[b2_in], writes=[rowbuf])
        for ns in range(8):
            ctx.dma("sp", lambda E, ns=ns: E.dma_start(
                argb[:, 0:4096].rearrange("p (c m) -> p c m", c=NCC),
                x1pT_d.t[:, :, ns * 512:(ns + 1) * 512].rearrange(
                    "c p m -> p c m")),
                reads=[x1pT_d], writes=[argb])
            for fc in range(32):
                pb = pp[fc % 2]
                ctx.op("pe", lambda E, fc=fc, pb=pb: E.matmul(
                    pb[:, :], rowbuf[32:33, fc * 128:(fc + 1) * 128],
                    rowbuf[32:33, 4096:4608], start=True, stop=False),
                    reads=[rowbuf], writes=[pb])
                for c in range(NCC):
                    ctx.op("pe", lambda E, fc=fc, c=c, pb=pb:
                           E.matmul(pb[:, :],
                                    big0[:, c * SEQ + fc * 128:
                                         c * SEQ + (fc + 1) * 128],
                                    argb[:, c * 512:(c + 1) * 512],
                                    start=False, stop=(c == NCC - 1)),
                           reads=[big0, argb], writes=[pb])
                ctx.op("act", lambda E, fc=fc, pb=pb: E.activation(
                    ar0[:, fc * 512:(fc + 1) * 512], pb[:, :], AF.Gelu,
                    bias=0.0, scale=1.0),
                    reads=[pb], writes=[ar0])
            if dbg and e == 0 and ns == 0:
                hfm_d = ctx.dout("hfm_d", [128, 32 * 512], BF16)
                ctx.dma("act", lambda E: E.dma_start(
                    hfm_d.t, ar0[:, 0:16384]), reads=[ar0], writes=[hfm_d])
            for nsub in range(4):
                i_abs = ns * 4 + nsub
                for half in range(2):
                    yp = pp[(nsub * 2 + half) % 6]
                    ctx.op("pe", lambda E, half=half, yp=yp: E.matmul(
                        yp[:, :], rowbuf[64:65, 4096:4224],
                        rowbuf[64:65, half * 512:(half + 1) * 512],
                        start=True, stop=False),
                        reads=[rowbuf], writes=[yp])
                    for fc in range(32):
                        ctx.op("pe", lambda E, fc=fc, nsub=nsub, half=half,
                               yp=yp:
                               E.matmul(yp[:, :],
                                        ar0[:, fc * 512 + nsub * 128:
                                            fc * 512 + (nsub + 1) * 128],
                                        big1[:, fc * DM + half * 512:
                                             fc * DM + (half + 1) * 512],
                                        start=False, stop=(fc == 31)),
                               reads=[ar0, big1], writes=[yp])
                    ctx.op("dve", lambda E, i_abs=i_abs, half=half, e=e,
                           yp=yp:
                           E.tensor_scalar_mul(
                               mtile[:, half * 512:(half + 1) * 512],
                               yp[:, :], comb[:, i_abs * 8 + e:
                                              i_abs * 8 + e + 1]),
                           reads=[yp, comb], writes=[mtile])
                ctx.dma("pool", lambda E, i_abs=i_abs, e=e: E.dma_start(
                    moe_d.t[i_abs * 128:(i_abs + 1) * 128, :],
                    mtile[:, 0:DM],
                    accum_op=(ALU.bypass if e == 0 else ALU.add)),
                    reads=[mtile], writes=[moe_d])

    ctx.dma("sp", lambda E: E.dma_start(
        gbufv[:, 0:DM],
        grow_in.t[0:1, 2 * DM:3 * DM].broadcast_to((128, DM))),
        reads=[grow_in], writes=[argb])
    ctx.dma("sp", lambda E: E.dma_start(
        gbufv[:, DM:2 * DM],
        grow_in.t[0:1, 3 * DM:4 * DM].broadcast_to((128, DM))),
        reads=[grow_in], writes=[argb])
    for i in range(NT):
        ctx.dma("sp", lambda E, i=i: E.dma_start(
            xtile[:, :], x1p_d.t[i * 128:(i + 1) * 128, :]),
            reads=[x1p_d], writes=[xtile])
        ctx.dma("sp", lambda E, i=i: E.dma_start(
            mtile[:, :], moe_d.t[i * 128:(i + 1) * 128, :]),
            reads=[moe_d], writes=[mtile])
        ctx.op("dve", lambda E: E.tensor_tensor(
            tmp2[:, 0:DM], xtile[:, :], mtile[:, :], ALU.add),
            reads=[xtile, mtile], writes=[tmp2])
        emit_ln(ctx, tmp1, np.s_[:, 0:DM], tmp2, np.s_[:, 0:DM], stat,
                mtile, np.s_[:, 0:DM], DM, eps=epsc)
        ctx.op("dve", lambda E: E.tensor_tensor(
            tmp1[:, 0:DM], tmp1[:, 0:DM], gbufv[:, 0:DM], ALU.mult),
            reads=[tmp1, argb], writes=[tmp1])
        ctx.op("dve", lambda E: E.tensor_tensor(
            tmp1[:, 0:DM], tmp1[:, 0:DM], gbufv[:, DM:2 * DM], ALU.add),
            reads=[tmp1, argb], writes=[tmp1])
        ctx.dma("sp", lambda E, i=i: E.dma_start(
            xtile[:, :], xin.t[i * 128:(i + 1) * 128, :]),
            reads=[xin], writes=[xtile])
        emit_ln(ctx, mtile, np.s_[:, 0:DM], tmp1, np.s_[:, 0:DM], stat,
                tmp2, np.s_[:, 0:DM], DM, res=xtile, res_sl=np.s_[:, :],
                eps=epsc)
        ctx.dma("act", lambda E, i=i: E.dma_start(
            xout.t[i * 128:(i + 1) * 128, :], mtile[:, 0:DM]),
            reads=[mtile], writes=[xout])




def host_consts_b2():
    c = np.zeros((128, 768), dtype=np.float32)
    c[:, 0:128] = np.eye(128, dtype=np.float32)
    c[0, 128:640] = 1.0
    return c


def host_rope_b2():
    HD = 64
    SC = float(1.0 / (2.0 * np.sqrt(2.0)))
    inv = 1.0 / 10000.0 ** (np.arange(0, HD, 2, dtype=np.float32) / HD)
    t = np.arange(SEQ, dtype=np.float32)
    fr = (t[:, None] * inv[None, :]).astype(np.float32)
    emb = np.concatenate([fr, fr], axis=-1)
    cos = (np.cos(emb) * SC).astype(np.float32)
    sin = np.sin(emb).astype(np.float32)
    ss = np.empty((SEQ, HD), dtype=np.float32)
    ss[:, 0:32] = -sin[:, 0:32] * SC
    ss[:, 32:64] = sin[:, 32:64] * SC
    return cos, ss


def emit_b4(ctx, com, xin, xout):
    nc = ctx.nc
    wqkv_in = ctx.din("wqkv_in", [6, DM, 3072])
    qs_in = ctx.din("qs_in", [6, 64])
    ks_in = ctx.din("ks_in", [6, 64])
    wo_in = ctx.din("wo_in", [6, DM, DM])
    lng_in = ctx.din("lng_in", [6, DM])
    lnb_in = ctx.din("lnb_in", [6, DM])
    w1a_in = ctx.din("w1a_in", [6, DM, B4_FFP])
    w1g_in = ctx.din("w1g_in", [6, DM, B4_FFP])
    b1a_in = ctx.din("b1a_in", [6, B4_FFP])
    b1g_in = ctx.din("b1g_in", [6, B4_FFP])
    w2p_in = ctx.din("w2p_in", [6, B4_FFP, DM])
    b2f_in = ctx.din("b2f_in", [6, DM])
    tab_in = ctx.din("tab_in", [SEQ, 4 * 64])
    msk_in = ctx.din("msk_in", [128, 4 * 256])
    cst = ctx.din("cst", [128, 768])

    xA_d = ctx.dram("b4xA_d", [SEQ, DM], F32)
    xB_d = ctx.dram("b4xB_d", [SEQ, DM], F32)
    qT_d = ctx.dram("b4qT_d", [NCC, 128, SEQ], BF16)
    kT_d = ctx.dram("b4kT_d", [NCC, 128, SEQ], BF16)
    yfm_d = ctx.dram("b4yfm_d", [NCC, 128, SEQ], BF16)

    xtile, lntile, stage = com.xtile, com.lntile, com.stage
    tmp1, tmp2, stat, epsc = com.tmp1, com.tmp2, com.stat, com.epsc
    ident, onesf = com.ident, com.onesf
    pp, pst = com.pp, com.pst

    arA = ctx.vsb("arA", [128, 24576], BF16)
    arB = ctx.vsb("arB", [128, 33792], BF16)
    arC = ctx.vsb("arC", [128, 22528], BF16)
    tabs = ctx.vsb("tabs", [128, 4 * NT * 64], BF16)
    qsb = ctx.vsb("qsb", [128, 4 * 64], BF16)
    xnslab = ctx.vsb("xnslab", [128, NCC * 512], BF16)
    shbuf = ctx.vsb("shbuf", [128, 3328], BF16)
    gg = ctx.vsb("gg", [128, 512], BF16)
    masks = shbuf
    rowbuf = shbuf
    pt = gg
    glbuf = tabs
    b2_in = b2f_in

    gbv = tabs.t[:, :].bitcast(F32)

    ctx.dma("pool", lambda E: E.dma_start(ident[:, :], cst[:, 0:128]),
            reads=[cst], writes=[ident])
    ctx.dma("sp", lambda E: E.dma_start(onesf[:, :], cst[0:1, 128:256]),
            reads=[cst], writes=[onesf])
    ctx.op("dve", lambda E: E.memset(epsc[:, 0:1], 1e-5),
           reads=[], writes=[epsc])
    ctx.op("dve", lambda E: E.memset(epsc[:, 1:2], -12.0),
           reads=[], writes=[epsc])

    for i in range(NT):
        ctx.dma("sp", lambda E, i=i: E.dma_start(
            xtile[:, :], xin.t[i * 128:(i + 1) * 128, :]),
            reads=[xin], writes=[xtile])
        emit_ln(ctx, tmp1, np.s_[:, :], xtile, np.s_[:, :], stat, tmp2,
                np.s_[:, :], DM, eps=epsc)
        ctx.dma("act", lambda E, i=i: E.dma_start(
            xA_d.t[i * 128:(i + 1) * 128, :], tmp1[:, :]),
            reads=[tmp1], writes=[xA_d])

    def shifted_load(xsrc, i):
        ctx.dma("sp", lambda E, i=i: E.dma_start(
            xtile[:, 0:512], xsrc.t[i * 128:(i + 1) * 128, 0:512]),
            reads=[xsrc], writes=[xtile])
        if i == 0:
            ctx.op("dve", lambda E: E.memset(xtile[0:1, 512:1024], 0.0),
                   reads=[], writes=[xtile])
            ctx.dma("sp", lambda E: E.dma_start(
                xtile[1:128, 512:1024], xsrc.t[0:127, 512:1024]),
                reads=[xsrc], writes=[xtile])
        else:
            ctx.dma("sp", lambda E, i=i: E.dma_start(
                xtile[:, 512:1024],
                xsrc.t[i * 128 - 1:(i + 1) * 128 - 1, 512:1024]),
                reads=[xsrc], writes=[xtile])

    for lyr in range(6):
        xCur, xNxt = (xA_d, xB_d)
        ctx.dma("pool", lambda E, lyr=lyr: E.dma_start(
            arA[:, :].rearrange("p (c m) -> p c m", c=NCC),
            wqkv_in.t[lyr].rearrange("(c p) m -> p c m", p=128)),
            reads=[wqkv_in], writes=[arA])
        for base, srcv in ((0, qs_in), (128, ks_in)):
            ctx.dma("pool", lambda E, base=base, srcv=srcv, lyr=lyr:
                    E.dma_start(qsb[:, base:base + 64],
                                srcv.t[lyr:lyr + 1, :].broadcast_to(
                                    (128, 64))),
                    reads=[srcv], writes=[qsb])
            ctx.dma("pool", lambda E, base=base, srcv=srcv, lyr=lyr:
                    E.dma_start(qsb[:, base + 64:base + 96],
                                srcv.t[lyr:lyr + 1, 32:64].broadcast_to(
                                    (128, 32))),
                    reads=[srcv], writes=[qsb])
            ctx.dma("pool", lambda E, base=base, srcv=srcv, lyr=lyr:
                    E.dma_start(qsb[:, base + 96:base + 128],
                                srcv.t[lyr:lyr + 1, 0:32].broadcast_to(
                                    (128, 32))),
                    reads=[srcv], writes=[qsb])
        for t4 in range(4):
            qv = (0, 64, 128, 192)[t4]
            for g2 in range(2):
                ctx.dma("sp", lambda E, t4=t4, g2=g2: E.dma_start(
                    tmp1[:, :].rearrange("p (t m) -> p t m", t=16),
                    tab_in.t[g2 * 2048:(g2 + 1) * 2048,
                             t4 * 64:(t4 + 1) * 64].rearrange(
                        "(t p) m -> p t m", p=128)),
                    reads=[tab_in], writes=[tmp1])
                ctx.op("dve", lambda E, t4=t4, g2=g2, qv=qv: E.tensor_tensor(
                    tabs[:, t4 * 2048 + g2 * 1024:
                         t4 * 2048 + (g2 + 1) * 1024].rearrange(
                        "p (t m) -> p t m", t=16),
                    tmp1[:, :].rearrange("p (t m) -> p t m", t=16),
                    qsb[:, qv:qv + 64].unsqueeze(1).broadcast_to(
                        (128, 16, 64)),
                    ALU.mult),
                    reads=[tmp1, qsb], writes=[tabs])
        ctx.op("dve", lambda E: E.memset(
            arB[:, 0:NT * 1040].rearrange("p (n c) -> p n c", c=65)
            [:, :, 64:65], 1.0),
            reads=[], writes=[arB])

        for i in range(NT):
            shifted_load(xCur, i)
            ctx.op("dve", lambda E: E.tensor_copy(lntile[:, :], xtile[:, :]),
                   reads=[xtile], writes=[lntile])
            for c in range(NCC):
                ctx.op("pe", lambda E, c=c: E.transpose(
                    pst[:, 0:128], lntile[:, c * 128:(c + 1) * 128],
                    ident[:, :]),
                    reads=[lntile, ident], writes=[pst])
                ctx.op("dve", lambda E, c=c: E.tensor_copy(
                    stage[:, c * 128:(c + 1) * 128], pst[:, 0:128]),
                    reads=[pst], writes=[stage])
            for fb in range(6):
                pb = pp[fb]
                for c in range(NCC):
                    ctx.op("pe", lambda E, fb=fb, c=c, pb=pb: E.matmul(
                        pb[:, :], stage[:, c * 128:(c + 1) * 128],
                        arA[:, c * 3072 + fb * 512:c * 3072 + (fb + 1) * 512],
                        start=(c == 0), stop=(c == NCC - 1)),
                        reads=[stage, arA], writes=[pb])
            for tq in range(2):
                tb = tq * 2048
                cqb = (0 if tq == 0 else 2) * 2048
                sqb = (1 if tq == 0 else 3) * 2048
                for hb in range(2):
                    pb = pp[tq * 2 + hb]
                    hofs = hb * 512
                    ctx.op("act", lambda E, pb=pb, hofs=hofs: E.activation(
                        tmp1[:, hofs:hofs + 512], pb[:, :], AF.Square,
                        bias=0.0, scale=1.0),
                        reads=[pb], writes=[tmp1])
                    ctx.op("dve", lambda E, hofs=hofs, hb=hb: E.reduce_sum(
                        stat[:, hb * 8:hb * 8 + 8],
                        tmp1[:, hofs:hofs + 512].rearrange(
                            "p (h j) -> p h j", j=64),
                        axis=AX.X),
                        reads=[tmp1], writes=[stat])
                ctx.op("act", lambda E: E.activation(
                    stat[:, 0:16], stat[:, 0:16], AF.Sqrt, bias=0.0,
                    scale=1.0),
                    reads=[stat], writes=[stat])
                ctx.op("dve", lambda E: E.tensor_scalar_max(
                    stat[:, 0:16], stat[:, 0:16], 1e-12),
                    reads=[stat], writes=[stat])
                ctx.op("dve", lambda E: E.reciprocal(stat[:, 0:16],
                                                     stat[:, 0:16]),
                       reads=[stat], writes=[stat])
                for hb in range(2):
                    pb = pp[tq * 2 + hb]
                    hofs = hb * 512
                    ctx.op("dve", lambda E, pb=pb, hofs=hofs, hb=hb:
                           E.tensor_tensor(
                               tmp2[:, hofs:hofs + 512].rearrange(
                                   "p (h j) -> p h j", j=64),
                               pb[:, :].rearrange("p (h j) -> p h j", j=64),
                               stat[:, hb * 8:hb * 8 + 8].unsqueeze(
                                   2).broadcast_to((128, 8, 64)),
                               ALU.mult),
                           reads=[pb, stat], writes=[tmp2])
                cosap = tabs[:, cqb + i * 64:cqb + (i + 1) * 64]
                sinap = tabs[:, sqb + i * 64:sqb + (i + 1) * 64]
                t23 = tmp2[:, :].rearrange("p (h j) -> p h j", j=64)
                ctx.op("dve", lambda E, sinap=sinap, t23=t23:
                       E.tensor_tensor(
                           tmp1[:, :].rearrange(
                               "p (h j) -> p h j", j=64)[:, :, 0:32],
                           t23[:, :, 32:64],
                           sinap[:, 0:32].unsqueeze(1).broadcast_to(
                               (128, 16, 32)),
                           ALU.mult),
                       reads=[tmp2, tabs], writes=[tmp1])
                ctx.op("dve", lambda E, sinap=sinap, t23=t23:
                       E.tensor_tensor(
                           tmp1[:, :].rearrange(
                               "p (h j) -> p h j", j=64)[:, :, 32:64],
                           t23[:, :, 0:32],
                           sinap[:, 32:64].unsqueeze(1).broadcast_to(
                               (128, 16, 32)),
                           ALU.mult),
                       reads=[tmp2, tabs], writes=[tmp1])
                ctx.op("dve", lambda E, cosap=cosap, t23=t23:
                       E.tensor_tensor(
                           lntile[:, :].rearrange("p (h j) -> p h j", j=64),
                           t23,
                           cosap.unsqueeze(1).broadcast_to((128, 16, 64)),
                           ALU.mult),
                       reads=[tmp2, tabs], writes=[lntile])
                ctx.op("dve", lambda E: E.tensor_add(
                    lntile[:, :], lntile[:, :], tmp1[:, :]),
                    reads=[lntile, tmp1], writes=[lntile])
                for c in range(NCC):
                    ctx.op("pe", lambda E, c=c: E.transpose(
                        pst[:, 0:128], lntile[:, c * 128:(c + 1) * 128],
                        ident[:, :]),
                        reads=[lntile, ident], writes=[pst])
                    ctx.op("dve", lambda E, c=c: E.tensor_copy(
                        stage[:, c * 128:(c + 1) * 128], pst[:, 0:128]),
                        reads=[pst], writes=[stage])
                dst = qT_d if tq == 0 else kT_d
                ctx.dma("sp", lambda E, i=i, dst=dst: E.dma_start(
                    dst.t[:, :, i * 128:(i + 1) * 128].rearrange(
                        "c p m -> p c m"),
                    stage[:, :].rearrange("p (c m) -> p c m", c=NCC)),
                    reads=[stage], writes=[dst])
            for hb in range(2):
                pb = pp[4 + hb]
                ctx.op("dve", lambda E, i=i, hb=hb, pb=pb: E.tensor_copy(
                    arB[:, 0:NT * 1040].rearrange(
                        "p (n h j) -> p n h j", h=16, j=65)
                    [:, i, hb * 8:(hb + 1) * 8, 0:64],
                    pb[:, :].rearrange("p (h j) -> p h j", j=64)),
                    reads=[pb], writes=[arB])

        ctx.dma("pool", lambda E: E.dma_start(shbuf[:, 0:1024], msk_in.t),
                reads=[msk_in], writes=[shbuf])
        for pair in range(8):
            ctx.dma("sp", lambda E, pair=pair: E.dma_start(
                arC[:, 0:SEQ], qT_d.t[pair]), reads=[qT_d], writes=[arC])
            ctx.dma("sp", lambda E, pair=pair: E.dma_start(
                arC[:, SEQ:2 * SEQ], kT_d.t[pair]), reads=[kT_d],
                writes=[arC])
            for hh in range(2):
                hsl = slice(0, 64) if hh == 0 else slice(64, 128)
                habs = pair * 2 + hh
                for w in range(16):
                    outp = pp[2]
                    j0 = 2 if w == 0 else 0
                    for j in range(j0, 4):
                        ktn = 2 * w - 2 + j
                        sb_ = pp[j % 2]
                        est = tmp1
                        ptb = gg
                        ctx.op("pe", lambda E, ktn=ktn, w=w, hsl=hsl,
                               sb_=sb_:
                               E.matmul(sb_[:, 0:256],
                                        arC[hsl, SEQ + ktn * 128:
                                            SEQ + (ktn + 1) * 128],
                                        arC[hsl, w * 256:(w + 1) * 256],
                                        start=True, stop=True),
                               reads=[arC], writes=[sb_])
                        ctx.op("act", lambda E, sb_=sb_, est=est:
                               E.activation(
                                   est[:, 0:256], sb_[:, 0:256], AF.Exp,
                                   bias=epsc[:, 1:2], scale=1.0),
                               reads=[sb_, epsc], writes=[est])
                        ctx.op("dve", lambda E, j=j, est=est, ptb=ptb:
                               E.tensor_tensor(
                                   ptb[:, 0:256], est[:, 0:256],
                                   masks[:, j * 256:(j + 1) * 256],
                                   ALU.mult),
                               reads=[est, masks], writes=[ptb])
                        ctx.op("pe", lambda E, ktn=ktn, habs=habs, ptb=ptb,
                               outp=outp, st=(j == j0), sp_=(j == 3):
                               E.matmul(outp[0:65, 0:256],
                                        arB[:, ktn * 1040 + habs * 65:
                                            ktn * 1040 + habs * 65 + 65],
                                        ptb[:, 0:256], start=st, stop=sp_),
                               reads=[arB, ptb], writes=[outp])
                    ctx.op("dve", lambda E, outp=outp: E.reciprocal(
                        tmp2[0:1, 0:256], outp[64:65, 0:256]),
                        reads=[outp], writes=[tmp2])
                    bc = pp[3]
                    ctx.op("pe", lambda E, bc=bc: E.matmul(
                        bc[:, 0:256], onesf[:, :], tmp2[0:1, 0:256],
                        start=True, stop=True),
                        reads=[onesf, tmp2], writes=[bc])
                    ctx.op("dve", lambda E, bc=bc: E.tensor_copy(
                        tmp1[:, 0:256], bc[:, 0:256]),
                        reads=[bc], writes=[tmp1])
                    ctx.op("dve", lambda E, outp=outp: E.tensor_tensor(
                        xnslab[0:64, 0:256], outp[0:64, 0:256],
                        tmp1[0:64, 0:256], ALU.mult),
                        reads=[outp, tmp1], writes=[xnslab])
                    ctx.dma("act", lambda E, pair=pair, hh=hh, w=w:
                            E.dma_start(
                                yfm_d.t[pair, hh * 64:(hh + 1) * 64,
                                        w * 256:(w + 1) * 256],
                                xnslab[0:64, 0:256]),
                            reads=[xnslab], writes=[yfm_d])

        ctx.dma("pool", lambda E, lyr=lyr: E.dma_start(
            arC[:, 2 * SEQ:2 * SEQ + NCC * DM].rearrange(
                "p (c m) -> p c m", c=NCC),
            wo_in.t[lyr].rearrange("(c p) m -> p c m", p=128)),
            reads=[wo_in], writes=[arC])
        for i in range(NT):
            ctx.dma("sp", lambda E, i=i: E.dma_start(
                lntile[:, :].rearrange("p (c m) -> p c m", c=NCC),
                yfm_d.t[:, :, i * 128:(i + 1) * 128].rearrange(
                    "c p m -> p c m")),
                reads=[yfm_d], writes=[lntile])
            for half in range(2):
                zp = pp[4 + half]
                for c in range(NCC):
                    ctx.op("pe", lambda E, c=c, half=half, zp=zp: E.matmul(
                        zp[:, :], lntile[:, c * 128:(c + 1) * 128],
                        arC[:, 2 * SEQ + c * DM + half * 512:
                            2 * SEQ + c * DM + (half + 1) * 512],
                        start=(c == 0), stop=(c == NCC - 1)),
                        reads=[lntile, arC], writes=[zp])
            ctx.dma("sp", lambda E, i=i: E.dma_start(
                xtile[:, :], xCur.t[i * 128:(i + 1) * 128, :]),
                reads=[xCur], writes=[xtile])
            for half in range(2):
                zp = pp[4 + half]
                ctx.op("dve", lambda E, half=half, zp=zp: E.tensor_tensor(
                    tmp1[:, half * 512:(half + 1) * 512], zp[:, :],
                    xtile[:, half * 512:(half + 1) * 512], ALU.add),
                    reads=[zp, xtile], writes=[tmp1])
            ctx.dma("act", lambda E, i=i: E.dma_start(
                xNxt.t[i * 128:(i + 1) * 128, :], tmp1[:, :]),
                reads=[tmp1], writes=[xNxt])

        ctx.dma("pool", lambda E, lyr=lyr: E.dma_start(
            arA[:, 0:NCC * B4_FFP].rearrange("p (c m) -> p c m", c=NCC),
            w1a_in.t[lyr].rearrange("(c p) m -> p c m", p=128)),
            reads=[w1a_in], writes=[arA])
        ctx.dma("pool", lambda E, lyr=lyr: E.dma_start(
            arB[:, 0:NCC * B4_FFP].rearrange("p (c m) -> p c m", c=NCC),
            w1g_in.t[lyr].rearrange("(c p) m -> p c m", p=128)),
            reads=[w1g_in], writes=[arB])
        ctx.dma("pool", lambda E, lyr=lyr: E.dma_start(
            arC[:, :].rearrange("p (c m) -> p c m", c=B4_NFC),
            w2p_in.t[lyr].rearrange("(c p) m -> p c m", p=128)),
            reads=[w2p_in], writes=[arC])
        ctx.dma("pool", lambda E, lyr=lyr: E.dma_start(
            rowbuf[0:1, 0:B4_FFP], b1a_in.t[lyr:lyr + 1, :]),
            reads=[b1a_in], writes=[rowbuf])
        ctx.dma("pool", lambda E, lyr=lyr: E.dma_start(
            rowbuf[32:33, 0:B4_FFP], b1g_in.t[lyr:lyr + 1, :]),
            reads=[b1g_in], writes=[rowbuf])
        ctx.dma("pool", lambda E, lyr=lyr: E.dma_start(
            rowbuf[64:65, 0:DM], b2_in.t[lyr:lyr + 1, :]),
            reads=[b2_in], writes=[rowbuf])
        ctx.op("dve", lambda E: E.memset(rowbuf[0:1, 2816:3328], 1.0),
               reads=[], writes=[rowbuf])
        ctx.op("dve", lambda E: E.memset(rowbuf[32:33, 2816:3328], 1.0),
               reads=[], writes=[rowbuf])
        ctx.op("dve", lambda E: E.memset(rowbuf[64:65, 1024:1152], 1.0),
               reads=[], writes=[rowbuf])
        ctx.dma("sp", lambda E, lyr=lyr: E.dma_start(
            gbv[:, 0:DM],
            lng_in.t[lyr:lyr + 1, :].broadcast_to((128, DM))),
            reads=[lng_in], writes=[tabs])
        ctx.dma("sp", lambda E, lyr=lyr: E.dma_start(
            gbv[:, DM:2 * DM],
            lnb_in.t[lyr:lyr + 1, :].broadcast_to((128, DM))),
            reads=[lnb_in], writes=[tabs])
        for ns in range(8):
            for sub in range(4):
                i = ns * 4 + sub
                shifted_load(xNxt, i)
                emit_ln(ctx, tmp1, np.s_[:, :], xtile, np.s_[:, :], stat,
                        tmp2, np.s_[:, :], DM, eps=epsc)
                ctx.op("dve", lambda E: E.tensor_tensor(
                    tmp1[:, :], tmp1[:, :], gbv[:, 0:DM], ALU.mult),
                    reads=[tmp1, tabs], writes=[tmp1])
                ctx.op("dve", lambda E: E.tensor_tensor(
                    tmp1[:, :], tmp1[:, :], gbv[:, DM:2 * DM], ALU.add),
                    reads=[tmp1, tabs], writes=[tmp1])
                ctx.op("dve", lambda E: E.tensor_copy(lntile[:, :],
                                                      tmp1[:, :]),
                       reads=[tmp1], writes=[lntile])
                for c in range(NCC):
                    ctx.op("pe", lambda E, c=c: E.transpose(
                        pst[:, 0:128], lntile[:, c * 128:(c + 1) * 128],
                        ident[:, :]),
                        reads=[lntile, ident], writes=[pst])
                    ctx.op("dve", lambda E, c=c, sub=sub: E.tensor_copy(
                        xnslab[:, c * 512 + sub * 128:
                               c * 512 + (sub + 1) * 128],
                        pst[:, 0:128]),
                        reads=[pst], writes=[xnslab])
            for fc in range(B4_NFC):
                gp_ = pp[0]
                ctx.op("pe", lambda E, fc=fc, gp_=gp_: E.matmul(
                    gp_[:, :], rowbuf[32:33, fc * 128:(fc + 1) * 128],
                    rowbuf[32:33, 2816:3328], start=True, stop=False),
                    reads=[rowbuf], writes=[gp_])
                for c in range(NCC):
                    ctx.op("pe", lambda E, fc=fc, c=c, gp_=gp_: E.matmul(
                        gp_[:, :],
                        arB[:, c * B4_FFP + fc * 128:
                            c * B4_FFP + (fc + 1) * 128],
                        xnslab[:, c * 512:(c + 1) * 512],
                        start=False, stop=(c == NCC - 1)),
                        reads=[arB, xnslab], writes=[gp_])
                ctx.op("act", lambda E, gp_=gp_: E.activation(
                    gg[:, :], gp_[:, :], AF.Gelu, bias=0.0, scale=1.0),
                    reads=[gp_], writes=[gg])
                ap_ = pp[1]
                ctx.op("pe", lambda E, fc=fc, ap_=ap_: E.matmul(
                    ap_[:, :], rowbuf[0:1, fc * 128:(fc + 1) * 128],
                    rowbuf[0:1, 2816:3328], start=True, stop=False),
                    reads=[rowbuf], writes=[ap_])
                for c in range(NCC):
                    ctx.op("pe", lambda E, fc=fc, c=c, ap_=ap_: E.matmul(
                        ap_[:, :],
                        arA[:, c * B4_FFP + fc * 128:
                            c * B4_FFP + (fc + 1) * 128],
                        xnslab[:, c * 512:(c + 1) * 512],
                        start=False, stop=(c == NCC - 1)),
                        reads=[arA, xnslab], writes=[ap_])
                ctx.op("dve", lambda E, fc=fc, ap_=ap_: E.tensor_tensor(
                    arB[:, NCC * B4_FFP + fc * 512:
                        NCC * B4_FFP + (fc + 1) * 512],
                    ap_[:, :], gg[:, :], ALU.mult),
                    reads=[ap_, gg], writes=[arB])
            for nsub in range(4):
                i = ns * 4 + nsub
                ctx.dma("sp", lambda E, i=i: E.dma_start(
                    xtile[:, :], xNxt.t[i * 128:(i + 1) * 128, :]),
                    reads=[xNxt], writes=[xtile])
                for half in range(2):
                    yp = pp[2 + half]
                    ctx.op("pe", lambda E, half=half, yp=yp: E.matmul(
                        yp[:, :], rowbuf[64:65, 1024:1152],
                        rowbuf[64:65, half * 512:(half + 1) * 512],
                        start=True, stop=False),
                        reads=[rowbuf], writes=[yp])
                    for fc in range(B4_NFC):
                        ctx.op("pe", lambda E, fc=fc, nsub=nsub, half=half,
                               yp=yp: E.matmul(
                            yp[:, :],
                            arB[:, NCC * B4_FFP + fc * 512 + nsub * 128:
                                NCC * B4_FFP + fc * 512 + (nsub + 1) * 128],
                            arC[:, fc * DM + half * 512:
                                fc * DM + (half + 1) * 512],
                            start=False, stop=(fc == B4_NFC - 1)),
                            reads=[arB, arC], writes=[yp])
                    ctx.op("dve", lambda E, half=half, yp=yp:
                           E.tensor_tensor(
                               tmp1[:, half * 512:(half + 1) * 512],
                               yp[:, :],
                               xtile[:, half * 512:(half + 1) * 512],
                               ALU.add),
                           reads=[yp, xtile], writes=[tmp1])
                ctx.dma("act", lambda E, i=i: E.dma_start(
                    xCur.t[i * 128:(i + 1) * 128, :], tmp1[:, :]),
                    reads=[tmp1], writes=[xCur])

    for i in range(NT):
        ctx.dma("sp", lambda E, i=i: E.dma_start(
            tmp2[:, :], xA_d.t[i * 128:(i + 1) * 128, :]),
            reads=[xA_d], writes=[tmp2])
        ctx.dma("sp", lambda E, i=i: E.dma_start(
            xtile[:, :], xin.t[i * 128:(i + 1) * 128, :]),
            reads=[xin], writes=[xtile])
        emit_ln(ctx, tmp1, np.s_[:, :], tmp2, np.s_[:, :], stat,
                lntile, np.s_[:, :], DM, res=xtile, res_sl=np.s_[:, :],
                eps=epsc)
        ctx.dma("act", lambda E, i=i: E.dma_start(
            xout.t[i * 128:(i + 1) * 128, :], tmp1[:, :]),
            reads=[tmp1], writes=[xout])




def host_tabs_b4():
    DIM_HEAD, WINDOW, n = 64, 256, SEQ
    QKS = float(np.sqrt(8.0))
    half = np.arange(0, DIM_HEAD, 2, dtype=np.float64)
    inv_freq = 1.0 / 10000.0 ** (half / DIM_HEAD)
    t = np.arange(n, dtype=np.float64)
    freqs = t[:, None] * inv_freq[None, :]
    freqs = np.concatenate([freqs, freqs], -1)
    base_scale = (half + 0.4 * DIM_HEAD) / (1.4 * DIM_HEAD)
    power = (t - n // 2) / (WINDOW // 2)
    scale = base_scale[None, :] ** power[:, None]
    scale = np.concatenate([scale, scale], -1)
    cos, sin = np.cos(freqs), np.sin(freqs)
    sgn = np.empty((n, 64))
    sgn[:, 0:32] = -sin[:, 0:32]
    sgn[:, 32:64] = sin[:, 32:64]
    inv = 1.0 / scale
    tab = np.concatenate([cos * scale * QKS, sgn * scale * QKS,
                          cos * inv * QKS, sgn * inv * QKS],
                         axis=1).astype(np.float32)
    return tab


def host_masks_b4():
    m = np.zeros((128, 4 * 256), dtype=np.float32)
    pc = np.arange(128)[:, None]
    pf = np.arange(256)[None, :]
    m[:, 0 * 256:1 * 256] = ((pf < 128) & (pc >= pf)).astype(np.float32)
    m[:, 1 * 256:2 * 256] = ((pf < 128) | (pc >= pf - 128)).astype(
        np.float32)
    m[:, 2 * 256:3 * 256] = ((pf >= 128) | (pc <= pf)).astype(np.float32)
    m[:, 3 * 256:4 * 256] = ((pf >= 128) & (pc <= pf - 128)).astype(
        np.float32)
    return m


def emit_b1(ctx, com, xin, xout):
    nc = ctx.nc
    wqT = ctx.din("wqT", [DM, DM])
    wkT = ctx.din("wkT", [DM, DM])
    wvT = ctx.din("wvT", [DM, DM])
    wlT = ctx.din("wlT", [DM, 32])
    wbT = ctx.din("wbT", [DM, 32])
    woT = ctx.din("woT", [DM, DM])
    winT_h = ctx.din("winT_h", [128, 512])
    woutT_h = ctx.din("woutT_h", [128, 512])
    woutD_h = ctx.din("woutD_h", [128, 512])
    bsel_in = ctx.din("bsel_in", [32, 16 * 128])
    cstb = ctx.din("cstb", [128, 512])

    o_d = ctx.dram("b1o_d", [NCC, 128, SEQ], BF16)

    xtile, lntile = com.xtile, com.lntile
    tmp1, tmp2, stat, epsc = com.tmp1, com.tmp2, com.stat, com.epsc
    ident = com.ident
    pst = com.pst
    z0, z1 = com.pp[0], com.pp[1]
    ps_a, ps_o, ps_s, ps_sb = com.pp[2], com.pp[3], com.pp[4], com.pp[5]

    hT = ctx.vsb("hT", [128, NCC * SEQ], BF16)
    qf = ctx.vsb("qf", [128, 8 * 1024], BF16)
    kf = ctx.vsb("kf", [128, 8 * 1024], BF16)
    vf = ctx.vsb("vf", [128, 8 * 1024], BF16)
    wpa = ctx.vsb("wpa", [128, NCC * DM], BF16)
    winT = ctx.vsb("winT", [128, 512], F32)
    woutT = ctx.vsb("woutT", [128, 512], F32)
    woutD = ctx.vsb("woutD", [128, 512], F32)
    winTb = ctx.vsb("winTb", [128, 512], BF16)
    woutTb = ctx.vsb("woutTb", [128, 512], BF16)
    woutDb = ctx.vsb("woutDb", [128, 512], BF16)
    lwt = ctx.vsb("lwt", [128, 2048], F32)
    lrc = ctx.vsb("lrc", [32, 128], F32)
    lrcb = ctx.vsb("lrcb", [32, 128], BF16)
    bselb = ctx.vsb("bselb", [32, 2048], BF16)
    wlb = ctx.vsb("wlb", [128, 512], BF16)
    e_all = ctx.vsb("e_all", [128, 1024], F32)
    p_all = ctx.vsb("p_all", [128, 1024], BF16)
    t_all = ctx.vsb("t_all", [128, 512], BF16)
    u_all = ctx.vsb("u_all", [128, 512], F32)
    gz = ctx.vsb("gz", [128, 512], BF16)
    ktok = ctx.vsb("ktok", [64, 1024], BF16)
    vtok = ctx.vsb("vtok", [64, 1024], BF16)
    ptok = ctx.vsb("ptok", [64, 1024], BF16)
    gztok = ctx.vsb("gztok", [64, 1024], BF16)
    ostage = ctx.vsb("ostage", [128, 512], BF16)
    updt = ctx.vsb("updt", [128, 512], F32)
    sseg = ctx.vsb("sseg", [2, 512], F32)
    e2b = ctx.vsb("e2b", [128, 2], BF16)
    e2tf = ctx.vsb("e2tf", [2, 128], F32)

    ctx.dma("pool", lambda E: E.dma_start(ident[:, :], cstb[:, 0:128]),
            reads=[cstb], writes=[ident])
    ctx.dma("pool", lambda E: E.dma_start(e2b[:, :], cstb[:, 128:130]),
            reads=[cstb], writes=[e2b])
    ctx.dma("sp", lambda E: E.dma_start(e2tf[:, :], cstb[0:2, 130:258]),
            reads=[cstb], writes=[e2tf])
    ctx.dma("pool", lambda E: E.dma_start(bselb[:, :], bsel_in.t),
            reads=[bsel_in], writes=[bselb])
    ctx.dma("pool", lambda E: E.dma_start(
        wlb[:, 0:256].rearrange("p (c m) -> p c m", c=NCC),
        wlT.t.rearrange("(c p) m -> p c m", p=128)),
        reads=[wlT], writes=[wlb])
    ctx.dma("pool", lambda E: E.dma_start(
        wlb[:, 256:512].rearrange("p (c m) -> p c m", c=NCC),
        wbT.t.rearrange("(c p) m -> p c m", p=128)),
        reads=[wbT], writes=[wlb])
    ctx.dma("sp", lambda E: E.dma_start(winT[:, :], winT_h.t),
            reads=[winT_h], writes=[winT])
    ctx.dma("sp", lambda E: E.dma_start(woutT[:, :], woutT_h.t),
            reads=[woutT_h], writes=[woutT])
    ctx.dma("sp", lambda E: E.dma_start(woutD[:, :], woutD_h.t),
            reads=[woutD_h], writes=[woutD])
    ctx.op("dve", lambda E: E.memset(epsc[:, 0:1], 1e-5),
           reads=[], writes=[epsc])

    for i in range(NT):
        ctx.dma("sp", lambda E, i=i: E.dma_start(
            xtile[:, :], xin.t[i * 128:(i + 1) * 128, :]),
            reads=[xin], writes=[xtile])
        emit_ln(ctx, tmp1, np.s_[:, :], xtile, np.s_[:, :], stat, tmp2,
                np.s_[:, :], DM, eps=epsc)
        ctx.op("dve", lambda E: E.tensor_copy(lntile[:, :], tmp1[:, :]),
               reads=[tmp1], writes=[lntile])
        for c in range(NCC):
            ctx.op("pe", lambda E, c=c: E.transpose(
                pst[:, 0:128], lntile[:, c * 128:(c + 1) * 128], ident[:, :]),
                reads=[lntile, ident], writes=[pst])
            ctx.op("dve", lambda E, c=c, i=i: E.tensor_copy(
                hT[:, c * SEQ + i * 128: c * SEQ + (i + 1) * 128],
                pst[:, 0:128]),
                reads=[pst], writes=[hT])

    for t2 in range(2):
        for ns in range(8):
            for c in range(NCC):
                ctx.op("pe", lambda E, t2=t2, ns=ns, c=c: E.matmul(
                    ps_a[0:32, :], wlb[:, t2 * 256 + c * 32:
                                       t2 * 256 + (c + 1) * 32],
                    hT[:, c * SEQ + ns * 512:c * SEQ + (ns + 1) * 512],
                    start=(c == 0), stop=(c == NCC - 1)),
                    reads=[wlb, hT], writes=[ps_a])
            ctx.op("act", lambda E: E.activation(
                tmp1[0:32, 0:512], ps_a[0:32, :], AF.Sigmoid,
                bias=0.0, scale=1.0),
                reads=[ps_a], writes=[tmp1])
            ctx.op("dve", lambda E, t2=t2, ns=ns: E.reduce_sum(
                lrc[0:32, t2 * 64 + ns * 8:t2 * 64 + (ns + 1) * 8],
                tmp1[0:32, 0:512].rearrange("p (t m) -> p t m", t=8),
                axis=AX.X),
                reads=[tmp1], writes=[lrc])
    ctx.op("dve", lambda E: E.tensor_scalar_mul(
        lrc[0:32, 0:64], lrc[0:32, 0:64], 0.001 / 64.0),
        reads=[lrc], writes=[lrc])
    ctx.op("dve", lambda E: E.tensor_scalar_mul(
        lrc[0:32, 64:128], lrc[0:32, 64:128], 0.9 / 64.0),
        reads=[lrc], writes=[lrc])
    ctx.op("dve", lambda E: E.tensor_copy(lrcb[0:32, :], lrc[0:32, :]),
           reads=[lrc], writes=[lrcb])
    for q4 in range(4):
        io = q4 % 2
        srcb = 0 if q4 < 2 else 64
        for j in range(8):
            ctx.op("pe", lambda E, io=io, j=j, srcb=srcb: E.matmul(
                ps_o[:, 0:64], bselb[0:32, (io * 8 + j) * 128:
                                     (io * 8 + j + 1) * 128],
                lrcb[0:32, srcb:srcb + 64], start=True, stop=True),
                reads=[bselb, lrcb], writes=[ps_o])
            ctx.op("dve", lambda E, q4=q4, j=j: E.tensor_copy(
                lwt[:, q4 * 512 + j * 64:q4 * 512 + (j + 1) * 64],
                ps_o[:, 0:64]),
                reads=[ps_o], writes=[lwt])

    for pss in range(4):
        for t3, (fmb, wsrc) in enumerate(((qf, wqT), (kf, wkT), (vf, wvT))):
            ctx.dma("pool", lambda E, wsrc=wsrc: E.dma_start(
                wpa[:, :].rearrange("p (c m) -> p c m", c=NCC),
                wsrc.t.rearrange("(c p) m -> p c m", p=128)),
                reads=[wsrc], writes=[wpa])
            for dchunk in range(NCC):
                for ns2 in range(2):
                    pb = z0 if ns2 == 0 else z1
                    for c in range(NCC):
                        ctx.op("pe", lambda E, dchunk=dchunk, ns2=ns2, c=c,
                               pss=pss, pb=pb: E.matmul(
                            pb[:, :],
                            wpa[:, c * DM + dchunk * 128:
                                c * DM + (dchunk + 1) * 128],
                            hT[:, c * SEQ + pss * 1024 + ns2 * 512:
                               c * SEQ + pss * 1024 + (ns2 + 1) * 512],
                            start=(c == 0), stop=(c == NCC - 1)),
                            reads=[wpa, hT], writes=[pb])
                    if t3 < 2:
                        ctx.op("act", lambda E, dchunk=dchunk, ns2=ns2,
                               fmb=fmb, pb=pb: E.activation(
                            fmb[:, dchunk * 1024 + ns2 * 512:
                                dchunk * 1024 + (ns2 + 1) * 512],
                            pb[:, :], AF.Silu, bias=0.0, scale=1.0),
                            reads=[pb], writes=[fmb])
                    else:
                        ctx.op("dve", lambda E, dchunk=dchunk, ns2=ns2,
                               fmb=fmb, pb=pb: E.tensor_copy(
                            fmb[:, dchunk * 1024 + ns2 * 512:
                                dchunk * 1024 + (ns2 + 1) * 512],
                            pb[:, :]),
                            reads=[pb], writes=[fmb])
        for il in range(16):
            ig = pss * 16 + il
            cb = il * 64
            for st_, stb_ in ((winT, winTb), (woutT, woutTb),
                              (woutD, woutDb)):
                ctx.op("dve", lambda E, st_=st_, stb_=stb_: E.tensor_copy(
                    stb_[:, :], st_[:, :]),
                    reads=[st_], writes=[stb_])
            for j in range(8):
                zp = z0 if j < 4 else z1
                zofs = (j % 4) * 128
                for hh in range(2):
                    hsl = slice(0, 64) if hh == 0 else slice(64, 128)
                    ctx.op("pe", lambda E, j=j, hh=hh, hsl=hsl, zp=zp,
                           zofs=zofs, cb=cb: E.matmul(
                        zp[hsl, zofs:zofs + 64],
                        winTb[hsl, j * 64:(j + 1) * 64],
                        qf[hsl, j * 1024 + cb:j * 1024 + cb + 64],
                        start=True, stop=True),
                        reads=[winTb, qf], writes=[zp])
                    ctx.op("pe", lambda E, j=j, hh=hh, hsl=hsl, zp=zp,
                           zofs=zofs, cb=cb: E.matmul(
                        zp[hsl, zofs + 64:zofs + 128],
                        winTb[hsl, j * 64:(j + 1) * 64],
                        kf[hsl, j * 1024 + cb:j * 1024 + cb + 64],
                        start=True, stop=True),
                        reads=[winTb, kf], writes=[zp])
            ctx.op("act", lambda E: E.activation(
                e_all[:, 0:512], z0[:, :], AF.Exp, bias=0.0, scale=1.0),
                reads=[z0], writes=[e_all])
            ctx.op("act", lambda E: E.activation(
                e_all[:, 512:1024], z1[:, :], AF.Exp, bias=0.0, scale=1.0),
                reads=[z1], writes=[e_all])
            ctx.op("dve", lambda E: E.reduce_sum(
                stat[:, 0:16],
                e_all[:, :].rearrange("p (t m) -> p t m", t=16),
                axis=AX.X),
                reads=[e_all], writes=[stat])
            ctx.op("dve", lambda E: E.reciprocal(stat[:, 0:16],
                                                 stat[:, 0:16]),
                   reads=[stat], writes=[stat])
            ctx.op("dve", lambda E: E.tensor_tensor(
                p_all[:, :].rearrange("p (t m) -> p t m", t=16),
                e_all[:, :].rearrange("p (t m) -> p t m", t=16),
                stat[:, 0:16].unsqueeze(2).broadcast_to((128, 16, 64)),
                ALU.mult),
                reads=[e_all, stat], writes=[p_all])
            for j in range(8):
                for hh in range(2):
                    hsl = slice(0, 64) if hh == 0 else slice(64, 128)
                    ctx.op("pe", lambda E, j=j, hsl=hsl, cb=cb: E.matmul(
                        ps_a[hsl, j * 64:(j + 1) * 64],
                        woutTb[hsl, j * 64:(j + 1) * 64],
                        vf[hsl, j * 1024 + cb:j * 1024 + cb + 64],
                        start=True, stop=True),
                        reads=[woutTb, vf], writes=[ps_a])
                    ctx.op("pe", lambda E, j=j, hsl=hsl: E.matmul(
                        ps_o[hsl, j * 64:(j + 1) * 64],
                        woutDb[hsl, j * 64:(j + 1) * 64],
                        p_all[hsl, j * 128:j * 128 + 64],
                        start=True, stop=True),
                        reads=[woutDb, p_all], writes=[ps_o])
            pk_ap = p_all[:, :].rearrange("p (j t n) -> p j t n",
                                          t=2, n=64)[:, :, 1, :]
            ctx.op("dve", lambda E, pk_ap=pk_ap: E.tensor_tensor(
                t_all[:, :].rearrange("p (j n) -> p j n", n=64),
                pk_ap,
                ps_a[:, :].rearrange("p (j n) -> p j n", n=64),
                ALU.mult),
                reads=[p_all, ps_a], writes=[t_all])
            ctx.op("pe", lambda E: E.matmul(
                ps_s[0:2, :], e2b[:, :], t_all[:, :], start=True, stop=True),
                reads=[e2b, t_all], writes=[ps_s])
            ctx.op("dve", lambda E: E.tensor_copy(sseg[:, :], ps_s[0:2, :]),
                   reads=[ps_s], writes=[sseg])
            ctx.op("pe", lambda E: E.matmul(
                ps_sb[:, :], e2tf[:, :], sseg[:, :], start=True, stop=True),
                reads=[e2tf, sseg], writes=[ps_sb])
            ctx.op("dve", lambda E, pk_ap=pk_ap: E.tensor_tensor(
                u_all[:, :].rearrange("p (j n) -> p j n", n=64),
                pk_ap,
                ps_sb[:, :].rearrange("p (j n) -> p j n", n=64),
                ALU.mult),
                reads=[p_all, ps_sb], writes=[u_all])
            ctx.op("dve", lambda E: E.tensor_sub(
                gz[:, :], u_all[:, :], t_all[:, :]),
                reads=[u_all, t_all], writes=[gz])
            for (srcb, dstb, ofs_fn) in (
                    (kf, ktok, lambda j: j * 1024 + cb),
                    (vf, vtok, lambda j: j * 1024 + cb),
                    (p_all, ptok, lambda j: j * 128 + 64),
                    (gz, gztok, lambda j: j * 64)):
                for j in range(8):
                    ctx.op("pe", lambda E, srcb=srcb, j=j, ofs=ofs_fn(j):
                           E.transpose(pst[0:64, 0:128],
                                       srcb[:, ofs:ofs + 64], ident[:, :]),
                           reads=[srcb, ident], writes=[pst])
                    ctx.op("dve", lambda E, dstb=dstb, j=j: E.tensor_copy(
                        dstb[0:64, j * 128:(j + 1) * 128], pst[0:64, 0:128]),
                        reads=[pst], writes=[dstb])
            for j in range(8):
                for hh in range(2):
                    co = j * 128 + hh * 64
                    go = j * 64
                    rsl = slice(0, 64) if hh == 0 else slice(64, 128)
                    ctx.op("pe", lambda E, co=co, go=go, rsl=rsl: E.matmul(
                        z0[rsl, go:go + 64], ktok[0:64, co:co + 64],
                        gztok[0:64, co:co + 64], start=True, stop=True),
                        reads=[ktok, gztok], writes=[z0])
                    ctx.op("pe", lambda E, co=co, go=go, rsl=rsl: E.matmul(
                        z1[rsl, go:go + 64], vtok[0:64, co:co + 64],
                        ptok[0:64, co:co + 64], start=True, stop=True),
                        reads=[vtok, ptok], writes=[z1])
                    ctx.op("pe", lambda E, co=co, go=go, rsl=rsl: E.matmul(
                        ps_a[rsl, go:go + 64], ptok[0:64, co:co + 64],
                        vtok[0:64, co:co + 64], start=True, stop=True),
                        reads=[ptok, vtok], writes=[ps_a])
            for (st_, gps, qlr, qwd, op1) in (
                    (winT, z0, 0, 2, ALU.subtract),
                    (woutT, z1, 1, 3, ALU.add),
                    (woutD, ps_a, 1, 3, ALU.add)):
                lr_ap = lwt[:, qlr * 512:(qlr + 1) * 512].rearrange(
                    "p (j n) -> p j n", n=64)[:, :, ig:ig + 1].broadcast_to(
                    (128, 8, 64))
                wd_ap = lwt[:, qwd * 512:(qwd + 1) * 512].rearrange(
                    "p (j n) -> p j n", n=64)[:, :, ig:ig + 1].broadcast_to(
                    (128, 8, 64))
                ctx.op("dve", lambda E, gps=gps, lr_ap=lr_ap:
                       E.tensor_tensor(
                           updt[:, :].rearrange("p (j n) -> p j n", n=64),
                           gps[:, :].rearrange("p (j n) -> p j n", n=64),
                           lr_ap, ALU.mult),
                       reads=[gps, lwt], writes=[updt])
                ctx.op("dve", lambda E, st_=st_, wd_ap=wd_ap:
                       E.tensor_tensor(
                           st_[:, :].rearrange("p (j n) -> p j n", n=64),
                           st_[:, :].rearrange("p (j n) -> p j n", n=64),
                           wd_ap, ALU.mult),
                       reads=[st_, lwt], writes=[st_])
                ctx.op("dve", lambda E, st_=st_, op1=op1: E.tensor_tensor(
                    st_[:, :], st_[:, :], updt[:, :], op1),
                    reads=[st_, updt], writes=[st_])
            ctx.op("dve", lambda E: E.tensor_copy(ostage[:, :], ps_o[:, :]),
                   reads=[ps_o], writes=[ostage])
            ctx.dma("act", lambda E, ig=ig: E.dma_start(
                o_d.t[:, :, ig * 64:(ig + 1) * 64].rearrange(
                    "c p m -> p c m"),
                ostage[:, :].rearrange("p (c m) -> p c m", c=NCC)),
                reads=[ostage], writes=[o_d])

    ctx.dma("pool", lambda E: E.dma_start(
        wpa[:, :].rearrange("p (c m) -> p c m", c=NCC),
        woT.t.rearrange("(c p) m -> p c m", p=128)),
        reads=[woT], writes=[wpa])
    for i in range(NT):
        ctx.dma("sp", lambda E, i=i: E.dma_start(
            lntile[:, :].rearrange("p (c m) -> p c m", c=NCC),
            o_d.t[:, :, i * 128:(i + 1) * 128].rearrange("c p m -> p c m")),
            reads=[o_d], writes=[lntile])
        for half in range(2):
            zp = z0 if half == 0 else z1
            for c in range(NCC):
                ctx.op("pe", lambda E, c=c, half=half, zp=zp: E.matmul(
                    zp[:, :], lntile[:, c * 128:(c + 1) * 128],
                    wpa[:, c * DM + half * 512:c * DM + (half + 1) * 512],
                    start=(c == 0), stop=(c == NCC - 1)),
                    reads=[lntile, wpa], writes=[zp])
        ctx.dma("sp", lambda E, i=i: E.dma_start(
            xtile[:, :], xin.t[i * 128:(i + 1) * 128, :]),
            reads=[xin], writes=[xtile])
        for half in range(2):
            zp = z0 if half == 0 else z1
            ctx.op("dve", lambda E, half=half, zp=zp: E.tensor_copy(
                tmp2[:, half * 512:(half + 1) * 512], zp[:, :]),
                reads=[zp], writes=[tmp2])
        emit_ln(ctx, tmp1, np.s_[:, 0:DM], tmp2, np.s_[:, 0:DM], stat,
                lntile, np.s_[:, 0:DM], DM, res=xtile, res_sl=np.s_[:, :],
                eps=epsc)
        ctx.dma("act", lambda E, i=i: E.dma_start(
            xout.t[i * 128:(i + 1) * 128, :], tmp1[:, 0:DM]),
            reads=[tmp1], writes=[xout])




def host_consts_b1():
    c = np.zeros((128, 512), dtype=np.float32)
    c[:, 0:128] = np.eye(128, dtype=np.float32)
    c[0:64, 128] = 1.0
    c[64:128, 129] = 1.0
    c[0, 130:194] = 1.0
    c[1, 194:258] = 1.0
    return c


def host_bsel_b1():
    b = np.zeros((32, 16 * 128), dtype=np.float32)
    for io in range(2):
        for j in range(8):
            for p in range(128):
                h = 2 * j + (1 if p >= 64 else 0)
                b[io * 16 + h, (io * 8 + j) * 128 + p] = 1.0
    return b


def host_winit_b1(W_in_init, W_out_init):
    wi = W_in_init[0]
    wo = W_out_init[0]
    winT = np.zeros((128, 512), dtype=np.float32)
    woutT = np.zeros((128, 512), dtype=np.float32)
    woutD = np.zeros((128, 512), dtype=np.float32)
    for p in range(128):
        for j in range(8):
            h = 2 * j + (1 if p >= 64 else 0)
            winT[p, j * 64:(j + 1) * 64] = wi[:, h, p % 64]
            woutT[p, j * 64:(j + 1) * 64] = wo[:, h, p % 64]
            woutD[p, j * 64:(j + 1) * 64] = wo[p % 64, h, :]
    return winT, woutT, woutD


ARENA_COLS = 97280


def com_alloc(ctx):
    from types import SimpleNamespace
    com = SimpleNamespace()
    com.xtile = ctx.sb("xtile", [128, DM], F32)
    com.lntile = ctx.sb("lntile", [128, DM], BF16)
    com.stage = ctx.sb("stage", [128, DM], BF16)
    com.tmp1 = ctx.sb("tmp1", [128, DM], F32)
    com.tmp2 = ctx.sb("tmp2", [128, DM], F32)
    com.stat = ctx.sb("stat", [128, 16], F32)
    com.epsc = ctx.sb("epsc", [128, 2], F32)
    com.ident = ctx.sb("ident", [128, 128], BF16)
    com.onesb = ctx.sb("onesb", [1, 512], BF16)
    com.onesf = ctx.sb("onesf", [1, 128], F32)
    com.pp = [ctx.ps(f"pp{i}", [128, 512], F32) for i in range(6)]
    com.pst = ctx.ps("pst", [128, 1024], BF16)
    return com


def build_solo(emitter, **kw):
    nc = bass.Bass("TRN2", target_bir_lowering=False)
    ctx = Ctx(nc)
    ctx.arena_init(ARENA_COLS)
    com = com_alloc(ctx)
    xin = ctx.din("xin", [SEQ, DM])
    xout = ctx.dout("xout", [SEQ, DM])
    emitter(ctx, com, xin, xout, **kw)
    ctx.barrier_out()
    ctx.finalize()
    return nc


def build_fused(dbg=False):
    nc = bass.Bass("TRN2", target_bir_lowering=False)
    ctx = Ctx(nc)
    ctx.arena_init(ARENA_COLS)
    com = com_alloc(ctx)
    mk = ctx.dout if dbg else ctx.dram
    x0 = ctx.din("x0", [SEQ, DM])
    x1_d = mk("x1_d", [SEQ, DM], F32)
    x2_d = mk("x2_d", [SEQ, DM], F32)
    x3_d = mk("x3_d", [SEQ, DM], F32)
    xout = ctx.dout("xout", [SEQ, DM])
    emit_b1(ctx, com, x0, x1_d)
    ctx.barrier()
    ctx.arena_reset()
    emit_b2(ctx, com, x1_d, x2_d)
    ctx.barrier()
    ctx.arena_reset()
    emit_b3(ctx, com, x2_d, x3_d)
    ctx.barrier()
    ctx.arena_reset()
    emit_b4(ctx, com, x3_d, xout)
    ctx.barrier_out()
    ctx.finalize()
    return nc


def prep_b1(z):
    cstb = host_consts_b1()
    bsel = host_bsel_b1()
    winT, woutT, woutD = host_winit_b1(z["L0_attn_W_in_init"],
                                       z["L0_attn_W_out_init"])
    t = lambda a: np.ascontiguousarray(np.asarray(a, dtype=np.float32).T)
    return {"wqT": t(z["L0_attn_Wq"]), "wkT": t(z["L0_attn_Wk"]),
            "wvT": t(z["L0_attn_Wv"]), "wlT": t(z["L0_attn_Wlr"]),
            "wbT": t(z["L0_attn_Wbeta"]), "woT": t(z["L0_attn_Wo"]),
            "winT_h": winT, "woutT_h": woutT, "woutD_h": woutD,
            "bsel_in": bsel, "cstb": cstb}


def prep_b2(z):
    cos, ss = host_rope_b2()
    cst2 = host_consts_b2()
    f = lambda a: np.ascontiguousarray(np.asarray(a, dtype=np.float32))
    brow = np.concatenate([f(z["L0_ffn_bq"]), f(z["L0_ffn_bk"]),
                           f(z["L0_ffn_bv"]),
                           f(z["L0_ffn_bo"])]).reshape(1, 4 * DM)
    grow = np.concatenate([f(z["L0_ffn_g1"]), f(z["L0_ffn_be1"]),
                           f(z["L0_ffn_g2"]),
                           f(z["L0_ffn_be2"])]).reshape(1, 4 * DM)
    return {"wq": f(z["L0_ffn_Wq"]), "wk": f(z["L0_ffn_Wk"]),
            "wv": f(z["L0_ffn_Wv"]), "wo": f(z["L0_ffn_Wo"]),
            "brow_in": brow, "wg_in": f(z["L0_ffn_Wg"]),
            "bg_in": f(z["L0_ffn_bg"]).reshape(1, 8),
            "w1_in": f(z["L0_ffn_W1"]), "b1_in": f(z["L0_ffn_b1"]),
            "w2_in": f(z["L0_ffn_W2"]), "b2_in": f(z["L0_ffn_b2"]),
            "grow_in": grow, "ropec": cos, "ropes": ss, "cst2": cst2}


def prep_b3(z):
    qtw, ktw, vtw, wtw = prep_b3_weights(
        np.asarray(z["L1_attn_Q"], dtype=np.float32),
        np.asarray(z["L1_attn_K"], dtype=np.float32),
        np.asarray(z["L1_attn_V"], dtype=np.float32),
        np.asarray(z["L1_attn_W"], dtype=np.float32))
    return {"qtw": qtw, "ktw": ktw, "vtw": vtw, "wtw": wtw,
            "consts": host_consts_b3()}


def prep_b4(z):
    cst = host_consts_b2()
    tab = host_tabs_b4()
    msk = host_masks_b4()
    f = lambda a: np.ascontiguousarray(np.asarray(a, dtype=np.float32))
    W1 = f(z["L1_ffn_W1"])
    b1 = f(z["L1_ffn_b1"])
    W2 = f(z["L1_ffn_W2"])
    FF = 2730
    w1a = np.zeros((6, DM, B4_FFP), dtype=np.float32)
    w1g = np.zeros((6, DM, B4_FFP), dtype=np.float32)
    w1a[:, :, 0:FF] = W1[:, :, 0:FF]
    w1g[:, :, 0:FF] = W1[:, :, FF:2 * FF]
    b1a = np.zeros((6, B4_FFP), dtype=np.float32)
    b1g = np.zeros((6, B4_FFP), dtype=np.float32)
    b1a[:, 0:FF] = b1[:, 0:FF]
    b1g[:, 0:FF] = b1[:, FF:2 * FF]
    w2p = np.zeros((6, B4_FFP, DM), dtype=np.float32)
    w2p[:, 0:FF, :] = W2
    return {"wqkv_in": f(z["L1_ffn_Wqkv"]),
            "qs_in": f(z["L1_ffn_q_scale"]), "ks_in": f(z["L1_ffn_k_scale"]),
            "wo_in": f(z["L1_ffn_Wo"]),
            "lng_in": f(z["L1_ffn_ln_g"]), "lnb_in": f(z["L1_ffn_ln_b"]),
            "w1a_in": w1a, "w1g_in": w1g, "b1a_in": b1a, "b1g_in": b1g,
            "w2p_in": w2p, "b2f_in": f(z["L1_ffn_b2"]),
            "tab_in": tab, "msk_in": msk, "cst": cst}


def _run_solo(emitter, prep, x, z, **kw):
    nc = build_solo(emitter, **kw)
    base = prep(z)
    in_maps = [dict(base, xin=np.ascontiguousarray(x[b]))
               for b in range(x.shape[0])]
    res = run_bass_kernel_spmd(nc, in_maps,
                               core_ids=list(range(x.shape[0])))
    if kw.get("dbg"):
        return res.results
    return np.stack([r["xout"] for r in res.results], axis=0)


def run_b1(x0, z):
    return _run_solo(emit_b1, prep_b1, x0, z)


def run_b2(x1, z, dbg=False, nexp=8):
    return _run_solo(emit_b2, prep_b2, x1, z, dbg=dbg, nexp=nexp)


def run_b3(x2, Q, K, V, W):
    z = {"L1_attn_Q": Q, "L1_attn_K": K, "L1_attn_V": V, "L1_attn_W": W}
    return _run_solo(emit_b3, prep_b3, x2, z)


def run_b4(x3, z):
    return _run_solo(emit_b4, prep_b4, x3, z)


def kernel(**inputs):
    z = inputs
    x0 = np.ascontiguousarray(np.asarray(z["L0_attn_x"], dtype=np.float32))
    nc = build_fused()
    base = {}
    base.update(prep_b1(z))
    base.update(prep_b2(z))
    base.update(prep_b3(z))
    base.update(prep_b4(z))
    in_maps = [dict(base, x0=np.ascontiguousarray(x0[b]))
               for b in range(8)]
    res = run_bass_kernel_spmd(nc, in_maps, core_ids=list(range(8)))
    return np.stack([r["xout"] for r in res.results],
                    axis=0).astype(np.float32)
